# version 25
# baseline (speedup 1.0000x reference)
import numpy as np
import ml_dtypes
from contextlib import ExitStack

import concourse.bass as bass
import concourse.mybir as mybir
from concourse.bass_utils import run_bass_kernel_spmd

F32 = mybir.dt.float32
BF16 = mybir.dt.bfloat16
I32T = mybir.dt.int32
AF = mybir.ActivationFunctionType
ALU = mybir.AluOpType
AX = mybir.AxisListType

DM = 2048
T = 2048
HALO = 3
NFT = 129
KD = 16
CS = 256
NCH = 8
H = 32
DI = 4096
E = 8
NKR = 80
KB = 8
NKB = 2
SLOTS = 7680
NST = 60
NGRP = 15
NSPL = 10
TPS = 6
SSL = 768
EPS = 1e-5

DEBUG_TAPS = set()
PHASE_LIMIT = 99
FT_LIMIT = NFT

SB_BASE = 16512
SB_LIMIT = 229344 - SB_BASE


def _build_map():
    m = {}

    def seq(start, items):
        o = start
        for name, byts in items:
            o = (o + 31) & ~31
            m[name] = o
            o += byts
        return o

    end0 = seq(0, [
        ("ident", 512), ("flag", 32), ("conv_w", 512), ("conv_b", 128),
        ("d_param", 128), ("a_dt", 32), ("negA", 32), ("b1", 256), ("sb1", 64),
        ("b2bf", 4096), ("sb2bf", 4096), ("ones1bf", 256), ("ones32", 512),
        ("gwbf", 256), ("g_rows", 8192), ("ln_stats", 64),
        ("lt2", 2048), ("s1all", 64), ("s2all", 64),
    ])
    assert end0 <= 25600, end0
    end1 = seq(25632, [
        ("sel32", 16384), ("sel8", 4096), ("seg_mask", 8192), ("dt_sb", 8192),
        ("h_f32", 16384), ("h_bf16", 8192),
        ("lasts", 64), ("ones8", 32), ("decays", 32),
        ("cum_tm", 256), ("neg_cum", 256), ("w_mh", 256),
        ("lastsbc", 128), ("el_bc", 128), ("D_bc", 128),
        ("cum_fm", 8192), ("dtscr", 8192),
    ])
    P2S = (end1 + 31) & ~31
    endp1 = seq(P2S, [
        ("xfm", 65632), ("wt0", 4096), ("wt1", 4096),
        ("valraw0", 8224), ("convacc", 8192),
        ("stg0", 4096), ("stg1", 4096), ("stg2", 4096),
    ])
    assert endp1 <= SB_LIMIT, endp1
    endp2 = seq(P2S, [
        ("val_fm", 16384), ("B_fm", 16384), ("C_fm", 16384),
        ("val_tm0", 8192), ("val_tm1", 8192), ("B_tm0", 8192), ("B_tm1", 8192),
        ("Bw", 1024), ("e0", 4096), ("Mt0", 2048), ("expcum", 1024),
        ("yd_st0", 4096), ("ce_st0", 4096),
    ])
    assert endp2 <= SB_LIMIT, endp2
    m["clbf"] = m["sel32"]
    m["ctmp"] = m["sel32"] + 1024
    m["wot"] = 49152
    assert 49152 + 131072 <= 180256
    m["yg_t"] = 180256
    m["yg_t2"] = 188448
    m["xtm_b"] = 196640
    m["sq_b"] = 25632
    m["x1stage"] = 33824
    m["x1fm"] = 49152
    m["w1t0"] = 114688
    m["w1t1"] = 118784
    m["ghe_st0"] = 155648
    m["ghe_st1"] = 159744
    m["gel0"] = 163840
    m["gel1"] = 165888
    m["gtmp"] = 167936
    m["gtmp2"] = 168000
    m["ind8"] = 25632
    m["pos8"] = 33824
    m["slot_tm0"] = 198656
    m["slot_tm1"] = 198720
    m["gate_tm0"] = 198784
    m["gate_tm1"] = 198848
    m["islot"] = 198912
    m["stage2a"] = 198976
    m["stage2b"] = 199008
    m["stage2c"] = 201344
    m["stage2d"] = 201376
    m["gvall"] = 199040
    m["gixall"] = 199296
    m["giiall"] = 199776
    m["wif"] = 200256
    m["wii0"] = 200288
    m["wii1"] = 200320
    m["wb_bc"] = 200352
    m["cmp16"] = 200416
    m["rts"] = 200480
    m["egrow"] = 200512
    m["wbrow"] = 200576
    m["ones1f"] = 200640
    m["l8u"] = 201152
    m["srow16"] = 201184
    m["iota_t"] = 201248
    m["iota_w"] = 201312
    m["w1cA0"] = 49152
    m["w1cA1"] = 65536
    m["w1cB0"] = 81920
    m["w1cB1"] = 98304
    m["w2cA0"] = 114688
    m["w2cA1"] = 131072
    m["w2cB0"] = 147456
    m["w2cB1"] = 163840
    m["xgg"] = 180224
    m["ones8r"] = 180224
    m["r1sb"] = 33824
    m["r2sb"] = 204800
    m["h_fm0"] = 204832
    m["h_fm1"] = 206880
    m["osb0"] = 33824
    m["osb1"] = 37920
    m["xg_sb0"] = 25632
    m["xg_sb1"] = 29728
    m["xg_sb2"] = 33824
    m["xg_sb3"] = 37920
    m["xg_sb4"] = 180224
    m["xg_sb5"] = 184320
    m["xg_sb6"] = 188416
    m["xg_sb7"] = 192512
    m["b2b"] = 42016
    m["b1b"] = 46112
    m["w2k0"] = 40960
    m["w2k1"] = 73728
    m["ghk0"] = 106496
    m["ghk1"] = 122880
    m["out_acc"] = 139264
    m["lnscr"] = 25632
    m["xts0a"] = m["g_rows"]
    m["xts0b"] = m["g_rows"] + 1536
    m["xts0c"] = m["g_rows"] + 3072
    m["xts0d"] = m["g_rows"] + 4608
    for k in m:
        m[k] += SB_BASE
    return m


_MAP = _build_map()


class Sem:
    __slots__ = ("h", "n", "name")

    def __init__(self, h, name):
        self.h = h
        self.n = 0
        self.name = name


class _I:
    __slots__ = ("r", "auto")

    def __init__(self, r, auto=None):
        self.r = r
        self.auto = auto

    def then_inc(self, sem, k=1):
        if sem is self.auto:
            return self
        sem.n += k
        if self.r is not None:
            self.r.then_inc(sem.h, k)
        return self


_NO_CHAIN = {"wait_ge", "dma_start", "dma_start_transpose", "collective_compute",
             "indirect_dma_start", "reg_load", "reg_mul", "reg_add", "reg_mov",
             "reg_sub", "nop", "sem_inc", "drain"}


class Eng:
    def __init__(self, prog, name, auto_sem_name=None):
        self._P = prog
        self._name = name
        self._auto_name = auto_sem_name

    def __getattr__(self, attr):
        P = self._P
        name = self._name
        auto_name = self._auto_name if attr not in _NO_CHAIN else None

        def call(*a, **kw):
            auto = P.sems[auto_name] if auto_name else None
            if P.active == name:
                real_eng = P.real[name]
                if auto is not None and auto.n > 0:
                    real_eng.wait_ge(auto.h, auto.n)
                ri = getattr(real_eng, attr)(*a, **kw)
                if auto is not None:
                    ri.then_inc(auto.h, 1)
                    auto.n += 1
                return _I(ri, auto)
            if auto is not None:
                auto.n += 1
            return _I(None, auto)

        return call

    def wait(self, sem, val=None):
        if val is None:
            val = sem.n
        if val <= 0:
            return
        if self._P.active == self._name:
            self._P.real[self._name].wait_ge(sem.h, val)


class Prog:
    def __init__(self, nc):
        self.nc = nc
        self.active = None
        self.real = {}
        self.sy = Eng(self, "sync")
        self.te = Eng(self, "tensor")
        self.sc = Eng(self, "scalar", "act")
        self.ve = Eng(self, "vector", "dve")
        self.gp = Eng(self, "gpsimd", "pool")

    def reset(self):
        for s in self.sems.values():
            s.n = 0
        self.bank = [{"wr": None, "rd": {}} for _ in range(8)]
        self.snap = {}

    def g(self, key):
        return self.snap.get(key, 0)

    def s(self, key, val):
        self.snap[key] = val

    def bank_w(self, eng, i):
        for s, v in self.bank[i]["rd"].items():
            eng.wait(s, v)

    def bank_r(self, eng, i):
        wr = self.bank[i]["wr"]
        if wr is not None:
            eng.wait(wr[0], wr[1])

    def bank_set_w(self, i, sem):
        self.bank[i]["wr"] = (sem, sem.n)
        self.bank[i]["rd"] = {}

    def bank_add_r(self, i, sem):
        self.bank[i]["rd"][sem] = sem.n


def _dump_taps(P, A, S):
    if not DEBUG_TAPS:
        return
    gp, sy = P.gp, P.sy
    for nm in ["w", "ld", "sp", "spg", "wta", "wtb", "ch", "gha", "ghb", "ig", "gx", "pe", "act", "dve", "pool"]:
        gp.wait(S[nm], S[nm].n)
    for name in sorted(DEBUG_TAPS):
        src = A[name]
        rows = src.shape[0]
        step = max(rows // 8, 1)
        for r0 in range(0, rows, step):
            r1 = min(r0 + step, rows)
            gp.dma_start(out=A["dbg_" + name][r0:r1, :], in_=src[r0:r1, :]).then_inc(S["st"], 16)
    sy.wait(S["st"], S["st"].n)


def emit(P, A, S, B):
    sy, te, sc, ve, gp = P.sy, P.te, P.sc, P.ve, P.gp
    if P.active == "gpsimd":
        for _r in A["reg_spares"]:
            P.real["gpsimd"].free_register(_r)
    s_w, s_ld, s_sp, s_st = S["w"], S["ld"], S["sp"], S["st"]
    s_pe, s_act, s_dve, s_pool, s_cc = S["pe"], S["act"], S["dve"], S["pool"], S["cc"]
    s_wt = [S["wta"], S["wtb"]]
    s_chl = S["ch"]
    s_gh = [S["gha"], S["ghb"]]
    s_spg = S["spg"]
    s_ig = S["ig"]
    s_gx = S["gx"]
    s_wc = S["b8a"]

    for nm in ["ident", "flag", "conv_w", "conv_b", "a_dt", "d_param", "b1", "sb1",
               "lt2", "sel32", "seg_mask"]:
        sy.dma_start(out=A[nm][:], in_=A["in_" + nm]).then_inc(s_ld, 16)
    p0_consts = s_ld.n
    gp.dma_start(out=A["b2bf"][:], in_=A["in_b2"]).then_inc(s_w, 16)
    gp.dma_start(out=A["sb2bf"][:], in_=A["in_sb2"]).then_inc(s_w, 16)
    p0_bias = s_w.n
    xfm3 = A["xfm"][:].rearrange("p (k t) -> p k t", k=KD)
    src3 = A["in_x_fm"].rearrange("(k p) t -> p k t", p=128)
    for i in range(4):
        gp.dma_start(out=xfm3[:, i * 4:(i + 1) * 4, :],
                     in_=src3[:, i * 4:(i + 1) * 4, :]).then_inc(s_w, 16)
    p0_xfm = s_w.n
    gp.memset(A["ones1bf"][:], 1.0).then_inc(s_pool, 1)
    gp.memset(A["ones32"][:], 1.0).then_inc(s_pool, 1)
    gp.memset(A["ones8"][:], 1.0).then_inc(s_pool, 1)
    p0_pool = s_pool.n
    ve.memset(A["h_f32"][:], 0.0).then_inc(s_dve, 1)
    ve.memset(A["h_bf16"][:], 0.0).then_inc(s_dve, 1)
    p0_h = s_dve.n
    sc.wait(s_ld, p0_consts)
    sc.activation(A["negA"][:], A["a_dt"][:, 0:1], AF.Exp).then_inc(s_act, 1)
    ve.wait(s_act, s_act.n)
    ve.tensor_scalar_mul(A["negA"][:], A["negA"][:], -1.0).then_inc(s_dve, 1)
    p0_negA = s_dve.n

    if PHASE_LIMIT < 1:
        return

    _pc = []
    for hh in range(2):
        _pc.append((bass.AP(A["d_sw1b"].tensor, hh * 1024 * 2048,
                            [[2048, 1024], [1, 2048]]),
                    bass.AP(A["in_sw1"].tensor, hh * 1024 * 2048,
                            [[2048, 1024], [1, 2048]]), S["swc"]))
    for hh in range(4):
        _pc.append((bass.AP(A["d_woutb"].tensor, hh * 1024 * 2048,
                            [[2048, 1024], [1, 2048]]),
                    bass.AP(A["in_w_out"].tensor, hh * 1024 * 2048,
                            [[2048, 1024], [1, 2048]]), S["swc"]))
    _pc.append((A["d_b1b"], A["in_b1r"], s_wc))
    _pc.append((A["d_b2b"], A["in_b2r"], s_wc))
    for hh in range(2):
        _pc.append((bass.AP(A["d_w2shb"].tensor, hh * 1024 * 2048,
                            [[2048, 1024], [1, 2048]]),
                    bass.AP(A["in_w2cat"].tensor, (8192 + hh * 1024) * 2048,
                            [[2048, 1024], [1, 2048]]), s_wc))
    _pci = [0]

    def _emit_pc():
        if _pci[0] < len(_pc):
            d, sr, sem = _pc[_pci[0]]
            gp.dma_start(out=d, in_=sr).then_inc(sem, 16)
            _pci[0] += 1

    wt = [A["wt0"], A["wt1"]]
    vr = [A["valraw0"]]
    stg = [A["stg0"], A["stg1"], A["stg2"]]
    wt_free = [0, 0]
    vr_free = [0]
    stg_free = [0, 0, 0]
    conv_act = 0

    for ft in range(min(NFT, FT_LIMIT)):
        typ = ("val" if ft < 32 else "gate" if ft < 64 else "B" if ft < 96 else
               "C" if ft < 128 else "dt")
        par = ft % 2
        vp = 0
        sg = ft % 3
        gp.wait(s_pe, wt_free[par])
        gp.dma_start(out=wt[par][:], in_=A["in_w_in"][ft * 128:(ft + 1) * 128, :]).then_inc(s_wt[par], 16)
        w_rdy = s_wt[par].n
        if ft >= 10 and ft % 5 == 0:
            _emit_pc()
        te.wait(s_wt[par], w_rdy)
        if ft == 0:
            te.wait(s_w, p0_xfm)
        halo_done = 0
        if typ == "val":
            P.bank_w(te, 2)
            for k in range(KD):
                mi = te.matmul(B[2][:, 0:HALO], wt[par][:, k * 128:(k + 1) * 128],
                               xfm3[:, k, 0:HALO], start=(k == 0), stop=(k == KD - 1))
            mi.then_inc(s_pe, 1)
            halo_done = s_pe.n
            P.bank_set_w(2, s_pe)
        if typ == "val":
            sc.wait(s_dve, vr_free[vp])
            sc.wait(s_pe, halo_done)
            P.bank_r(sc, 2)
            sc.copy(vr[vp][:, 0:HALO], B[2][:, 0:HALO]).then_inc(s_act, 1)
            P.bank_add_r(2, s_act)
        if typ in ("gate",) or typ == "val":
            sc.wait(s_sp, s_sp.n)
        if typ in ("B", "C"):
            ve.wait(s_sp, s_sp.n)
        for tg in range(4):
            bk = tg % 2
            P.bank_w(te, bk)
            for k in range(KD):
                mi = te.matmul(B[bk][:], wt[par][:, k * 128:(k + 1) * 128],
                               xfm3[:, k, HALO + tg * 512: HALO + (tg + 1) * 512],
                               start=(k == 0), stop=(k == KD - 1))
            mi.then_inc(s_pe, 1)
            mm_done = s_pe.n
            P.bank_set_w(bk, s_pe)
            if typ == "val":
                if tg == 0:
                    sc.wait(s_dve, vr_free[vp])
                sc.wait(s_pe, mm_done)
                P.bank_r(sc, bk)
                sc.copy(vr[vp][:, HALO + tg * 512: HALO + (tg + 1) * 512],
                        B[bk][:]).then_inc(s_act, 1)
                P.bank_add_r(bk, s_act)
            elif typ == "gate":
                sc.wait(s_pe, mm_done)
                P.bank_r(sc, bk)
                sc.activation(stg[sg][:, tg * 512:(tg + 1) * 512], B[bk][:],
                              AF.Silu).then_inc(s_act, 1)
                P.bank_add_r(bk, s_act)
            elif typ in ("B", "C"):
                ve.wait(s_pe, mm_done)
                P.bank_r(ve, bk)
                ve.tensor_copy(stg[sg][:, tg * 512:(tg + 1) * 512], B[bk][:]).then_inc(s_dve, 1)
                P.bank_add_r(bk, s_dve)
            else:
                sc.wait(s_pe, mm_done)
                P.bank_r(sc, bk)
                sc.copy(A["dt_sb"][:, tg * 512:(tg + 1) * 512], B[bk][0:32, :]).then_inc(s_act, 1)
                P.bank_add_r(bk, s_act)
        wt_free[par] = s_pe.n

        if typ == "val":
            evac_done = s_act.n
            ve.wait(s_act, max(evac_done, conv_act))
            ve.tensor_scalar_mul(A["convacc"][:], vr[vp][:, 0:T],
                                 A["conv_w"][:, ft * 4: ft * 4 + 1]).then_inc(s_dve, 1)
            for k in range(1, 4):
                ve.scalar_tensor_tensor(A["convacc"][:], vr[vp][:, k:k + T],
                                        A["conv_w"][:, ft * 4 + k: ft * 4 + k + 1],
                                        A["convacc"][:], ALU.mult, ALU.add).then_inc(s_dve, 1)
            vr_free[vp] = s_dve.n
            conv_done = s_dve.n
            sc.wait(s_dve, conv_done)
            sc.wait(s_sp, s_sp.n)
            sc.activation(stg[sg][:], A["convacc"][:], AF.Silu,
                          bias=A["conv_b"][:, ft:ft + 1]).then_inc(s_act, 1)
            conv_act = s_act.n
            sy.wait(s_act, s_act.n)
            sy.dma_start(out=A["sp_val"][ft * 128:(ft + 1) * 128, :], in_=stg[sg][:]).then_inc(s_sp, 16)
            stg_free[sg] = s_sp.n
        elif typ == "gate":
            sy.wait(s_act, s_act.n)
            sy.dma_start(out=A["sp_gate"][(ft - 32) * 128:(ft - 31) * 128, :],
                         in_=stg[sg][:]).then_inc(s_sp, 16)
            stg_free[sg] = s_sp.n
        elif typ in ("B", "C"):
            sy.wait(s_dve, s_dve.n)
            dst = A["sp_B"] if typ == "B" else A["sp_C"]
            base = (ft - 64) if typ == "B" else (ft - 96)
            sy.dma_start(out=dst[base * 128:(base + 1) * 128, :], in_=stg[sg][:]).then_inc(s_sp, 16)
            stg_free[sg] = s_sp.n
    p1_spills = s_sp.n
    p1_dt = s_act.n
    p1_pe = s_pe.n
    while _pci[0] < len(_pc):
        _emit_pc()
    P.s("sw1b_rdy", S["swc"].n)

    if PHASE_LIMIT < 2:
        sy.wait(s_sp, p1_spills)
        _dump_taps(P, A, S)
        return

    ve.wait(s_act, p1_dt)
    ve.wait(s_ld, p0_consts)
    ve.wait(s_dve, p0_negA)
    ve.tensor_scalar_add(A["dt_sb"][:], A["dt_sb"][:], A["a_dt"][:, 1:2]).then_inc(s_dve, 1)
    dt_abs = s_dve.n
    sc.wait(s_dve, dt_abs)
    sc.activation(A["dtscr"][:], A["dt_sb"][:], AF.Abs).then_inc(s_act, 1)
    sc.activation(A["dtscr"][:], A["dtscr"][:], AF.Exp, scale=-1.0).then_inc(s_act, 1)
    sc.activation(A["dtscr"][:], A["dtscr"][:], AF.Ln, bias=1.0).then_inc(s_act, 1)
    dt_ln = s_act.n
    ve.wait(s_act, dt_ln)
    ve.tensor_scalar_max(A["dt_sb"][:], A["dt_sb"][:], 0.0).then_inc(s_dve, 1)
    ve.tensor_add(A["dt_sb"][:], A["dt_sb"][:], A["dtscr"][:]).then_inc(s_dve, 1)
    ve.tensor_scalar_mul(A["dt_sb"][:], A["dt_sb"][:], A["negA"][:, 0:1]).then_inc(s_dve, 1)
    ve.tensor_tensor_scan(A["cum_fm"][:], A["seg_mask"][:], A["dt_sb"][:], 0.0,
                          ALU.mult, ALU.add).then_inc(s_dve, 1)
    for c in range(NCH):
        ve.tensor_copy(A["lasts"][:, c:c + 1],
                       A["cum_fm"][:, c * CS + CS - 1: c * CS + CS]).then_inc(s_dve, 1)
    ve.wait(s_pool, p0_pool)
    ve.tensor_tensor_scan(A["decays"][:], A["ones8"][:], A["lasts"][:], 0.0,
                          ALU.mult, ALU.add).then_inc(s_dve, 1)
    ve.tensor_sub(A["decays"][:], A["decays"][:], A["lasts"][:]).then_inc(s_dve, 1)
    p2_cum = s_dve.n
    v_tm = [A["val_tm0"], A["val_tm1"]]
    b_tm = [A["B_tm0"], A["B_tm1"]]
    yd_st = [A["yd_st0"], A["yd_st0"]]
    ce_st = [A["ce_st0"], A["ce_st0"]]
    spl_v3 = A["sp_val"].rearrange("(i p) t -> p i t", p=128)
    spl_b3 = A["sp_B"].rearrange("(i p) t -> p i t", p=128)
    spl_c3 = A["sp_C"].rearrange("(i p) t -> p i t", p=128)
    vfm3 = A["val_fm"][:].rearrange("p (i t) -> p i t", i=32)
    bfm3 = A["B_fm"][:].rearrange("p (i t) -> p i t", i=32)
    cfm3 = A["C_fm"][:].rearrange("p (i t) -> p i t", i=32)
    yd_free = [0, 0]
    ce_free = [0, 0]

    for c in range(NCH):
        sy.wait(s_sp, p1_spills)
        sy.wait(s_pe, max(P.g("p2_fm_pe"), p1_pe))
        sy.wait(s_dve, P.g("p2_fm_dve"))
        sy.wait(s_pool, P.g("p2_fm_pool"))
        for q in range(4):
            sl = slice(q * 8, (q + 1) * 8)
            sy.dma_start(out=vfm3[:, sl, :], in_=spl_v3[:, sl, c * CS:(c + 1) * CS]).then_inc(s_chl, 16)
            sy.dma_start(out=bfm3[:, sl, :], in_=spl_b3[:, sl, c * CS:(c + 1) * CS]).then_inc(s_chl, 16)
            sy.dma_start(out=cfm3[:, sl, :], in_=spl_c3[:, sl, c * CS:(c + 1) * CS]).then_inc(s_chl, 16)
        for mh in range(2):
            col = c * CS + mh * 128
            sy.dma_start(out=v_tm[mh][:], in_=A["sp_val"][:, col:col + 128], transpose=True).then_inc(s_chl, 16)
            sy.dma_start(out=b_tm[mh][:], in_=A["sp_B"][:, col:col + 128], transpose=True).then_inc(s_chl, 16)
        ld_chunk = s_chl.n

        cum_c = A["cum_fm"][:, c * CS:(c + 1) * CS]
        te.wait(s_dve, p2_cum)
        for mh in range(2):
            P.bank_w(te, 6)
            te.transpose(B[6][0:128, 0:32], cum_c[:, mh * 128:(mh + 1) * 128],
                         A["ident"][0:32, 0:32]).then_inc(s_pe, 1)
            P.bank_set_w(6, s_pe)
            ve.wait(s_pe, s_pe.n)
            P.bank_r(ve, 6)
            ve.wait(s_act, P.g("p2_acts_prev"))
            ve.tensor_copy(A["cum_tm"][:, mh * 32:(mh + 1) * 32], B[6][0:128, 0:32]).then_inc(s_dve, 1)
            P.bank_add_r(6, s_dve)
        ve.tensor_scalar_mul(A["neg_cum"][:], A["cum_tm"][:], -1.0).then_inc(s_dve, 1)
        ve.wait(s_pe, P.g("p2_diag_pe"))
        ve.tensor_scalar_mul(A["dtscr"][0:32, 0:32], A["ident"][0:32, 0:32],
                             A["lasts"][:, c:c + 1]).then_inc(s_dve, 1)
        diag_rdy = s_dve.n
        te.wait(s_dve, diag_rdy)
        P.bank_w(te, 7)
        te.matmul(B[7][:, 0:32], A["ones32"][:], A["dtscr"][0:32, 0:32],
                  start=True, stop=True).then_inc(s_pe, 1)
        P.s("p2_diag_pe", s_pe.n)
        P.bank_set_w(7, s_pe)
        ve.wait(s_pe, s_pe.n)
        P.bank_r(ve, 7)
        ve.tensor_copy(A["lastsbc"][:], B[7][:, 0:32]).then_inc(s_dve, 1)
        P.bank_add_r(7, s_dve)
        lbc_rdy = s_dve.n
        sc.wait(s_dve, lbc_rdy)
        sc.wait(s_dve, P.g("p2_h_valid"))
        sc.activation(A["el_bc"][:], A["lastsbc"][:], AF.Exp).then_inc(s_act, 1)
        el_rdy = s_act.n
        for mh in range(2):
            ve.tensor_sub(A["w_mh"][:, mh * 32:(mh + 1) * 32], A["lastsbc"][:],
                          A["cum_tm"][:, mh * 32:(mh + 1) * 32]).then_inc(s_dve, 1)
        w_sub = s_dve.n
        sc.wait(s_dve, w_sub)
        sc.activation(A["w_mh"][:], A["w_mh"][:], AF.Exp).then_inc(s_act, 1)
        w_rdy2 = s_act.n
        pre_dve = s_dve.n

        h_valid_chunk = P.g("p2_h_valid")
        pend = {}

        def s1(h):
            hp = h % 2
            d = pend.setdefault(h, {})
            te.wait(s_dve, pre_dve)
            te.wait(s_chl, ld_chunk)
            P.bank_w(te, hp)
            te.matmul(B[hp][:, 0:256],
                      A["sel32"][:, h * 128:(h + 1) * 128], cum_c,
                      start=True, stop=True).then_inc(s_pe, 1)
            d["bc"] = s_pe.n
            P.bank_set_w(hp, s_pe)
            P.bank_w(te, 2 + hp)
            for mh in range(2):
                mi = te.matmul(B[2 + hp][:, mh * 256:(mh + 1) * 256],
                               bfm3[:, h, mh * 128:(mh + 1) * 128],
                               cfm3[:, h, :], start=True, stop=True)
            mi.then_inc(s_pe, 1)
            d["cbt"] = s_pe.n
            P.bank_set_w(2 + hp, s_pe)

        def sdiff(h):
            hp = h % 2
            d = pend[h]
            ve.wait(s_pe, d["bc"])
            if hp == 0:
                ve.wait(s_act, P.g("p2_e_act"))
                ve.wait(s_dve, P.g("p2_mt"))
            P.bank_r(ve, hp)
            for mh in range(2):
                ve.scalar_tensor_tensor(
                    A["e0"][:, hp * 512 + mh * 256: hp * 512 + (mh + 1) * 256],
                    B[hp][:, 0:256],
                    A["neg_cum"][:, mh * 32 + h: mh * 32 + h + 1],
                    A["lt2"][:, mh * 256:(mh + 1) * 256],
                    ALU.add, ALU.add).then_inc(s_dve, 1)
            P.bank_add_r(hp, s_dve)
            d["diff"] = s_dve.n

        def s2pair(h):
            grp = h // 8
            gpar = grp % 2
            hr = h % 8
            d = pend[h]
            d1 = pend[h + 1]
            sc.wait(s_dve, d1["diff"])
            sc.activation(A["e0"][:], A["e0"][:], AF.Exp).then_inc(s_act, 1)
            P.s("p2_e_act", s_act.n)
            d["e"] = s_act.n
            sc.wait(s_pool, P.g("p2_expcum_pool"))
            for hh in range(2):
                sc.wait(s_pe, pend[h + hh]["bc"])
                P.bank_r(sc, hh)
                sc.activation(A["expcum"][:, hh * 256:(hh + 1) * 256],
                              B[hh][:, 0:256], AF.Exp).then_inc(s_act, 1)
                P.bank_add_r(hh, s_act)
            d["ec"] = s_act.n
            sc.wait(s_chl, ld_chunk)
            sc.wait(s_act, w_rdy2)
            for hh in range(2):
                hp = hh
                sc.wait(s_pe, P.g("p2_bw_pe_%d" % hp))
                for mh in range(2):
                    sc.activation(A["Bw"][:, hp * 256 + mh * 128: hp * 256 + (mh + 1) * 128],
                                  b_tm[mh][:, (h + hh) * 128:(h + hh + 1) * 128], AF.Copy,
                                  scale=A["w_mh"][:, mh * 32 + h + hh: mh * 32 + h + hh + 1]).then_inc(s_act, 1)
                pend[h + hh]["bw"] = s_act.n
            gp.wait(s_chl, ld_chunk)
            gp.wait(s_act, d["ec"])
            gp.wait(s_sp, s_sp.n)
            gp.wait(s_pe, P.g("p2_ce_pe_%d" % gpar))
            for hh in range(2):
                gp.tensor_mul(ce_st[gpar][:, (hr + hh) * 256:(hr + hh + 1) * 256],
                              cfm3[:, h + hh, :],
                              A["expcum"][:, hh * 256:(hh + 1) * 256]).then_inc(s_pool, 1)
            P.s("p2_expcum_pool", s_pool.n)
            d["ce"] = s_pool.n
            d1["ce"] = s_pool.n
            for hh in range(2):
                hp = hh
                dd = pend[h + hh]
                ve.wait(s_pe, dd["cbt"])
                ve.wait(s_act, d["e"])
                ve.wait(s_pe, P.g("p2_y_pe_%d" % hp))
                P.bank_r(ve, 2 + hp)
                ve.tensor_mul(A["Mt0"][:, hp * 512:(hp + 1) * 512], B[2 + hp][:, 0:512],
                              A["e0"][:, hp * 512:(hp + 1) * 512]).then_inc(s_dve, 1)
                P.bank_add_r(2 + hp, s_dve)
                dd["mt"] = s_dve.n
            P.s("p2_mt", s_dve.n)

        def s3(h):
            hp = h % 2
            grp = h // 8
            gpar = grp % 2
            hr = h % 8
            d = pend[h]
            te.wait(s_dve, d["mt"])
            te.wait(s_pool, d["ce"])
            te.wait(s_dve, h_valid_chunk)
            P.bank_w(te, 4 + hp)
            te.matmul(B[4 + hp][:, 0:256], v_tm[0][:, h * 128:(h + 1) * 128],
                      A["Mt0"][:, hp * 512: hp * 512 + 256], start=True, stop=False)
            te.matmul(B[4 + hp][:, 0:256], v_tm[1][:, h * 128:(h + 1) * 128],
                      A["Mt0"][:, hp * 512 + 256: hp * 512 + 512], start=False, stop=False)
            te.matmul(B[4 + hp][:, 0:256], A["h_bf16"][:, h * 128:(h + 1) * 128],
                      ce_st[gpar][:, hr * 256:(hr + 1) * 256],
                      start=False, stop=True).then_inc(s_pe, 1)
            d["y"] = s_pe.n
            P.s("p2_y_pe_%d" % hp, s_pe.n)
            P.s("p2_ce_pe_%d" % gpar, s_pe.n)
            P.bank_set_w(4 + hp, s_pe)
            te.wait(s_act, d["bw"])
            P.bank_w(te, 6 + hp)
            te.matmul(B[6 + hp][:, 0:128], A["Bw"][:, hp * 256: hp * 256 + 128],
                      v_tm[0][:, h * 128:(h + 1) * 128], start=True, stop=False)
            te.matmul(B[6 + hp][:, 0:128], A["Bw"][:, hp * 256 + 128: hp * 256 + 256],
                      v_tm[1][:, h * 128:(h + 1) * 128], start=False, stop=True).then_inc(s_pe, 1)
            d["st"] = s_pe.n
            P.s("p2_bw_pe_%d" % hp, s_pe.n)
            P.bank_set_w(6 + hp, s_pe)

        def s4(h):
            hp = h % 2
            grp = h // 8
            gpar = grp % 2
            hr = h % 8
            d = pend[h]
            ve.wait(s_pe, d["y"])
            ve.wait(s_sp, s_sp.n)
            P.bank_r(ve, 4 + hp)
            ve.scalar_tensor_tensor(yd_st[gpar][:, hr * 256:(hr + 1) * 256],
                                    vfm3[:, h, :], A["d_param"][:, h:h + 1],
                                    B[4 + hp][:, 0:256], ALU.mult, ALU.add).then_inc(s_dve, 1)
            P.bank_add_r(4 + hp, s_dve)
            P.s("p2_fm_dve", s_dve.n)
            d["yd"] = s_dve.n
            ve.wait(s_pe, d["st"])
            ve.wait(s_act, el_rdy)
            P.bank_r(ve, 6 + hp)
            ve.scalar_tensor_tensor(A["h_f32"][:, h * 128:(h + 1) * 128],
                                    A["h_f32"][:, h * 128:(h + 1) * 128],
                                    A["el_bc"][:, h:h + 1],
                                    B[6 + hp][:, 0:128], ALU.mult, ALU.add).then_inc(s_dve, 1)
            P.bank_add_r(6 + hp, s_dve)
            ve.tensor_copy(A["h_bf16"][:, h * 128:(h + 1) * 128],
                           A["h_f32"][:, h * 128:(h + 1) * 128]).then_inc(s_dve, 1)
            P.s("p2_h_valid", s_dve.n)
            if h % 8 == 7:
                sy.wait(s_dve, d["yd"])
                sy.dma_start(out=A["sp_Yd"][c * 128:(c + 1) * 128, grp * 2048:(grp + 1) * 2048],
                             in_=yd_st[gpar][:]).then_inc(s_sp, 16)
                sy.wait(s_pool, d["ce"])
                sy.wait(s_pe, d["y"])
                sy.dma_start(out=A["sp_Cexp"][c * 128:(c + 1) * 128, grp * 2048:(grp + 1) * 2048],
                             in_=ce_st[gpar][:]).then_inc(s_sp, 16)
            del pend[h]

        s1(0)
        sdiff(0)
        s1(1)
        sdiff(1)
        for p in range(H // 2):
            h = 2 * p
            s2pair(h)
            if h + 2 < H:
                s1(h + 2)
                sdiff(h + 2)
                s1(h + 3)
                sdiff(h + 3)
            s3(h)
            s4(h)
            s3(h + 1)
            s4(h + 1)

        P.s("p2_fm_pe", s_pe.n)
        P.s("p2_fm_pool", s_pool.n)
        P.s("p2_acts_prev", s_act.n)
        P.s("p2_pool_prev", s_pool.n)
    p2_spills = s_sp.n
    p2_h = P.g("p2_h_valid")

    if PHASE_LIMIT < 3:
        sy.wait(s_sp, p2_spills)
        _dump_taps(P, A, S)
        return

    ag3 = A["ag_in"].rearrange("(h n) d -> n h d", n=128)
    h3 = A["h_f32"][:].rearrange("p (h d) -> p h d", h=H)
    gp.wait(s_dve, p2_h)
    for q in range(4):
        sl = slice(q * 8, (q + 1) * 8)
        gp.dma_start(out=ag3[:, sl, :], in_=h3[:, sl, :]).then_inc(s_w, 16)
    ag_in_done = s_w.n
    gp.wait(s_w, ag_in_done)
    gp.collective_compute("AllGather", ALU.bypass,
                          replica_groups=[[0, 1], [2, 3], [4, 5], [6, 7]],
                          ins=[A["ag_in"].opt()], outs=[A["ag_out"].opt()]).then_inc(s_cc, 1)
    ago3 = A["ag_out"][0:4096, :].rearrange("(h n) d -> n h d", n=128)
    sy.wait(s_cc, 1)
    for q in range(4):
        sl = slice(q * 8, (q + 1) * 8)
        sy.dma_start(out=h3[:, sl, :], in_=ago3[:, sl, :]).then_inc(s_ld, 16)
    h8_done = s_ld.n

    yd_l = A["val_fm"]
    ce_l = A["B_fm"]
    gt_l = A["C_fm"]
    gt3 = gt_l[:].rearrange("p (i t) -> p i t", i=32)
    spg3 = A["sp_gate"].rearrange("(i p) t -> p i t", p=128)
    yg_st = [A["yd_st0"], A["ce_st0"]]
    yg_free = [0, 0]
    spyg3 = A["sp_Yg"].rearrange("(i p) t -> p i t", p=128)
    for c in range(NCH):
        sy.wait(s_sp, p2_spills)
        sy.wait(s_pe, P.g("p25_ld_pe"))
        sy.wait(s_dve, P.g("p25_ld_dve"))
        sy.wait(s_pool, P.g("p25_ld_pool"))
        sy.dma_start(out=yd_l[:], in_=A["sp_Yd"][c * 128:(c + 1) * 128, :]).then_inc(s_chl, 16)
        sy.dma_start(out=ce_l[:], in_=A["sp_Cexp"][c * 128:(c + 1) * 128, :]).then_inc(s_chl, 16)
        for q in range(4):
            sl = slice(q * 8, (q + 1) * 8)
            sy.dma_start(out=gt3[:, sl, :], in_=spg3[:, sl, c * CS:(c + 1) * CS]).then_inc(s_chl, 16)
        ld_c = s_chl.n
        ve.wait(s_dve, p2_cum)
        ve.wait(s_ld, h8_done)
        ve.wait(s_pe, P.g("p2_diag_pe"))
        ve.tensor_scalar_mul(A["dtscr"][0:32, 0:32], A["ident"][0:32, 0:32],
                             A["decays"][:, c:c + 1]).then_inc(s_dve, 1)
        diag_rdy = s_dve.n
        te.wait(s_dve, diag_rdy)
        P.bank_w(te, 7)
        te.matmul(B[7][:, 0:32], A["ones32"][:], A["dtscr"][0:32, 0:32],
                  start=True, stop=True).then_inc(s_pe, 1)
        P.s("p2_diag_pe", s_pe.n)
        P.bank_set_w(7, s_pe)
        ve.wait(s_pe, s_pe.n)
        P.bank_r(ve, 7)
        ve.wait(s_act, P.g("p25_dbc_act"))
        ve.tensor_copy(A["lastsbc"][:], B[7][:, 0:32]).then_inc(s_dve, 1)
        P.bank_add_r(7, s_dve)
        sc.wait(s_dve, s_dve.n)
        sc.wait(s_dve, P.g("p25_cl_last"))
        sc.activation(A["D_bc"][:], A["lastsbc"][:], AF.Exp).then_inc(s_act, 1)
        P.s("p25_dbc_act", s_act.n)
        dbc_rdy = s_act.n
        def cl(h):
            hp = h % 4
            ve.wait(s_act, dbc_rdy)
            ve.wait(s_ld, h8_done)
            ve.wait(s_pe, P.g("p25_cl_pe_%d" % hp))
            ve.tensor_scalar(A["clbf"][:, hp * 128:(hp + 1) * 128],
                             A["h_f32"][:, h * 128:(h + 1) * 128],
                             A["flag"][:, 0:1], A["D_bc"][:, h:h + 1],
                             ALU.mult, ALU.mult).then_inc(s_dve, 1)
            P.s("p25_cl_last", s_dve.n)
            P.s("p25_cl_rdy_%d" % hp, s_dve.n)

        def corr(h):
            hp = h % 4
            te.wait(s_dve, P.g("p25_cl_rdy_%d" % hp))
            te.wait(s_chl, ld_c)
            P.bank_w(te, hp)
            te.matmul(B[hp][:, 0:256], A["clbf"][:, hp * 128:(hp + 1) * 128],
                      ce_l[:, h * 256:(h + 1) * 256], start=True, stop=True).then_inc(s_pe, 1)
            P.s("p25_ld_pe", s_pe.n)
            P.s("p25_cl_pe_%d" % hp, s_pe.n)
            P.s("p25_corr_%d" % hp, s_pe.n)
            P.bank_set_w(hp, s_pe)

        def finish(h):
            hp = h % 4
            grp = h // 8
            gpar = grp % 2
            ve.wait(s_pe, P.g("p25_corr_%d" % hp))
            ve.wait(s_sp, s_sp.n)
            P.bank_r(ve, hp)
            ve.tensor_add(A["ctmp"][:, hp * 256:(hp + 1) * 256],
                          yd_l[:, h * 256:(h + 1) * 256], B[hp][:, 0:256]).then_inc(s_dve, 1)
            P.bank_add_r(hp, s_dve)
            ve.tensor_mul(yg_st[gpar][:, (h % 8) * 256:(h % 8 + 1) * 256],
                          A["ctmp"][:, hp * 256:(hp + 1) * 256],
                          gt3[:, h, :]).then_inc(s_dve, 1)
            P.s("p25_ld_dve", s_dve.n)
            P.s("p25_ct_pool_%d" % hp, s_pool.n)
            P.s("p25_ld_pool", s_pool.n)
            yg_rdy = s_dve.n
            if h % 8 == 7:
                sy.wait(s_dve, yg_rdy)
                src = yg_st[gpar][:].rearrange("p (i t) -> p i t", i=8)
                sy.dma_start(out=spyg3[:, grp * 8:(grp + 1) * 8, c * CS:(c + 1) * CS],
                             in_=src).then_inc(s_sp, 16)

        for h0 in range(4):
            cl(h0)
        for h in range(H):
            corr(h)
            finish(h)
            if h + 4 < H:
                cl(h + 4)
    p25_spills = s_sp.n

    if PHASE_LIMIT < 4:
        sy.wait(s_sp, p25_spills)
        _dump_taps(P, A, S)
        return

    for i in range(8):
        gp.dma_start(out=bass.AP(A["d_w1p"].tensor, i * 128 * 16384,
                                 [[16384, 128], [2048, 8], [1, 2048]]),
                     in_=bass.AP(A["in_w1"].tensor, i * 8 * 128 * 2048,
                                 [[2048, 128], [128 * 2048, 8], [1, 2048]])).then_inc(s_wc, 16)
        gp.dma_start(out=bass.AP(A["d_w2p"].tensor, i * 128 * 16384,
                                 [[16384, 128], [2048, 8], [1, 2048]]),
                     in_=bass.AP(A["in_w2cat"].tensor, i * 8 * 128 * 2048,
                                 [[2048, 128], [128 * 2048, 8], [1, 2048]])).then_inc(s_wc, 16)
    P.s("wb_cast", s_wc.n)
    sy.wait(s_sp, p25_spills)
    sy.wait(s_pool, s_pool.n)
    sy.wait(s_dve, s_dve.n)
    sy.wait(s_pe, s_pe.n)
    sy.wait(S["swc"], P.g("sw1b_rdy"))
    s_wh = [S["wh0"], S["wh1"]]
    for k in range(32):
        sy.dma_start(out=A["wot"][:, k * 2048:(k + 1) * 2048],
                     in_=A["d_woutb"][k * 128:(k + 1) * 128, :]).then_inc(s_wh[0], 16)
    wot_rdy = s_wh[0].n
    ygb = [A["yg_t"], A["yg_t2"]]
    for tt in range(16):
        par = tt % 2
        sy.wait(s_sp, p25_spills)
        sy.wait(s_pe, P.g("p3_mm_%d" % par))
        ygv = ygb[par][:].rearrange("p (i t) -> p i t", i=32)
        for q in range(4):
            sl = slice(q * 8, (q + 1) * 8)
            sy.dma_start(out=ygv[:, sl, :], in_=spyg3[:, sl, tt * 128:(tt + 1) * 128]).then_inc(s_chl, 16)
        sy.wait(s_dve, P.g("p3_x1_rdy"))
        sy.dma_start(out=A["xtm_b"][:], in_=A["in_x_tm"][tt * 128:(tt + 1) * 128, :]).then_inc(s_chl, 16)
        yg_rdy2 = s_chl.n
        te.wait(s_chl, yg_rdy2)
        te.wait(s_wh[0], wot_rdy)
        mmd = [0] * 4
        for dmg in range(4):
            P.bank_w(te, dmg)
            for k in range(32):
                mi = te.matmul(B[dmg][:], ygv[:, k, :],
                               A["wot"][:, k * 2048 + dmg * 512: k * 2048 + (dmg + 1) * 512],
                               start=(k == 0), stop=(k == 31))
            mi.then_inc(s_pe, 1)
            mmd[dmg] = s_pe.n
            P.bank_set_w(dmg, s_pe)
        P.s("p3_mm_%d" % par, s_pe.n)
        for dmg in range(4):
            ve.wait(s_pe, mmd[dmg])
            P.bank_r(ve, dmg)
            ve.scalar_tensor_tensor(A["xtm_b"][:, dmg * 512:(dmg + 1) * 512],
                                    B[dmg][:], 1.0,
                                    A["xtm_b"][:, dmg * 512:(dmg + 1) * 512],
                                    ALU.mult, ALU.add).then_inc(s_dve, 1)
            P.bank_add_r(dmg, s_dve)
        ve.reduce_sum(A["ln_stats"][:, 0:1], A["xtm_b"][:], axis=AX.X).then_inc(s_dve, 1)
        x2_rdy = s_dve.n
        sc.wait(s_dve, x2_rdy)
        sc.activation(A["sq_b"][:], A["xtm_b"][:], AF.Square,
                      accum_out=A["ln_stats"][:, 1:2]).then_inc(s_act, 1)
        sq_rdy = s_act.n
        ve.wait(s_act, sq_rdy)
        ve.tensor_scalar_mul(A["ln_stats"][:, 2:3], A["ln_stats"][:, 0:1], 1.0 / DM).then_inc(s_dve, 1)
        ve.tensor_scalar_mul(A["ln_stats"][:, 3:4], A["ln_stats"][:, 1:2], 1.0 / DM).then_inc(s_dve, 1)
        ve.tensor_mul(A["ln_stats"][:, 4:5], A["ln_stats"][:, 2:3], A["ln_stats"][:, 2:3]).then_inc(s_dve, 1)
        ve.tensor_sub(A["ln_stats"][:, 4:5], A["ln_stats"][:, 3:4], A["ln_stats"][:, 4:5]).then_inc(s_dve, 1)
        ve.tensor_scalar_add(A["ln_stats"][:, 4:5], A["ln_stats"][:, 4:5], EPS).then_inc(s_dve, 1)
        var_rdy = s_dve.n
        sc.wait(s_dve, var_rdy)
        sc.activation(A["ln_stats"][:, 5:6], A["ln_stats"][:, 4:5], AF.Ln).then_inc(s_act, 1)
        ln_rdy = s_act.n
        ve.wait(s_act, ln_rdy)
        ve.tensor_scalar_mul(A["ln_stats"][:, 5:6], A["ln_stats"][:, 5:6], -0.5).then_inc(s_dve, 1)
        nl_rdy = s_dve.n
        sc.wait(s_dve, nl_rdy)
        sc.activation(A["ln_stats"][:, 6:7], A["ln_stats"][:, 5:6], AF.Exp).then_inc(s_act, 1)
        rstd_rdy = s_act.n
        ve.wait(s_act, rstd_rdy)
        ve.wait(s_sp, s_sp.n)
        ve.wait(s_spg, s_spg.n)
        ve.tensor_scalar(A["x1stage"][:], A["xtm_b"][:], A["ln_stats"][:, 2:3],
                         A["ln_stats"][:, 6:7], ALU.subtract, ALU.mult).then_inc(s_dve, 1)
        x1_rdy = s_dve.n
        P.s("p3_x1_rdy", x1_rdy)
        sy.wait(s_dve, x1_rdy)
        sy.dma_start(out=A["sp_x1f"][tt * 128:(tt + 1) * 128, :], in_=A["x1stage"][:]).then_inc(s_sp, 16)
        gp.wait(s_dve, x1_rdy)
        gp.dma_start(out=A["sp_x1bf"][tt * 128:(tt + 1) * 128, :], in_=A["x1stage"][:]).then_inc(s_spg, 16)
        P.s("p3_x1_sp", s_sp.n)
        P.s("p3_x1_spg", s_spg.n)
    p3_done = s_sp.n
    p3_done_g = s_spg.n

    if PHASE_LIMIT < 5:
        sy.wait(s_sp, p3_done)
        _dump_taps(P, A, S)
        return

    for i in range(16):
        sy.wait(s_sp, p3_done)
        sy.wait(s_spg, p3_done_g)
        sy.dma_start(out=A["x1fm"][:, i * 2048:(i + 1) * 2048],
                     in_=A["sp_x1bf"][:, i * 128:(i + 1) * 128], transpose=True).then_inc(s_chl, 16)
    x1fm_rdy = s_chl.n
    gp.dma_start(out=A["gwbf"][:].rearrange("p (k e) -> p k e", k=16),
                 in_=A["in_gate_w"].rearrange("(k p) e -> p k e", p=128)).then_inc(s_w, 16)
    gw_rdy = s_w.n
    x1v = A["x1fm"][:].rearrange("p (k t) -> p k t", k=16)
    gwv = A["gwbf"][:].rearrange("p (k e) -> p k e", k=16)
    g = A["gtmp"]
    g2 = A["gtmp2"]
    for tt in range(16):
        te.wait(s_chl, x1fm_rdy)
        te.wait(s_w, gw_rdy)
        P.bank_w(te, 2)
        for k in range(16):
            mi = te.matmul(B[2][:, 0:8], x1v[:, k, tt * 128:(tt + 1) * 128], gwv[:, k, :],
                           start=(k == 0), stop=(k == 15))
        mi.then_inc(s_pe, 1)
        lg_done = s_pe.n
        P.bank_set_w(2, s_pe)
        ve.wait(s_pe, lg_done)
        ve.wait(s_act, P.g("p4_g_act"))
        P.bank_r(ve, 2)
        ve.tensor_copy(g[:, 0:8], B[2][:, 0:8]).then_inc(s_dve, 1)
        P.bank_add_r(2, s_dve)
        ve.reduce_max(g[:, 8:9], g[:, 0:8], axis=AX.X).then_inc(s_dve, 1)
        ve.tensor_scalar(g2[:, 0:8], g[:, 0:8], g[:, 8:9], None, ALU.is_ge).then_inc(s_dve, 1)
        ve.scalar_tensor_tensor(g2[:, 8:16], g2[:, 0:8], -1e30, g[:, 0:8],
                                ALU.mult, ALU.add).then_inc(s_dve, 1)
        ve.reduce_max(g[:, 9:10], g2[:, 8:16], axis=AX.X).then_inc(s_dve, 1)
        ve.tensor_scalar_mul(g[:, 10:11], g[:, 8:9], -1.0).then_inc(s_dve, 1)
        ve.tensor_sub(g[:, 11:12], g[:, 9:10], g[:, 8:9]).then_inc(s_dve, 1)
        pre_e = s_dve.n
        sc.wait(s_dve, pre_e)
        sc.activation(g2[:, 0:8], g[:, 0:8], AF.Exp, bias=g[:, 10:11]).then_inc(s_act, 1)
        sc.activation(g[:, 12:13], g[:, 11:12], AF.Exp).then_inc(s_act, 1)
        P.s("p4_g_act", s_act.n)
        e_done2 = s_act.n
        ve.wait(s_act, e_done2)
        ve.tensor_scalar_add(g[:, 12:13], g[:, 12:13], 1.0).then_inc(s_dve, 1)
        ve.reciprocal(g[:, 13:14], g[:, 12:13]).then_inc(s_dve, 1)
        ve.tensor_scalar(g2[:, 8:16], g[:, 0:8], g[:, 9:10], None, ALU.is_ge).then_inc(s_dve, 1)
        ve.tensor_scalar(g2[:, 0:8], g2[:, 0:8], g[:, 13:14], None, ALU.mult).then_inc(s_dve, 1)
        ve.tensor_mul(g[:, 0:8], g2[:, 0:8], g2[:, 8:16]).then_inc(s_dve, 1)
        gates_rdy = s_dve.n
        te.wait(s_dve, gates_rdy)
        P.bank_w(te, 3)
        te.transpose(B[3][0:8, 0:128], g[:, 0:8], A["ident"][:]).then_inc(s_pe, 1)
        gtr_done = s_pe.n
        P.bank_set_w(3, s_pe)
        ve.wait(s_pe, gtr_done)
        P.bank_r(ve, 3)
        ve.tensor_copy(A["g_rows"][:, tt * 128:(tt + 1) * 128], B[3][0:8, 0:128]).then_inc(s_dve, 1)
        P.bank_add_r(3, s_dve)
    grows_rdy = s_dve.n

    sy.dma_start(out=A["d_gidx"], in_=A["in_gidx_init"]).then_inc(s_gx, 16)
    for nm in ["l8u", "srow16", "iota_t", "iota_w"]:
        sy.dma_start(out=A[nm][:], in_=A["in_" + nm]).then_inc(s_gx, 16)
    rt_consts = s_gx.n
    gp.memset(A["ones1f"][:], 1.0).then_inc(s_pool, 1)
    ones1f_rdy = s_pool.n

    rr = A["rts"]
    ve.wait(s_dve, grows_rdy)
    ve.tensor_scalar(A["ind8"][:], A["g_rows"][:], 0.0, None, ALU.is_gt).then_inc(s_dve, 1)
    ve.memset(A["ones8r"][:], 1.0).then_inc(s_dve, 1)
    ve.tensor_tensor_scan(A["pos8"][:], A["ones8r"][:], A["ind8"][:], 0.0,
                          ALU.mult, ALU.add).then_inc(s_dve, 1)
    ve.tensor_copy(rr[:, 0:1], A["pos8"][:, 2047:2048]).then_inc(s_dve, 1)
    ve.wait(s_gx, rt_consts)
    ve.tensor_scalar(A["cmp16"][:], A["srow16"][:], rr[:, 0:1], None, ALU.is_lt).then_inc(s_dve, 1)
    ve.tensor_reduce(rr[:, 1:2], A["cmp16"][:], AX.X, ALU.add).then_inc(s_dve, 1)
    ve.tensor_scalar_mul(rr[:, 2:3], rr[:, 1:2], 512.0).then_inc(s_dve, 1)
    c128_rdy = s_dve.n
    te.wait(s_dve, c128_rdy)
    te.wait(s_gx, rt_consts)
    P.bank_w(te, 6)
    te.matmul(B[6][0:8, 0:1], A["l8u"][:], rr[:, 2:3], start=True, stop=True).then_inc(s_pe, 1)
    P.bank_set_w(6, s_pe)
    ve.wait(s_pe, s_pe.n)
    P.bank_r(ve, 6)
    ve.tensor_copy(rr[:, 3:4], B[6][0:8, 0:1]).then_inc(s_dve, 1)
    P.bank_add_r(6, s_dve)
    ve.tensor_add(rr[:, 4:5], rr[:, 3:4], rr[:, 2:3]).then_inc(s_dve, 1)
    ve.tensor_scalar(A["pos8"][:], A["pos8"][:], rr[:, 3:4], -1.0, ALU.add, ALU.add).then_inc(s_dve, 1)
    ve.tensor_scalar(A["ind8"][:], A["ind8"][:], -1e9, 1e9, ALU.mult, ALU.add).then_inc(s_dve, 1)
    ve.tensor_add(A["pos8"][:], A["pos8"][:], A["ind8"][:]).then_inc(s_dve, 1)
    ve.tensor_scalar(A["cmp16"][:], A["srow16"][:], rr[:, 4:5], None, ALU.is_ge).then_inc(s_dve, 1)
    slot_rdy = s_dve.n
    te.wait(s_dve, slot_rdy)
    P.bank_w(te, 6)
    te.matmul(B[6][0:1, 0:16], A["ones32"][0:8, 0:1], A["cmp16"][:],
              start=True, stop=True).then_inc(s_pe, 1)
    P.bank_set_w(6, s_pe)
    ve.wait(s_pe, s_pe.n)
    P.bank_r(ve, 6)
    ve.tensor_copy(A["egrow"][:], B[6][0:1, 0:16]).then_inc(s_dve, 1)
    P.bank_add_r(6, s_dve)
    ve.tensor_scalar_mul(A["wbrow"][:], A["egrow"][:], 128.0).then_inc(s_dve, 1)
    et_rdy = s_dve.n
    te.wait(s_dve, et_rdy)
    te.wait(s_pool, ones1f_rdy)
    P.bank_w(te, 7)
    te.matmul(B[7][:, 0:16], A["ones1f"][:], A["wbrow"][:], start=True, stop=True).then_inc(s_pe, 1)
    P.bank_set_w(7, s_pe)
    ve.wait(s_pe, s_pe.n)
    P.bank_r(ve, 7)
    ve.tensor_copy(A["wb_bc"][:], B[7][:, 0:16]).then_inc(s_dve, 1)
    P.bank_add_r(7, s_dve)
    wbbc_rdy = s_dve.n

    slot_tm = [A["slot_tm0"], A["slot_tm1"]]
    gate_tm = [A["gate_tm0"], A["gate_tm1"]]
    stage2 = [A["stage2a"], A["stage2b"], A["stage2c"], A["stage2d"]]
    s_scat = [s_ig, S["e8a"]]
    g1t = A["gtmp"]
    for tt in range(16):
        par = tt % 2
        te.wait(s_dve, slot_rdy)
        te.wait(s_dve, P.g("p4_rt_ve_%d" % par))
        P.bank_w(te, 6)
        te.transpose(B[6][0:128, 0:8], A["pos8"][:, tt * 128:(tt + 1) * 128],
                     A["ident"][0:8, 0:8]).then_inc(s_pe, 1)
        P.bank_set_w(6, s_pe)
        tr1 = s_pe.n
        P.bank_w(te, 7)
        te.transpose(B[7][0:128, 0:8], A["g_rows"][:, tt * 128:(tt + 1) * 128],
                     A["ident"][0:8, 0:8]).then_inc(s_pe, 1)
        P.bank_set_w(7, s_pe)
        tr2 = s_pe.n
        ve.wait(s_pe, tr2)
        P.bank_r(ve, 6)
        ve.tensor_copy(slot_tm[par][:], B[6][0:128, 0:8]).then_inc(s_dve, 1)
        P.bank_add_r(6, s_dve)
        P.bank_r(ve, 7)
        ve.tensor_copy(gate_tm[par][:], B[7][0:128, 0:8]).then_inc(s_dve, 1)
        P.bank_add_r(7, s_dve)
        g2t = A["gtmp2"]
        ve.tensor_reduce(g1t[:, 0:1], slot_tm[par][:], AX.X, ALU.min).then_inc(s_dve, 1)
        ve.tensor_scalar(g1t[:, 2:10], slot_tm[par][:], g1t[:, 0:1], None, ALU.is_le).then_inc(s_dve, 1)
        ve.scalar_tensor_tensor(g2t[:, 0:8], g1t[:, 2:10], 1e9, slot_tm[par][:],
                                ALU.mult, ALU.add).then_inc(s_dve, 1)
        ve.tensor_reduce(g1t[:, 1:2], g2t[:, 0:8], AX.X, ALU.min).then_inc(s_dve, 1)
        ve.tensor_copy(A["s1all"][:, tt:tt + 1], g1t[:, 0:1]).then_inc(s_dve, 1)
        ve.tensor_copy(A["s2all"][:, tt:tt + 1], g1t[:, 1:2]).then_inc(s_dve, 1)
        ve.tensor_scalar(g2t[:, 8:16], slot_tm[par][:], g1t[:, 1:2], None, ALU.is_le).then_inc(s_dve, 1)
        ve.tensor_sub(g2t[:, 8:16], g2t[:, 8:16], g1t[:, 2:10]).then_inc(s_dve, 1)
        ve.tensor_mul(g1t[:, 2:10], g1t[:, 2:10], gate_tm[par][:]).then_inc(s_dve, 1)
        ve.tensor_mul(g2t[:, 8:16], g2t[:, 8:16], gate_tm[par][:]).then_inc(s_dve, 1)
        scq = tt % 2
        ic0 = 2 * scq
        ve.wait(s_scat[scq], P.g("p4_scat_%d" % scq))
        ve.wait(s_gx, rt_consts)
        ve.tensor_copy(A["islot"][:, ic0:ic0 + 1], g1t[:, 0:1]).then_inc(s_dve, 1)
        ve.tensor_copy(A["islot"][:, ic0 + 1:ic0 + 2], g1t[:, 1:2]).then_inc(s_dve, 1)
        ve.tensor_copy(stage2[2 * scq][:, 0:1], A["iota_t"][:, tt:tt + 1]).then_inc(s_dve, 1)
        ve.tensor_reduce(stage2[2 * scq][:, 1:2], g1t[:, 2:10], AX.X, ALU.add).then_inc(s_dve, 1)
        ve.tensor_copy(stage2[2 * scq + 1][:, 0:1], A["iota_t"][:, tt:tt + 1]).then_inc(s_dve, 1)
        ve.tensor_reduce(stage2[2 * scq + 1][:, 1:2], g2t[:, 8:16], AX.X, ALU.add).then_inc(s_dve, 1)
        st_rdy = s_dve.n
        gp.wait(s_dve, st_rdy)
        gp.wait(s_gx, rt_consts)
        if tt == 0:
            gp.reg_mov(A["reg_bchk"], SLOTS - 1)
        for j in range(2):
            gp.indirect_dma_start(
                out=A["d_gidx"],
                out_offset=bass.IndirectOffsetOnAxis(A["islot"][:, ic0 + j:ic0 + j + 1], 0),
                in_=stage2[2 * scq + j][:], in_offset=None,
                bounds_check=A["reg_bchk"], oob_is_err=False).then_inc(s_scat[scq], 16)
        P.s("p4_scat_%d" % scq, s_scat[scq].n)
    scat_done = [s_scat[0].n, s_scat[1].n]

    s_gst = [S["gst"], S["gse"]]
    w1t = [A["w1t0"], A["w1t1"]]
    ghst = [A["ghe_st0"], A["ghe_st1"]]
    w1_free = [0, 0]
    for rs in range(16):
        par = rs % 2
        sy.wait(s_pe, w1_free[par])
        if rs == 0:
            sy.wait(S["swc"], P.g("sw1b_rdy"))
        sy.dma_start(out=w1t[par][:], in_=A["d_sw1b"][rs * 128:(rs + 1) * 128, :]).then_inc(s_wh[par], 16)
        w1_rdy = s_wh[par].n
        sc.wait(s_gst[par], P.g("p4_ghst_%d" % par))
        for tg in range(4):
            bk = tg % 2
            te.wait(s_wh[par], w1_rdy)
            te.wait(s_chl, x1fm_rdy)
            P.bank_w(te, bk)
            for k in range(16):
                mi = te.matmul(B[bk][:], w1t[par][:, k * 128:(k + 1) * 128],
                               x1v[:, k, tg * 512:(tg + 1) * 512], start=(k == 0), stop=(k == 15))
            mi.then_inc(s_pe, 1)
            h_done = s_pe.n
            P.bank_set_w(bk, s_pe)
            if tg == 3:
                w1_free[par] = s_pe.n
            sc.wait(s_pe, h_done)
            P.bank_r(sc, bk)
            sc.activation(ghst[par][:, tg * 512:(tg + 1) * 512], B[bk][:], AF.Gelu,
                          bias=A["sb1"][:, rs:rs + 1]).then_inc(s_act, 1)
            P.bank_add_r(bk, s_act)
        sc.wait(s_act, s_act.n)
        sc.dma_start(out=A["sp_ghe"][rs * 128:(rs + 1) * 128, :], in_=ghst[par][:]).then_inc(s_gst[par], 16)
        P.s("p4_ghst_%d" % par, s_gst[par].n)
    shared_done = [s_gst[0].n, s_gst[1].n]
    P.s("p4_wc_pe_0", s_pe.n)
    P.s("p4_wc_pe_1", s_pe.n)

    gp.wait(s_ig, scat_done[0])
    gp.wait(S["e8a"], scat_done[1])
    gp.dma_start(out=A["gixall"][:].rearrange("p (s c) -> p s c", c=2),
                 in_=bass.AP(A["d_gidx"].tensor, 0,
                             [[2, 128], [256, NST], [1, 2]])).then_inc(s_ig, 16)
    gix_rdy = s_ig.n
    ve.wait(s_ig, gix_rdy)
    ve.tensor_copy(A["giiall"][:], A["gixall"][:]).then_inc(s_dve, 1)
    ve.tensor_copy(A["gvall"][:].rearrange("p (s o) -> p s o", o=1),
                   A["gixall"][:].rearrange("p (s c) -> p s c", c=2)[:, :, 1:2]).then_inc(s_dve, 1)
    cv_rdy = s_dve.n
    P.s("p4_cv", cv_rdy)
    gp.wait(s_dve, cv_rdy)
    gp.reg_mov(A["reg_bchk"], T - 1)
    slot = [s_ig, S["e8a"], S["e8b"], S["b8b"],
            S["g4"], S["g5"], S["g6"], S["g7"]]
    ssp = [S["gt%d" % i] for i in range(NSPL)]
    assert NSPL == 10
    xg_sb = [A["xg_sb%d" % i] for i in range(8)]
    g_cnt = {}
    st_cnt = {}
    LAG = 7
    for it in range(NST + LAG):
        si = it
        if si < NST:
            q = si % 8
            if si >= 8:
                gp.wait(ssp[(si - 8) // TPS], st_cnt[si - 8])
            gp.indirect_dma_start(
                out=xg_sb[q][:], out_offset=None,
                in_=A["sp_x1bf"],
                in_offset=bass.IndirectOffsetOnAxis(A["giiall"][:, 2 * si:2 * si + 1], 0),
                bounds_check=A["reg_bchk"], oob_is_err=False).then_inc(slot[q], 16)
            g_cnt[si] = slot[q].n
        sj = it - LAG
        if 0 <= sj < NST:
            q = sj % 8
            gp.wait(slot[q], g_cnt[sj])
            gp.dma_start(out=A["d_xg"][sj * 128:(sj + 1) * 128, :],
                         in_=xg_sb[q][:]).then_inc(ssp[sj // TPS], 16)
            st_cnt[sj] = ssp[sj // TPS].n
    split_cnt = [st_cnt[s5 * TPS + TPS - 1] for s5 in range(NSPL)]
    xtb = [A["xts0a"], A["xts0b"], A["xts0c"], A["xts0d"]]
    s_xt4 = [S["xts"], S["xto"], S["xss0"], S["xss1"]]
    s_ld4 = [s_gx, S["gxo"], S["gxs0"], S["gxs1"]]
    qeng = [sy, sc]
    ld_cnt = {}
    xtr_done = [None] * NSPL

    def _xt_store(j):
        qj = j % 2
        sl = qj * 2 + (j // 2) % 2
        qeng[qj].wait(s_ld4[sl], ld_cnt[j])
        s5j, ij = j // 16, j % 16
        qeng[qj].dma_start(
            out=A["d_xgT"][ij * 128:(ij + 1) * 128, s5j * SSL:(s5j + 1) * SSL],
            in_=xtb[sl][:, 0:SSL]).then_inc(s_xt4[sl], 16)
        P.s("p4_xtb_%d" % sl, s_xt4[sl].n)
        if ij == 15:
            xtr_done[s5j] = tuple(x.n for x in s_xt4)

    for idx in range(NSPL * 16):
        s5, i = idx // 16, idx % 16
        q = idx % 2
        sl = q * 2 + (idx // 2) % 2
        qeng[q].wait(ssp[s5], split_cnt[s5])
        qeng[q].wait(s_xt4[sl], P.g("p4_xtb_%d" % sl))
        qeng[q].dma_start(out=xtb[sl][:, 0:SSL],
                          in_=A["d_xg"][s5 * SSL:(s5 + 1) * SSL, i * 128:(i + 1) * 128],
                          transpose=True).then_inc(s_ld4[sl], 16)
        ld_cnt[idx] = s_ld4[sl].n
        if idx >= 2:
            _xt_store(idx - 2)
    _xt_store(NSPL * 16 - 2)
    _xt_store(NSPL * 16 - 1)

    w1c = [[A["w1cA0"], A["w1cA1"]], [A["w1cB0"], A["w1cB1"]]]
    w2c = [[A["w2cA0"], A["w2cA1"]], [A["w2cB0"], A["w2cB1"]]]
    h_fm = [A["h_fm0"], A["h_fm1"]]
    osb = [A["osb0"], A["osb1"]]
    wii = [A["wii0"], A["wii1"]]
    s_m8 = [S["m8a"], S["m8b"]]
    s_os = [S["os0"], S["os1"]]
    s_xq = [s_gh[0], S["xq1"], S["xq2"], S["xq3"]]
    xv3 = A["xgg"][:].rearrange("p (k t) -> p k t", k=16)
    dxt3 = A["d_xgT"].rearrange("(k p) t -> p k t", p=128)
    for si in range(NST + 1):
      if si < NST:
        spar = si % 2
        grp, st = si // 4, si % 4
        gpar = grp % 2
        if st == 0:
            for _q in range(4):
                sy.wait(s_xt4[_q], xtr_done[(grp * 512 + 511) // SSL][_q])
            sy.wait(s_pe, P.g("p4_xgg_pe"))
            for q in range(4):
                sy.dma_start(out=xv3[:, q * 4:(q + 1) * 4, :],
                             in_=dxt3[:, q * 4:(q + 1) * 4, grp * 512:(grp + 1) * 512]).then_inc(s_xq[q], 16)
                P.s("p4_xgq_%d" % q, s_xq[q].n)
            ve.wait(s_dve, wbbc_rdy)
            ve.wait(s_m8[gpar], P.g("p4_m8b_%d" % gpar))
            ve.tensor_scalar_add(A["wif"][:, 0:1], A["iota_w"][:, 0:1],
                                 A["wb_bc"][:, grp:grp + 1]).then_inc(s_dve, 1)
            ve.tensor_scalar_mul(A["wif"][:, 1:2], A["wb_bc"][:, grp:grp + 1],
                                 1.0 / 128.0).then_inc(s_dve, 1)
            ve.tensor_copy(wii[gpar][:, 0:2], A["wif"][:, 0:2]).then_inc(s_dve, 1)
            wi_rdy = s_dve.n
            gp.wait(s_dve, wi_rdy)
            if si == 0:
                gp.wait(s_wc, P.g("wb_cast"))
                gp.wait(s_gst[0], shared_done[0])
                gp.wait(s_gst[1], shared_done[1])
                gp.reg_mov(A["reg_bchk"], 1023)
            gp.wait(s_pe, P.g("p4_wc_pe_%d" % gpar))
            for half in range(2):
                gp.indirect_dma_start(
                    out=bass.AP(w1c[half][gpar].tensor, 0, [[8192, 128], [1, 8192]]),
                    out_offset=None,
                    in_=A["d_w1p"], in_offset=bass.IndirectOffsetOnAxis(wii[gpar][:, 0:1], 0),
                    element_offset=half * 8192,
                    bounds_check=A["reg_bchk"], oob_is_err=False).then_inc(s_m8[gpar], 16)
                gp.indirect_dma_start(
                    out=bass.AP(w2c[half][gpar].tensor, 0, [[8192, 128], [1, 8192]]),
                    out_offset=None,
                    in_=A["d_w2p"], in_offset=bass.IndirectOffsetOnAxis(wii[gpar][:, 0:1], 0),
                    element_offset=half * 8192,
                    bounds_check=A["reg_bchk"], oob_is_err=False).then_inc(s_m8[gpar], 16)
            gp.wait(s_pe, P.g("p4_xgg_pe"))
            gp.indirect_dma_start(
                out=A["b1b"][:], out_offset=None,
                in_=A["d_b1b"], in_offset=bass.IndirectOffsetOnAxis(wii[gpar][:, 1:2], 0),
                bounds_check=A["reg_bchk"], oob_is_err=False).then_inc(s_m8[gpar], 16)
            P.s("p4_m8_%d" % gpar, s_m8[gpar].n)
        if st == 1:
            gp.wait(s_m8[gpar], P.g("p4_m8_%d" % gpar))
            gp.wait(s_pe, P.g("p4_w2pe_%d" % spar))
            gp.indirect_dma_start(
                out=A["b2b"][:], out_offset=None,
                in_=A["d_b2b"], in_offset=bass.IndirectOffsetOnAxis(wii[gpar][:, 1:2], 0),
                bounds_check=A["reg_bchk"], oob_is_err=False).then_inc(s_m8[gpar], 16)
            P.s("p4_m8b_%d" % gpar, s_m8[gpar].n)
        te.wait(s_m8[gpar], P.g("p4_m8_%d" % gpar))
        for bk in range(4, 8):
            P.bank_w(te, bk)
        for k in range(16):
            if k % 4 == 0:
                te.wait(s_xq[k // 4], P.g("p4_xgq_%d" % (k // 4)))
            for ht in range(8):
                half, hq = ht // 4, ht % 4
                bk = 4 + ht // 2
                c0 = (ht % 2) * 256
                te.matmul(B[bk][:, c0:c0 + 128],
                          w1c[half][gpar][:, hq * 2048 + k * 128: hq * 2048 + (k + 1) * 128],
                          xv3[:, k, st * 128:(st + 1) * 128],
                          start=(k == 0 and ht % 2 == 0), stop=False)
        for ht in range(8):
            bk = 4 + ht // 2
            c0 = (ht % 2) * 256
            mi = te.matmul(B[bk][:, c0:c0 + 128],
                           A["b1b"][0:1, ht * 128:(ht + 1) * 128],
                           A["ones1bf"][:], start=False, stop=(ht % 2 == 1))
        mi.then_inc(s_pe, 1)
        h_all = s_pe.n
        for bk in range(4, 8):
            P.bank_set_w(bk, s_pe)
        if st == 3:
            P.s("p4_xgg_pe", s_pe.n)
        sc.wait(s_pe, P.g("p4_w2pe_%d" % spar))
        sc.wait(s_pe, h_all)
        for ht in range(8):
            bk = 4 + ht // 2
            c0 = (ht % 2) * 256
            P.bank_r(sc, bk)
            sc.activation(h_fm[spar][:, ht * 128:(ht + 1) * 128], B[bk][:, c0:c0 + 128],
                          AF.Gelu).then_inc(s_act, 1)
            P.bank_add_r(bk, s_act)
        P.s("p4_gl_%d" % spar, s_act.n)
      sj = si - 1
      if 0 <= sj < NST:
        sparj = sj % 2
        grpj, stj = sj // 4, sj % 4
        gparj = grpj % 2
        te.wait(s_act, P.g("p4_gl_%d" % sparj))
        te.wait(s_m8[gparj], P.g("p4_m8b_%d" % gparj))
        for dmg in range(4):
            P.bank_w(te, dmg)
        for ht in range(8):
            half, hq = ht // 4, ht % 4
            for dmg in range(4):
                te.matmul(B[dmg][:], h_fm[sparj][:, ht * 128:(ht + 1) * 128],
                          w2c[half][gparj][:, hq * 2048 + dmg * 512: hq * 2048 + (dmg + 1) * 512],
                          start=(ht == 0), stop=False)
        for dmg in range(4):
            mi = te.matmul(B[dmg][:], A["ones1bf"][:],
                           A["b2b"][0:1, dmg * 512:(dmg + 1) * 512], start=False, stop=True)
            mi.then_inc(s_pe, 1)
            P.s("p4_o_done_%d" % dmg, s_pe.n)
            P.bank_set_w(dmg, s_pe)
        P.s("p4_w2pe_%d" % sparj, s_pe.n)
        if stj == 3:
            P.s("p4_wc_pe_%d" % gparj, s_pe.n)
        ve.wait(s_dve, P.g("p4_cv"))
        ve.wait(s_os[sparj], P.g("p4_ost_%d" % sparj))
        for dmg in range(4):
            ve.wait(s_pe, P.g("p4_o_done_%d" % dmg))
            P.bank_r(ve, dmg)
            ve.tensor_scalar_mul(osb[sparj][:, dmg * 512:(dmg + 1) * 512], B[dmg][:],
                                 A["gvall"][:, sj:sj + 1]).then_inc(s_dve, 1)
            P.bank_add_r(dmg, s_dve)
        o_rdy = s_dve.n
        sy.wait(s_dve, o_rdy)
        sy.dma_start(out=A["d_rout2"][sj * 128:(sj + 1) * 128, :],
                     in_=osb[sparj][:]).then_inc(s_os[sparj], 16)
        P.s("p4_ost_%d" % sparj, s_os[sparj].n)
    p4_done = [s_os[0].n, s_os[1].n]
    p4_pe = s_pe.n

    if PHASE_LIMIT < 6:
        sy.wait(s_os[0], p4_done[0])
        sy.wait(s_os[1], p4_done[1])
        _dump_taps(P, A, S)
        return

    w2k = [A["w2k0"], A["w2k1"]]
    ghk = [A["ghk0"], A["ghk1"]]
    w2_free = [0, 0]
    gh2_free = [0, 0]
    spghe3 = A["sp_ghe"].rearrange("(r p) t -> p r t", p=128)
    accv = A["out_acc"][:].rearrange("p (tt d) -> p tt d", tt=8)
    rgb = [A["r1sb"], A["r2sb"]]
    for th in range(2):
        sy.wait(s_dve, P.g("p5_acc_free"))
        sy.wait(s_os[0], p4_done[0])
        sy.wait(s_os[1], p4_done[1])
        for tt in range(8):
            row = th * 1024 + tt * 128
            sy.dma_start(out=accv[:, tt, :], in_=A["sp_x1f"][row:row + 128, :]).then_inc(s_chl, 16)
        acc_init = s_chl.n
        for kb in range(NKB):
            par = kb % 2
            sy.wait(s_pe, w2_free[par] if (th > 0 or kb >= 2) else p4_pe)
            for k in range(KB):
                row = kb * 1024 + k * 128
                sy.dma_start(out=w2k[par][:, k * 2048:(k + 1) * 2048],
                             in_=A["d_w2shb"][row:row + 128, :]).then_inc(s_wh[par], 16)
            w2_rdy = s_wh[par].n
            sy.wait(s_gst[0], shared_done[0])
            sy.wait(s_gst[1], shared_done[1])
            sy.wait(s_pe, gh2_free[par])
            sy.dma_start(out=ghk[par][:].rearrange("p (k t) -> p k t", k=KB),
                         in_=spghe3[:, kb * KB:(kb + 1) * KB, th * 1024:(th + 1) * 1024]).then_inc(s_gh[par], 16)
            gh_rdy = s_gh[par].n
            ghv = ghk[par][:].rearrange("p (k t) -> p k t", k=KB)
            for tt in range(8):
                for dmg in range(4):
                    bk = (tt * 4 + dmg) % 2
                    te.wait(s_wh[par], w2_rdy)
                    te.wait(s_gh[par], gh_rdy)
                    P.bank_w(te, bk)
                    n_extra = 1 if kb == 0 else 0
                    for k in range(KB):
                        mi = te.matmul(B[bk][:], ghv[:, k, tt * 128:(tt + 1) * 128],
                                       w2k[par][:, k * 2048 + dmg * 512: k * 2048 + (dmg + 1) * 512],
                                       start=(k == 0), stop=(k == KB - 1 and n_extra == 0))
                    if kb == 0:
                        te.wait(s_w, s_w.n)
                        mi = te.matmul(B[bk][:], A["ones1bf"][:],
                                       A["sb2bf"][:, dmg * 512:(dmg + 1) * 512], start=False, stop=True)
                    mi.then_inc(s_pe, 1)
                    mm_fin = s_pe.n
                    P.bank_set_w(bk, s_pe)
                    ve.wait(s_pe, mm_fin)
                    ve.wait(s_chl, acc_init)
                    P.bank_r(ve, bk)
                    ve.scalar_tensor_tensor(accv[:, tt, dmg * 512:(dmg + 1) * 512],
                                            B[bk][:], 1.0,
                                            accv[:, tt, dmg * 512:(dmg + 1) * 512],
                                            ALU.mult, ALU.add).then_inc(s_dve, 1)
                    P.bank_add_r(bk, s_dve)
            w2_free[par] = s_pe.n
            gh2_free[par] = s_pe.n
        shared_acc = s_dve.n
        for tt in range(8):
            gtt = th * 8 + tt
            gp.wait(s_os[0], p4_done[0])
            gp.wait(s_os[1], p4_done[1])
            gp.wait(s_dve, P.g("p5_rg_ve"))
            if th == 0 and tt == 0:
                gp.reg_mov(A["reg_bchk"], SLOTS - 1)
            for j in range(2):
                idx = A["s1all"] if j == 0 else A["s2all"]
                gp.indirect_dma_start(
                    out=rgb[j][:], out_offset=None,
                    in_=A["d_rout2"], in_offset=bass.IndirectOffsetOnAxis(idx[:, gtt:gtt + 1], 0),
                    bounds_check=A["reg_bchk"], oob_is_err=False).then_inc(s_ig, 16)
            rg_rdy = s_ig.n
            ve.wait(s_ig, rg_rdy)
            ve.wait(s_dve, shared_acc)
            for j in range(2):
                ve.tensor_add(accv[:, tt, :], accv[:, tt, :], rgb[j][:]).then_inc(s_dve, 1)
            P.s("p5_rg_ve", s_dve.n)
        acc_done = s_dve.n
        ve.wait(s_dve, acc_done)
        for tt in range(8):
            ve.reduce_sum(A["ln_stats"][:, tt:tt + 1], accv[:, tt, :],
                          axis=AX.X).then_inc(s_dve, 1)
        sum_rdy = s_dve.n
        sc.wait(s_dve, sum_rdy)
        sc.wait(s_st, s_st.n)
        for tt in range(8):
            sc.activation(A["lnscr"][:], accv[:, tt, :], AF.Square,
                          accum_out=A["ln_stats"][:, 8 + tt:9 + tt]).then_inc(s_act, 1)
        sq2 = s_act.n
        ve.wait(s_act, sq2)
        ve.tensor_scalar_mul(A["ln_stats"][:, 0:8], A["ln_stats"][:, 0:8], 1.0 / DM).then_inc(s_dve, 1)
        ve.tensor_scalar_mul(A["ln_stats"][:, 8:16], A["ln_stats"][:, 8:16], 1.0 / DM).then_inc(s_dve, 1)
        ve.tensor_mul(A["lnscr"][:, 0:8], A["ln_stats"][:, 0:8], A["ln_stats"][:, 0:8]).then_inc(s_dve, 1)
        ve.tensor_sub(A["ln_stats"][:, 8:16], A["ln_stats"][:, 8:16], A["lnscr"][:, 0:8]).then_inc(s_dve, 1)
        ve.tensor_scalar_add(A["ln_stats"][:, 8:16], A["ln_stats"][:, 8:16], EPS).then_inc(s_dve, 1)
        v2 = s_dve.n
        sc.wait(s_dve, v2)
        sc.activation(A["ln_stats"][:, 8:16], A["ln_stats"][:, 8:16], AF.Ln).then_inc(s_act, 1)
        l2 = s_act.n
        ve.wait(s_act, l2)
        ve.tensor_scalar_mul(A["ln_stats"][:, 8:16], A["ln_stats"][:, 8:16], -0.5).then_inc(s_dve, 1)
        n2 = s_dve.n
        sc.wait(s_dve, n2)
        sc.activation(A["ln_stats"][:, 8:16], A["ln_stats"][:, 8:16], AF.Exp).then_inc(s_act, 1)
        rstd_rdy2 = s_act.n
        ve.wait(s_act, rstd_rdy2)
        st_prev = s_st.n
        for tt in range(8):
            row = th * 1024 + tt * 128
            ve.wait(s_st, st_prev)
            ve.tensor_scalar(A["lnscr"][:], accv[:, tt, :], A["ln_stats"][:, tt:tt + 1],
                             A["ln_stats"][:, 8 + tt:9 + tt],
                             ALU.subtract, ALU.mult).then_inc(s_dve, 1)
            fin = s_dve.n
            sy.wait(s_dve, fin)
            sy.dma_start(out=A["out"][row:row + 128, :], in_=A["lnscr"][:]).then_inc(s_st, 16)
            st_prev = s_st.n
        P.s("p5_acc_free", s_dve.n)
    _dump_taps(P, A, S)
    sy.wait(s_st, s_st.n)


def build_nc():
    nc = bass.Bass("TRN2", target_bir_lowering=False, debug=False)
    P = Prog(nc)
    A = {}

    def din(name, shape, dtype=F32):
        A["in_" + name] = nc.dram_tensor("in_" + name, list(shape), dtype,
                                         kind="ExternalInput").ap()

    din("x_fm", (DM, T + HALO))
    din("x_tm", (T, DM))
    din("w_in", (NFT * 128, 2048))
    din("conv_w", (128, 128))
    din("conv_b", (128, 32))
    din("a_dt", (32, 2))
    din("d_param", (128, 32))
    din("w_out", (DI, DM))
    din("gate_w", (DM, 8))
    din("w1", (8192, 2048))
    din("b1", (128, 64))
    din("sw1", (2048, 2048))
    din("sb1", (128, 16))
    din("w2cat", (10240, 2048))
    din("b2", (8, 2048))
    din("sb2", (1, 2048))
    din("ident", (128, 128))
    din("flag", (128, 1))
    din("lt2", (128, 512))
    din("sel32", (32, 4096))
    din("seg_mask", (32, 2048))
    din("iota_t", (128, 16))
    din("iota_w", (128, 8))
    din("srow16", (8, 16))
    din("l8u", (8, 8))
    din("gidx_init", (SLOTS, 2))
    din("b1r", (16, 1024))
    din("b2r", (16, 2048))
    A["out"] = nc.dram_tensor("out", [T, DM], F32, kind="ExternalOutput").ap()

    def internal(name, shape, dtype):
        A[name] = nc.dram_tensor(name, list(shape), dtype, kind="Internal").ap()
        if name in DEBUG_TAPS:
            A["dbg_" + name] = nc.dram_tensor("dbg_" + name, list(shape), F32,
                                              kind="ExternalOutput").ap()

    internal("sp_val", (DI, T), BF16)
    internal("sp_gate", (DI, T), BF16)
    internal("sp_B", (DI, T), BF16)
    internal("sp_C", (DI, T), BF16)
    internal("sp_Yd", (NCH * 128, 8192), BF16)
    internal("sp_Cexp", (NCH * 128, 8192), BF16)
    internal("sp_Yg", (DI, T), BF16)
    internal("sp_x1f", (T, DM), F32)
    internal("sp_x1bf", (T, DM), BF16)
    internal("sp_ghe", (2048, T), BF16)
    internal("d_w1p", (1024, 16384), BF16)
    internal("d_w2p", (1024, 16384), BF16)
    internal("d_b1b", (16, 1024), BF16)
    internal("d_b2b", (16, 2048), BF16)
    internal("d_sw1b", (2048, 2048), BF16)
    internal("d_w2shb", (2048, 2048), BF16)
    internal("d_woutb", (4096, 2048), BF16)
    internal("d_gidx", (SLOTS, 2), F32)
    internal("d_xg", (SLOTS, DM), BF16)
    internal("d_xgT", (DM, SLOTS), BF16)
    internal("d_rout2", (SLOTS, DM), BF16)
    internal("ag_in", (4096, 128), F32)
    internal("ag_out", (8192, 128), F32)

    sdefs = [
        ("ident", [128, 128], F32), ("flag", [128, 1], F32), ("conv_w", [128, 128], F32),
        ("conv_b", [128, 32], F32), ("d_param", [128, 32], F32), ("a_dt", [32, 2], F32),
        ("negA", [32, 1], F32), ("b1", [128, 64], F32), ("sb1", [128, 16], F32),
        ("b2bf", [8, 2048], BF16), ("sb2bf", [1, 2048], BF16), ("ones1bf", [1, 128], BF16),
        ("ones32", [32, 128], F32), ("gwbf", [128, 128], BF16), ("g_rows", [8, 2048], F32),
        ("ln_stats", [128, 16], F32), ("s1all", [128, 16], I32T), ("s2all", [128, 16], I32T),
        ("sel32", [32, 4096], F32), ("seg_mask", [32, 2048], F32),
        ("dt_sb", [32, 2048], F32), ("h_f32", [128, 4096], F32), ("h_bf16", [128, 4096], BF16),
        ("lasts", [32, 8], F32), ("ones8", [32, 8], F32), ("decays", [32, 8], F32),
        ("cum_tm", [128, 64], F32), ("neg_cum", [128, 64], F32), ("w_mh", [128, 64], F32),
        ("lastsbc", [128, 32], F32), ("el_bc", [128, 32], F32), ("D_bc", [128, 32], F32),
        ("cum_fm", [32, 2048], F32), ("dtscr", [32, 2048], F32),
        ("xfm", [128, 32816], BF16), ("wt0", [128, 2048], BF16), ("wt1", [128, 2048], BF16),
        ("valraw0", [128, 2056], F32), ("convacc", [128, 2048], F32),
        ("stg0", [128, 2048], BF16), ("stg1", [128, 2048], BF16), ("stg2", [128, 2048], BF16),
        ("val_fm", [128, 8192], BF16), ("B_fm", [128, 8192], BF16), ("C_fm", [128, 8192], BF16),
        ("val_tm0", [128, 4096], BF16), ("val_tm1", [128, 4096], BF16),
        ("B_tm0", [128, 4096], BF16), ("B_tm1", [128, 4096], BF16),
        ("Bw", [128, 512], BF16), ("e0", [128, 1024], F32),
        ("Mt0", [128, 1024], BF16), ("expcum", [128, 512], BF16),
        ("lt2", [128, 512], F32),
        ("yd_st0", [128, 2048], BF16), ("ce_st0", [128, 2048], BF16),
        ("clbf", [128, 512], BF16), ("ctmp", [128, 1024], F32),
        ("wot", [128, 65536], BF16), ("yg_t", [128, 4096], BF16), ("yg_t2", [128, 4096], BF16), ("x1stage", [128, 2048], F32),
        ("xtm_b", [128, 2048], F32), ("sq_b", [128, 2048], F32),
        ("x1fm", [128, 32768], BF16), ("w1t0", [128, 2048], BF16), ("w1t1", [128, 2048], BF16),
        ("ghe_st0", [128, 2048], BF16), ("ghe_st1", [128, 2048], BF16),
        ("gel0", [128, 512], F32), ("gel1", [128, 512], F32),
        ("gtmp", [128, 16], F32), ("gtmp2", [128, 16], F32),
        ("ind8", [8, 2048], F32), ("pos8", [8, 2048], F32),
        ("slot_tm0", [128, 8], F32), ("slot_tm1", [128, 8], F32),
        ("gate_tm0", [128, 8], F32), ("gate_tm1", [128, 8], F32),
        ("islot", [128, 8], I32T),
        ("stage2a", [128, 2], F32), ("stage2b", [128, 2], F32),
        ("stage2c", [128, 2], F32), ("stage2d", [128, 2], F32),
        ("gvall", [128, 60], F32),
        ("gixall", [128, 120], F32), ("giiall", [128, 120], I32T),
        ("wif", [128, 8], F32), ("wii0", [128, 8], I32T), ("wii1", [128, 8], I32T),
        ("wb_bc", [128, 16], F32), ("cmp16", [8, 16], F32), ("rts", [8, 8], F32),
        ("egrow", [1, 16], F32), ("wbrow", [1, 16], F32),
        ("ones1f", [1, 128], F32),
        ("w1cA0", [128, 8192], BF16), ("w1cA1", [128, 8192], BF16),
        ("w1cB0", [128, 8192], BF16), ("w1cB1", [128, 8192], BF16),
        ("w2cA0", [128, 8192], BF16), ("w2cA1", [128, 8192], BF16),
        ("w2cB0", [128, 8192], BF16), ("w2cB1", [128, 8192], BF16),
        ("xgg", [128, 8192], BF16),
        ("ones8r", [8, 2048], F32),
        ("xts0a", [128, 768], BF16), ("xts0b", [128, 768], BF16),
        ("xts0c", [128, 768], BF16), ("xts0d", [128, 768], BF16),
        ("r1sb", [128, 2048], BF16), ("r2sb", [128, 2048], BF16),
        ("h_fm0", [128, 1024], BF16), ("h_fm1", [128, 1024], BF16),
        ("osb0", [128, 2048], BF16), ("osb1", [128, 2048], BF16),
        ("xg_sb0", [128, 2048], BF16), ("xg_sb1", [128, 2048], BF16),
        ("xg_sb2", [128, 2048], BF16), ("xg_sb3", [128, 2048], BF16),
        ("xg_sb4", [128, 2048], BF16), ("xg_sb5", [128, 2048], BF16),
        ("xg_sb6", [128, 2048], BF16), ("xg_sb7", [128, 2048], BF16),
        ("b2b", [128, 2048], BF16), ("b1b", [128, 1024], BF16),
        ("l8u", [8, 8], F32), ("srow16", [8, 16], F32),
        ("iota_t", [128, 16], F32), ("iota_w", [128, 8], F32),
        ("w2k0", [128, 16384], BF16), ("w2k1", [128, 16384], BF16),
        ("ghk0", [128, 8192], BF16), ("ghk1", [128, 8192], BF16),
        ("out_acc", [128, 16384], F32), ("lnscr", [128, 2048], F32),
    ]
    for nm, shp, dt in sdefs:
        A[nm] = nc.alloc_sbuf_tensor_at("sb_" + nm, shp, dt, offset=_MAP[nm]).ap()

    A["reg_exp"] = nc.gpsimd.alloc_register("r_exp")
    A["reg_off"] = nc.gpsimd.alloc_register("r_off")
    A["reg_bchk"] = nc.gpsimd.alloc_register("r_bchk")
    A["reg_spares"] = [nc.gpsimd.alloc_register("r_spare%d" % i) for i in range(16)]
    with ExitStack() as stack:
        e = stack.enter_context
        S = {nm: Sem(e(nc.semaphore("s_" + nm)), nm) for nm in
             ["w", "ld", "sp", "st", "pe", "act", "dve", "pool", "cc",
              "wta", "wtb", "ch", "gha", "ghb", "spg",
              "b8a", "b8b", "e8a", "e8b", "m8a", "m8b", "ig", "gx",
              "gt0", "gt1", "gt2", "gt3", "gt4", "gt5", "gt6", "gt7", "gt8", "gt9", "gst", "gse", "xts", "xto", "wh0", "wh1", "swc", "gxo", "gxs0", "gxs1", "xss0", "xss1", "xq1", "xq2", "xq3", "os0", "os1", "g4", "g5", "g6", "g7"]}
        Bh = [e(nc.psum_tensor("pb%d" % i, [128, 512], F32)) for i in range(8)]
        B = [b[:] for b in Bh]
        P.sems = S
        block = e(nc.Block())
        for en in ["sync", "tensor", "scalar", "vector", "gpsimd"]:
            P.reset()
            P.active = en

            def run(eng, en=en):
                P.real[en] = eng
                emit(P, A, S, B)

            getattr(block, en)(run)
    return nc


def pack_inputs(inputs):
    f32 = np.float32
    x = np.asarray(inputs["L0_attn_x"], f32)
    W_in = np.asarray(inputs["L0_attn_W_in"], f32)
    conv_w = np.asarray(inputs["L0_attn_conv_w"], f32)
    conv_b = np.asarray(inputs["L0_attn_conv_b"], f32)
    A_log = np.asarray(inputs["L0_attn_A_log"], f32)
    D_param = np.asarray(inputs["L0_attn_D_param"], f32)
    dt_bias = np.asarray(inputs["L0_attn_dt_bias"], f32)
    W_out = np.asarray(inputs["L0_attn_W_out"], f32)
    gate_w = np.asarray(inputs["L0_ffn_gate_w"], f32)
    w1 = np.asarray(inputs["L0_ffn_w1"], f32)
    b1 = np.asarray(inputs["L0_ffn_b1"], f32)
    w2 = np.asarray(inputs["L0_ffn_w2"], f32)
    b2 = np.asarray(inputs["L0_ffn_b2"], f32)
    sw1 = np.asarray(inputs["L0_ffn_shared_w1"], f32)
    sb1 = np.asarray(inputs["L0_ffn_shared_b1"], f32)
    sw2 = np.asarray(inputs["L0_ffn_shared_w2"], f32)
    sb2 = np.asarray(inputs["L0_ffn_shared_b2"], f32)

    W_pad = np.zeros((NFT * 128, 2048), f32)
    W_pad[:16416] = W_in
    w_in_p = np.ascontiguousarray(
        W_pad.reshape(NFT, 128, 16, 128).transpose(0, 3, 2, 1).reshape(NFT * 128, 2048))
    conv_w_p = np.ascontiguousarray(conv_w.reshape(32, 128, 4).transpose(1, 0, 2).reshape(128, 128))
    conv_b_p = np.ascontiguousarray(conv_b.reshape(32, 128).T)
    d_p = np.ascontiguousarray(D_param.reshape(32, 128).T)
    a_dt = np.stack([A_log, dt_bias], axis=1).astype(f32)
    w_out_p = np.ascontiguousarray(W_out.T)
    w1_p = np.ascontiguousarray(
        w1.reshape(8, 16, 128, 8, 128).transpose(0, 3, 2, 1, 4).reshape(8192, 2048))
    b1_p = np.ascontiguousarray(b1.reshape(8, 8, 128).transpose(2, 0, 1).reshape(128, 64))
    sw1_p = np.ascontiguousarray(
        sw1.reshape(16, 128, 16, 128).transpose(2, 1, 0, 3).reshape(2048, 2048))
    sb1_p = np.ascontiguousarray(sb1.reshape(16, 128).T)
    w2cat = np.ascontiguousarray(np.concatenate([w2.reshape(8192, 2048), sw2], axis=0))
    sb2_p = np.ascontiguousarray(sb2.reshape(1, 2048))
    ident = np.eye(128, dtype=f32)
    lt2 = np.zeros((128, 512), f32)
    ll = np.arange(256)[None, :]
    pp = np.arange(128)[:, None]
    lt2[:, 0:256] = np.where(ll >= pp, 0.0, -60000.0)
    lt2[:, 256:512] = np.where(ll >= 128 + pp, 0.0, -60000.0)
    sel32 = (np.eye(32, dtype=f32)[:, :, None] * np.ones(128, f32)).reshape(32, 4096)
    iota_t = (np.arange(16)[None, :] * 128 + np.arange(128)[:, None]).astype(f32)
    iota_w = (np.arange(8)[None, :] * 128 + np.arange(128)[:, None]).astype(f32)
    srow16 = np.broadcast_to((np.arange(16) * 512).astype(f32), (8, 16)).copy()
    l8u = np.triu(np.ones((8, 8), f32), 1)
    gidx_init = np.zeros((SLOTS, 2), f32)
    seg = np.ones((32, 2048), f32)
    seg[:, ::CS] = 0.0

    common = {
        "in_w_in": w_in_p, "in_conv_w": conv_w_p, "in_conv_b": conv_b_p,
        "in_a_dt": a_dt, "in_d_param": d_p, "in_w_out": w_out_p,
        "in_gate_w": gate_w, "in_w1": w1_p, "in_b1": b1_p, "in_sw1": sw1_p,
        "in_sb1": sb1_p, "in_w2cat": w2cat, "in_b2": b2, "in_sb2": sb2_p,
        "in_ident": ident, "in_lt2": lt2, "in_sel32": sel32, "in_seg_mask": seg,
        "in_iota_t": iota_t, "in_iota_w": iota_w, "in_srow16": srow16, "in_l8u": l8u,
        "in_gidx_init": gidx_init,
        "in_b1r": np.concatenate([b1.reshape(8, 1024), np.zeros((8, 1024), f32)], axis=0),
        "in_b2r": np.concatenate([b2, np.zeros((8, 2048), f32)], axis=0),
    }
    maps = []
    for core in range(8):
        b_i = core // 2
        half = core % 2
        t0 = half * T
        xs = x[b_i, t0:t0 + T]
        if half == 0:
            xh = np.concatenate([np.zeros((HALO, DM), f32), xs], axis=0)
        else:
            xh = x[b_i, t0 - HALO:t0 + T]
        m = dict(common)
        m["in_x_fm"] = np.ascontiguousarray(xh.T)
        m["in_x_tm"] = np.ascontiguousarray(xs)
        m["in_flag"] = np.full((128, 1), float(half), f32)
        maps.append(m)
    return maps


def kernel(**inputs):
    maps = pack_inputs(inputs)
    nc = build_nc()
    res = run_bass_kernel_spmd(nc, maps, core_ids=list(range(8)))
    out = np.zeros((4, 4096, 2048), np.float32)
    for core in range(8):
        b_i = core // 2
        half = core % 2
        out[b_i, half * T:(half + 1) * T] = np.asarray(res.results[core]["out"], np.float32)
    if DEBUG_TAPS:
        kernel.debug = res.results
    return out


# version 26
# speedup vs baseline: 1.0285x; 1.0285x over previous
import numpy as np
import ml_dtypes
from contextlib import ExitStack

import concourse.bass as bass
import concourse.mybir as mybir
from concourse.bass_utils import run_bass_kernel_spmd

F32 = mybir.dt.float32
BF16 = mybir.dt.bfloat16
I32T = mybir.dt.int32
AF = mybir.ActivationFunctionType
ALU = mybir.AluOpType
AX = mybir.AxisListType

DM = 2048
T = 2048
HALO = 3
NFT = 129
KD = 16
CS = 256
NCH = 8
H = 32
DI = 4096
E = 8
NKR = 80
KB = 8
NKB = 2
SLOTS = 7680
NST = 60
NGRP = 15
NSPL = 10
TPS = 6
SSL = 768
EPS = 1e-5

DEBUG_TAPS = set()
PHASE_LIMIT = 99
FT_LIMIT = NFT

SB_BASE = 16512
SB_LIMIT = 229344 - SB_BASE


def _build_map():
    m = {}

    def seq(start, items):
        o = start
        for name, byts in items:
            o = (o + 31) & ~31
            m[name] = o
            o += byts
        return o

    end0 = seq(0, [
        ("ident", 512), ("flag", 32), ("conv_w", 512), ("conv_b", 128),
        ("d_param", 128), ("a_dt", 32), ("negA", 32), ("b1", 256), ("sb1", 64),
        ("b2bf", 4096), ("sb2bf", 4096), ("ones1bf", 256), ("ones2bf", 512), ("ones32", 512),
        ("gwbf", 256), ("g_rows", 8192), ("ln_stats", 64),
        ("lt2", 2048), ("s1all", 64), ("s2all", 64),
    ])
    assert end0 <= 25600, end0
    end1 = seq(25632, [
        ("sel32", 16384), ("sel8", 4096), ("seg_mask", 8192), ("dt_sb", 8192),
        ("h_f32", 16384), ("h_bf16", 8192),
        ("lasts", 64), ("ones8", 32), ("decays", 32),
        ("cum_tm", 256), ("neg_cum", 256), ("w_mh", 256),
        ("lastsbc", 128), ("el_bc", 128), ("D_bc", 128),
        ("cum_fm", 8192), ("dtscr", 8192),
    ])
    P2S = (end1 + 31) & ~31
    endp1 = seq(P2S, [
        ("xfm", 65632), ("wt0", 4096), ("wt1", 4096),
        ("valraw0", 8224), ("convacc", 8192),
        ("stg0", 4096), ("stg1", 4096), ("stg2", 4096),
    ])
    assert endp1 <= SB_LIMIT, endp1
    endp2 = seq(P2S, [
        ("val_fm", 16384), ("B_fm", 16384), ("C_fm", 16384),
        ("val_tm0", 8192), ("val_tm1", 8192), ("B_tm0", 8192), ("B_tm1", 8192),
        ("Bw", 1024), ("e0", 4096), ("Mt0", 2048), ("expcum", 1024),
        ("yd_st0", 4096), ("ce_st0", 4096),
    ])
    assert endp2 <= SB_LIMIT, endp2
    m["clbf"] = m["sel32"]
    m["ctmp"] = m["sel32"] + 1024
    m["wot"] = 49152
    assert 49152 + 131072 <= 180256
    m["yg_t"] = 180256
    m["yg_t2"] = 188448
    m["xtm_b"] = 196640
    m["sq_b"] = 25632
    m["x1stage"] = 33824
    m["x1fm"] = 49152
    m["w1t0"] = 114688
    m["w1t1"] = 118784
    m["ghe_st0"] = 155648
    m["ghe_st1"] = 159744
    m["gel0"] = 163840
    m["gel1"] = 165888
    m["gtmp"] = 167936
    m["gtmp2"] = 168000
    m["ind8"] = 25632
    m["pos8"] = 33824
    m["slot_tm0"] = 198656
    m["slot_tm1"] = 198720
    m["gate_tm0"] = 198784
    m["gate_tm1"] = 198848
    m["islot"] = 198912
    m["stage2a"] = 198976
    m["stage2b"] = 199008
    m["stage2c"] = 201344
    m["stage2d"] = 201376
    m["gvall"] = 199040
    m["gixall"] = 199296
    m["giiall"] = 199776
    m["wif"] = 200256
    m["wii0"] = 200288
    m["wii1"] = 200320
    m["wb_bc"] = 200352
    m["cmp16"] = 200416
    m["rts"] = 200480
    m["egrow"] = 200512
    m["wbrow"] = 200576
    m["ones1f"] = 200640
    m["l8u"] = 201152
    m["srow16"] = 201184
    m["iota_t"] = 201248
    m["iota_w"] = 201312
    m["w1cA0"] = 49152
    m["w1cA1"] = 65536
    m["w1cB0"] = 81920
    m["w1cB1"] = 98304
    m["w2cA0"] = 114688
    m["w2cA1"] = 131072
    m["w2cB0"] = 147456
    m["w2cB1"] = 163840
    m["xgg"] = 180224
    m["ones8r"] = 180224
    m["r1sb"] = 33824
    m["r2sb"] = 204800
    m["h_fm0"] = 204832
    m["h_fm1"] = 206880
    m["osb0"] = 33824
    m["osb1"] = 37920
    m["xg_sb0"] = 25632
    m["xg_sb1"] = 29728
    m["xg_sb2"] = 33824
    m["xg_sb3"] = 37920
    m["xg_sb4"] = 180224
    m["xg_sb5"] = 184320
    m["xg_sb6"] = 188416
    m["xg_sb7"] = 192512
    m["b2b"] = 42016
    m["b1b"] = 46112
    m["w2k0"] = 40960
    m["w2k1"] = 73728
    m["ghk0"] = 106496
    m["ghk1"] = 122880
    m["out_acc"] = 139264
    m["lnscr"] = 25632
    m["xts0a"] = m["g_rows"]
    m["xts0b"] = m["g_rows"] + 1536
    m["xts0c"] = m["g_rows"] + 3072
    m["xts0d"] = m["g_rows"] + 4608
    for k in m:
        m[k] += SB_BASE
    return m


_MAP = _build_map()


class Sem:
    __slots__ = ("h", "n", "name")

    def __init__(self, h, name):
        self.h = h
        self.n = 0
        self.name = name


class _I:
    __slots__ = ("r", "auto")

    def __init__(self, r, auto=None):
        self.r = r
        self.auto = auto

    def then_inc(self, sem, k=1):
        if sem is self.auto:
            return self
        sem.n += k
        if self.r is not None:
            self.r.then_inc(sem.h, k)
        return self


_NO_CHAIN = {"wait_ge", "dma_start", "dma_start_transpose", "collective_compute",
             "indirect_dma_start", "reg_load", "reg_mul", "reg_add", "reg_mov",
             "reg_sub", "nop", "sem_inc", "drain"}


class Eng:
    def __init__(self, prog, name, auto_sem_name=None):
        self._P = prog
        self._name = name
        self._auto_name = auto_sem_name

    def __getattr__(self, attr):
        P = self._P
        name = self._name
        auto_name = self._auto_name if attr not in _NO_CHAIN else None

        def call(*a, **kw):
            auto = P.sems[auto_name] if auto_name else None
            if P.active == name:
                real_eng = P.real[name]
                if auto is not None and auto.n > 0:
                    real_eng.wait_ge(auto.h, auto.n)
                ri = getattr(real_eng, attr)(*a, **kw)
                if auto is not None:
                    ri.then_inc(auto.h, 1)
                    auto.n += 1
                return _I(ri, auto)
            if auto is not None:
                auto.n += 1
            return _I(None, auto)

        return call

    def wait(self, sem, val=None):
        if val is None:
            val = sem.n
        if val <= 0:
            return
        if self._P.active == self._name:
            self._P.real[self._name].wait_ge(sem.h, val)


class Prog:
    def __init__(self, nc):
        self.nc = nc
        self.active = None
        self.real = {}
        self.sy = Eng(self, "sync")
        self.te = Eng(self, "tensor")
        self.sc = Eng(self, "scalar", "act")
        self.ve = Eng(self, "vector", "dve")
        self.gp = Eng(self, "gpsimd", "pool")

    def reset(self):
        for s in self.sems.values():
            s.n = 0
        self.bank = [{"wr": None, "rd": {}} for _ in range(8)]
        self.snap = {}

    def g(self, key):
        return self.snap.get(key, 0)

    def s(self, key, val):
        self.snap[key] = val

    def bank_w(self, eng, i):
        for s, v in self.bank[i]["rd"].items():
            eng.wait(s, v)

    def bank_r(self, eng, i):
        wr = self.bank[i]["wr"]
        if wr is not None:
            eng.wait(wr[0], wr[1])

    def bank_set_w(self, i, sem):
        self.bank[i]["wr"] = (sem, sem.n)
        self.bank[i]["rd"] = {}

    def bank_add_r(self, i, sem):
        self.bank[i]["rd"][sem] = sem.n


def _dump_taps(P, A, S):
    if not DEBUG_TAPS:
        return
    gp, sy = P.gp, P.sy
    for nm in ["w", "ld", "sp", "spg", "wta", "wtb", "ch", "gha", "ghb", "ig", "gx", "pe", "act", "dve", "pool"]:
        gp.wait(S[nm], S[nm].n)
    for name in sorted(DEBUG_TAPS):
        src = A[name]
        rows = src.shape[0]
        step = max(rows // 8, 1)
        for r0 in range(0, rows, step):
            r1 = min(r0 + step, rows)
            gp.dma_start(out=A["dbg_" + name][r0:r1, :], in_=src[r0:r1, :]).then_inc(S["st"], 16)
    sy.wait(S["st"], S["st"].n)


def emit(P, A, S, B):
    sy, te, sc, ve, gp = P.sy, P.te, P.sc, P.ve, P.gp
    if P.active == "gpsimd":
        for _r in A["reg_spares"]:
            P.real["gpsimd"].free_register(_r)
    s_w, s_ld, s_sp, s_st = S["w"], S["ld"], S["sp"], S["st"]
    s_pe, s_act, s_dve, s_pool, s_cc = S["pe"], S["act"], S["dve"], S["pool"], S["cc"]
    s_wt = [S["wta"], S["wtb"]]
    s_chl = S["ch"]
    s_gh = [S["gha"], S["ghb"]]
    s_spg = S["spg"]
    s_ig = S["ig"]
    s_gx = S["gx"]
    s_wc = S["b8a"]

    for nm in ["ident", "flag", "conv_w", "conv_b", "a_dt", "d_param", "b1", "sb1",
               "lt2", "sel32", "seg_mask"]:
        sy.dma_start(out=A[nm][:], in_=A["in_" + nm]).then_inc(s_ld, 16)
    p0_consts = s_ld.n
    gp.dma_start(out=A["b2bf"][:], in_=A["in_b2"]).then_inc(s_w, 16)
    gp.dma_start(out=A["sb2bf"][:], in_=A["in_sb2"]).then_inc(s_w, 16)
    p0_bias = s_w.n
    xfm3 = A["xfm"][:].rearrange("p (k t) -> p k t", k=KD)
    src3 = A["in_x_fm"].rearrange("(k p) t -> p k t", p=128)
    for i in range(4):
        gp.dma_start(out=xfm3[:, i * 4:(i + 1) * 4, :],
                     in_=src3[:, i * 4:(i + 1) * 4, :]).then_inc(s_w, 16)
    p0_xfm = s_w.n
    gp.memset(A["ones1bf"][:], 1.0).then_inc(s_pool, 1)
    gp.memset(A["ones2bf"][:], 1.0).then_inc(s_pool, 1)
    gp.memset(A["ones32"][:], 1.0).then_inc(s_pool, 1)
    gp.memset(A["ones8"][:], 1.0).then_inc(s_pool, 1)
    p0_pool = s_pool.n
    ve.memset(A["h_f32"][:], 0.0).then_inc(s_dve, 1)
    ve.memset(A["h_bf16"][:], 0.0).then_inc(s_dve, 1)
    p0_h = s_dve.n
    sc.wait(s_ld, p0_consts)
    sc.activation(A["negA"][:], A["a_dt"][:, 0:1], AF.Exp).then_inc(s_act, 1)
    ve.wait(s_act, s_act.n)
    ve.tensor_scalar_mul(A["negA"][:], A["negA"][:], -1.0).then_inc(s_dve, 1)
    p0_negA = s_dve.n

    if PHASE_LIMIT < 1:
        return

    _pc = []
    for hh in range(2):
        _pc.append((bass.AP(A["d_sw1b"].tensor, hh * 1024 * 2048,
                            [[2048, 1024], [1, 2048]]),
                    bass.AP(A["in_sw1"].tensor, hh * 1024 * 2048,
                            [[2048, 1024], [1, 2048]]), S["swc"]))
    for hh in range(4):
        _pc.append((bass.AP(A["d_woutb"].tensor, hh * 1024 * 2048,
                            [[2048, 1024], [1, 2048]]),
                    bass.AP(A["in_w_out"].tensor, hh * 1024 * 2048,
                            [[2048, 1024], [1, 2048]]), S["swc"]))
    for i in range(8):
        _pc.append((bass.AP(A["d_w1p"].tensor, i * 128 * 16384,
                            [[16384, 128], [2048, 8], [1, 2048]]),
                    bass.AP(A["in_w1"].tensor, i * 8 * 128 * 2048,
                            [[2048, 128], [128 * 2048, 8], [1, 2048]]), s_wc))
        _pc.append((bass.AP(A["d_w2p"].tensor, i * 128 * 16384,
                            [[16384, 128], [2048, 8], [1, 2048]]),
                    bass.AP(A["in_w2cat"].tensor, i * 8 * 128 * 2048,
                            [[2048, 128], [128 * 2048, 8], [1, 2048]]), s_wc))
    _pc.append((A["d_b1b"], A["in_b1r"], s_wc))
    _pc.append((A["d_b2b"], A["in_b2r"], s_wc))
    for hh in range(2):
        _pc.append((bass.AP(A["d_w2shb"].tensor, hh * 1024 * 2048,
                            [[2048, 1024], [1, 2048]]),
                    bass.AP(A["in_w2cat"].tensor, (8192 + hh * 1024) * 2048,
                            [[2048, 1024], [1, 2048]]), s_wc))
    _pci = [0]

    def _emit_pc():
        if _pci[0] < len(_pc):
            d, sr, sem = _pc[_pci[0]]
            gp.dma_start(out=d, in_=sr).then_inc(sem, 16)
            _pci[0] += 1

    wt = [A["wt0"], A["wt1"]]
    vr = [A["valraw0"]]
    stg = [A["stg0"], A["stg1"], A["stg2"]]
    wt_free = [0, 0]
    vr_free = [0]
    stg_free = [0, 0, 0]
    conv_act = 0

    for ft in range(min(NFT, FT_LIMIT)):
        typ = ("val" if ft < 32 else "gate" if ft < 64 else "B" if ft < 96 else
               "C" if ft < 128 else "dt")
        par = ft % 2
        vp = 0
        sg = ft % 3
        gp.wait(s_pe, wt_free[par])
        gp.dma_start(out=wt[par][:], in_=A["in_w_in"][ft * 128:(ft + 1) * 128, :]).then_inc(s_wt[par], 16)
        w_rdy = s_wt[par].n
        if ft >= 10 and ft % 5 == 0:
            _emit_pc()
        te.wait(s_wt[par], w_rdy)
        if ft == 0:
            te.wait(s_w, p0_xfm)
        halo_done = 0
        if typ == "val":
            P.bank_w(te, 2)
            for k in range(KD):
                mi = te.matmul(B[2][:, 0:HALO], wt[par][:, k * 128:(k + 1) * 128],
                               xfm3[:, k, 0:HALO], start=(k == 0), stop=(k == KD - 1))
            mi.then_inc(s_pe, 1)
            halo_done = s_pe.n
            P.bank_set_w(2, s_pe)
        if typ == "val":
            sc.wait(s_dve, vr_free[vp])
            sc.wait(s_pe, halo_done)
            P.bank_r(sc, 2)
            sc.copy(vr[vp][:, 0:HALO], B[2][:, 0:HALO]).then_inc(s_act, 1)
            P.bank_add_r(2, s_act)
        if typ in ("gate",) or typ == "val":
            sc.wait(s_sp, s_sp.n)
        if typ in ("B", "C"):
            ve.wait(s_sp, s_sp.n)
        for tg in range(4):
            bk = tg % 2
            P.bank_w(te, bk)
            for k in range(KD):
                mi = te.matmul(B[bk][:], wt[par][:, k * 128:(k + 1) * 128],
                               xfm3[:, k, HALO + tg * 512: HALO + (tg + 1) * 512],
                               start=(k == 0), stop=(k == KD - 1))
            mi.then_inc(s_pe, 1)
            mm_done = s_pe.n
            P.bank_set_w(bk, s_pe)
            if typ == "val":
                if tg == 0:
                    sc.wait(s_dve, vr_free[vp])
                sc.wait(s_pe, mm_done)
                P.bank_r(sc, bk)
                sc.copy(vr[vp][:, HALO + tg * 512: HALO + (tg + 1) * 512],
                        B[bk][:]).then_inc(s_act, 1)
                P.bank_add_r(bk, s_act)
            elif typ == "gate":
                sc.wait(s_pe, mm_done)
                P.bank_r(sc, bk)
                sc.activation(stg[sg][:, tg * 512:(tg + 1) * 512], B[bk][:],
                              AF.Silu).then_inc(s_act, 1)
                P.bank_add_r(bk, s_act)
            elif typ in ("B", "C"):
                ve.wait(s_pe, mm_done)
                P.bank_r(ve, bk)
                ve.tensor_copy(stg[sg][:, tg * 512:(tg + 1) * 512], B[bk][:]).then_inc(s_dve, 1)
                P.bank_add_r(bk, s_dve)
            else:
                sc.wait(s_pe, mm_done)
                P.bank_r(sc, bk)
                sc.copy(A["dt_sb"][:, tg * 512:(tg + 1) * 512], B[bk][0:32, :]).then_inc(s_act, 1)
                P.bank_add_r(bk, s_act)
        wt_free[par] = s_pe.n

        if typ == "val":
            evac_done = s_act.n
            ve.wait(s_act, max(evac_done, conv_act))
            ve.tensor_scalar_mul(A["convacc"][:], vr[vp][:, 0:T],
                                 A["conv_w"][:, ft * 4: ft * 4 + 1]).then_inc(s_dve, 1)
            for k in range(1, 4):
                ve.scalar_tensor_tensor(A["convacc"][:], vr[vp][:, k:k + T],
                                        A["conv_w"][:, ft * 4 + k: ft * 4 + k + 1],
                                        A["convacc"][:], ALU.mult, ALU.add).then_inc(s_dve, 1)
            vr_free[vp] = s_dve.n
            conv_done = s_dve.n
            sc.wait(s_dve, conv_done)
            sc.wait(s_sp, s_sp.n)
            sc.activation(stg[sg][:], A["convacc"][:], AF.Silu,
                          bias=A["conv_b"][:, ft:ft + 1]).then_inc(s_act, 1)
            conv_act = s_act.n
            sy.wait(s_act, s_act.n)
            sy.dma_start(out=A["sp_val"][ft * 128:(ft + 1) * 128, :], in_=stg[sg][:]).then_inc(s_sp, 16)
            stg_free[sg] = s_sp.n
        elif typ == "gate":
            sy.wait(s_act, s_act.n)
            sy.dma_start(out=A["sp_gate"][(ft - 32) * 128:(ft - 31) * 128, :],
                         in_=stg[sg][:]).then_inc(s_sp, 16)
            stg_free[sg] = s_sp.n
        elif typ in ("B", "C"):
            sy.wait(s_dve, s_dve.n)
            dst = A["sp_B"] if typ == "B" else A["sp_C"]
            base = (ft - 64) if typ == "B" else (ft - 96)
            sy.dma_start(out=dst[base * 128:(base + 1) * 128, :], in_=stg[sg][:]).then_inc(s_sp, 16)
            stg_free[sg] = s_sp.n
    p1_spills = s_sp.n
    p1_dt = s_act.n
    p1_pe = s_pe.n
    while _pci[0] < len(_pc):
        _emit_pc()
    P.s("sw1b_rdy", S["swc"].n)
    P.s("wb_cast", s_wc.n)

    if PHASE_LIMIT < 2:
        sy.wait(s_sp, p1_spills)
        _dump_taps(P, A, S)
        return

    ve.wait(s_act, p1_dt)
    ve.wait(s_ld, p0_consts)
    ve.wait(s_dve, p0_negA)
    ve.tensor_scalar_add(A["dt_sb"][:], A["dt_sb"][:], A["a_dt"][:, 1:2]).then_inc(s_dve, 1)
    dt_abs = s_dve.n
    sc.wait(s_dve, dt_abs)
    sc.activation(A["dtscr"][:], A["dt_sb"][:], AF.Abs).then_inc(s_act, 1)
    sc.activation(A["dtscr"][:], A["dtscr"][:], AF.Exp, scale=-1.0).then_inc(s_act, 1)
    sc.activation(A["dtscr"][:], A["dtscr"][:], AF.Ln, bias=1.0).then_inc(s_act, 1)
    dt_ln = s_act.n
    ve.wait(s_act, dt_ln)
    ve.tensor_scalar_max(A["dt_sb"][:], A["dt_sb"][:], 0.0).then_inc(s_dve, 1)
    ve.tensor_add(A["dt_sb"][:], A["dt_sb"][:], A["dtscr"][:]).then_inc(s_dve, 1)
    ve.tensor_scalar_mul(A["dt_sb"][:], A["dt_sb"][:], A["negA"][:, 0:1]).then_inc(s_dve, 1)
    ve.tensor_tensor_scan(A["cum_fm"][:], A["seg_mask"][:], A["dt_sb"][:], 0.0,
                          ALU.mult, ALU.add).then_inc(s_dve, 1)
    for c in range(NCH):
        ve.tensor_copy(A["lasts"][:, c:c + 1],
                       A["cum_fm"][:, c * CS + CS - 1: c * CS + CS]).then_inc(s_dve, 1)
    ve.wait(s_pool, p0_pool)
    ve.tensor_tensor_scan(A["decays"][:], A["ones8"][:], A["lasts"][:], 0.0,
                          ALU.mult, ALU.add).then_inc(s_dve, 1)
    ve.tensor_sub(A["decays"][:], A["decays"][:], A["lasts"][:]).then_inc(s_dve, 1)
    p2_cum = s_dve.n
    v_tm = [A["val_tm0"], A["val_tm1"]]
    b_tm = [A["B_tm0"], A["B_tm1"]]
    yd_st = [A["yd_st0"], A["yd_st0"]]
    ce_st = [A["ce_st0"], A["ce_st0"]]
    spl_v3 = A["sp_val"].rearrange("(i p) t -> p i t", p=128)
    spl_b3 = A["sp_B"].rearrange("(i p) t -> p i t", p=128)
    spl_c3 = A["sp_C"].rearrange("(i p) t -> p i t", p=128)
    vfm3 = A["val_fm"][:].rearrange("p (i t) -> p i t", i=32)
    bfm3 = A["B_fm"][:].rearrange("p (i t) -> p i t", i=32)
    cfm3 = A["C_fm"][:].rearrange("p (i t) -> p i t", i=32)
    yd_free = [0, 0]
    ce_free = [0, 0]

    for c in range(NCH):
        sy.wait(s_sp, p1_spills)
        sy.wait(s_pe, max(P.g("p2_fm_pe"), p1_pe))
        sy.wait(s_dve, P.g("p2_fm_dve"))
        sy.wait(s_pool, P.g("p2_fm_pool"))
        for q in range(4):
            sl = slice(q * 8, (q + 1) * 8)
            sy.dma_start(out=vfm3[:, sl, :], in_=spl_v3[:, sl, c * CS:(c + 1) * CS]).then_inc(s_chl, 16)
            sy.dma_start(out=bfm3[:, sl, :], in_=spl_b3[:, sl, c * CS:(c + 1) * CS]).then_inc(s_chl, 16)
            sy.dma_start(out=cfm3[:, sl, :], in_=spl_c3[:, sl, c * CS:(c + 1) * CS]).then_inc(s_chl, 16)
        for mh in range(2):
            col = c * CS + mh * 128
            sy.dma_start(out=v_tm[mh][:], in_=A["sp_val"][:, col:col + 128], transpose=True).then_inc(s_chl, 16)
            sy.dma_start(out=b_tm[mh][:], in_=A["sp_B"][:, col:col + 128], transpose=True).then_inc(s_chl, 16)
        ld_chunk = s_chl.n

        cum_c = A["cum_fm"][:, c * CS:(c + 1) * CS]
        te.wait(s_dve, p2_cum)
        for mh in range(2):
            P.bank_w(te, 6)
            te.transpose(B[6][0:128, 0:32], cum_c[:, mh * 128:(mh + 1) * 128],
                         A["ident"][0:32, 0:32]).then_inc(s_pe, 1)
            P.bank_set_w(6, s_pe)
            ve.wait(s_pe, s_pe.n)
            P.bank_r(ve, 6)
            ve.wait(s_act, P.g("p2_acts_prev"))
            ve.tensor_copy(A["cum_tm"][:, mh * 32:(mh + 1) * 32], B[6][0:128, 0:32]).then_inc(s_dve, 1)
            P.bank_add_r(6, s_dve)
        ve.tensor_scalar_mul(A["neg_cum"][:], A["cum_tm"][:], -1.0).then_inc(s_dve, 1)
        ve.wait(s_pe, P.g("p2_diag_pe"))
        ve.tensor_scalar_mul(A["dtscr"][0:32, 0:32], A["ident"][0:32, 0:32],
                             A["lasts"][:, c:c + 1]).then_inc(s_dve, 1)
        diag_rdy = s_dve.n
        te.wait(s_dve, diag_rdy)
        P.bank_w(te, 7)
        te.matmul(B[7][:, 0:32], A["ones32"][:], A["dtscr"][0:32, 0:32],
                  start=True, stop=True).then_inc(s_pe, 1)
        P.s("p2_diag_pe", s_pe.n)
        P.bank_set_w(7, s_pe)
        ve.wait(s_pe, s_pe.n)
        P.bank_r(ve, 7)
        ve.tensor_copy(A["lastsbc"][:], B[7][:, 0:32]).then_inc(s_dve, 1)
        P.bank_add_r(7, s_dve)
        lbc_rdy = s_dve.n
        sc.wait(s_dve, lbc_rdy)
        sc.wait(s_dve, P.g("p2_h_valid"))
        sc.activation(A["el_bc"][:], A["lastsbc"][:], AF.Exp).then_inc(s_act, 1)
        el_rdy = s_act.n
        for mh in range(2):
            ve.tensor_sub(A["w_mh"][:, mh * 32:(mh + 1) * 32], A["lastsbc"][:],
                          A["cum_tm"][:, mh * 32:(mh + 1) * 32]).then_inc(s_dve, 1)
        w_sub = s_dve.n
        sc.wait(s_dve, w_sub)
        sc.activation(A["w_mh"][:], A["w_mh"][:], AF.Exp).then_inc(s_act, 1)
        w_rdy2 = s_act.n
        pre_dve = s_dve.n

        h_valid_chunk = P.g("p2_h_valid")
        pend = {}

        def s1(h):
            hp = h % 2
            d = pend.setdefault(h, {})
            te.wait(s_dve, pre_dve)
            te.wait(s_chl, ld_chunk)
            P.bank_w(te, hp)
            te.matmul(B[hp][:, 0:256],
                      A["sel32"][:, h * 128:(h + 1) * 128], cum_c,
                      start=True, stop=True).then_inc(s_pe, 1)
            d["bc"] = s_pe.n
            P.bank_set_w(hp, s_pe)
            P.bank_w(te, 2 + hp)
            for mh in range(2):
                mi = te.matmul(B[2 + hp][:, mh * 256:(mh + 1) * 256],
                               bfm3[:, h, mh * 128:(mh + 1) * 128],
                               cfm3[:, h, :], start=True, stop=True)
            mi.then_inc(s_pe, 1)
            d["cbt"] = s_pe.n
            P.bank_set_w(2 + hp, s_pe)

        def sdiff(h):
            hp = h % 2
            d = pend[h]
            ve.wait(s_pe, d["bc"])
            if hp == 0:
                ve.wait(s_act, P.g("p2_e_act"))
                ve.wait(s_dve, P.g("p2_mt"))
            P.bank_r(ve, hp)
            for mh in range(2):
                ve.scalar_tensor_tensor(
                    A["e0"][:, hp * 512 + mh * 256: hp * 512 + (mh + 1) * 256],
                    B[hp][:, 0:256],
                    A["neg_cum"][:, mh * 32 + h: mh * 32 + h + 1],
                    A["lt2"][:, mh * 256:(mh + 1) * 256],
                    ALU.add, ALU.add).then_inc(s_dve, 1)
            P.bank_add_r(hp, s_dve)
            d["diff"] = s_dve.n

        def s2pair(h):
            grp = h // 8
            gpar = grp % 2
            hr = h % 8
            d = pend[h]
            d1 = pend[h + 1]
            sc.wait(s_dve, d1["diff"])
            sc.activation(A["e0"][:], A["e0"][:], AF.Exp).then_inc(s_act, 1)
            P.s("p2_e_act", s_act.n)
            d["e"] = s_act.n
            sc.wait(s_pool, P.g("p2_expcum_pool"))
            for hh in range(2):
                sc.wait(s_pe, pend[h + hh]["bc"])
                P.bank_r(sc, hh)
                sc.activation(A["expcum"][:, hh * 256:(hh + 1) * 256],
                              B[hh][:, 0:256], AF.Exp).then_inc(s_act, 1)
                P.bank_add_r(hh, s_act)
            d["ec"] = s_act.n
            sc.wait(s_chl, ld_chunk)
            sc.wait(s_act, w_rdy2)
            for hh in range(2):
                hp = hh
                sc.wait(s_pe, P.g("p2_bw_pe_%d" % hp))
                for mh in range(2):
                    sc.activation(A["Bw"][:, hp * 256 + mh * 128: hp * 256 + (mh + 1) * 128],
                                  b_tm[mh][:, (h + hh) * 128:(h + hh + 1) * 128], AF.Copy,
                                  scale=A["w_mh"][:, mh * 32 + h + hh: mh * 32 + h + hh + 1]).then_inc(s_act, 1)
                pend[h + hh]["bw"] = s_act.n
            gp.wait(s_chl, ld_chunk)
            gp.wait(s_act, d["ec"])
            gp.wait(s_sp, s_sp.n)
            gp.wait(s_pe, P.g("p2_ce_pe_%d" % gpar))
            for hh in range(2):
                gp.tensor_mul(ce_st[gpar][:, (hr + hh) * 256:(hr + hh + 1) * 256],
                              cfm3[:, h + hh, :],
                              A["expcum"][:, hh * 256:(hh + 1) * 256]).then_inc(s_pool, 1)
            P.s("p2_expcum_pool", s_pool.n)
            d["ce"] = s_pool.n
            d1["ce"] = s_pool.n
            for hh in range(2):
                hp = hh
                dd = pend[h + hh]
                ve.wait(s_pe, dd["cbt"])
                ve.wait(s_act, d["e"])
                ve.wait(s_pe, P.g("p2_y_pe_%d" % hp))
                P.bank_r(ve, 2 + hp)
                ve.tensor_mul(A["Mt0"][:, hp * 512:(hp + 1) * 512], B[2 + hp][:, 0:512],
                              A["e0"][:, hp * 512:(hp + 1) * 512]).then_inc(s_dve, 1)
                P.bank_add_r(2 + hp, s_dve)
                dd["mt"] = s_dve.n
            P.s("p2_mt", s_dve.n)

        def s3(h):
            hp = h % 2
            grp = h // 8
            gpar = grp % 2
            hr = h % 8
            d = pend[h]
            te.wait(s_dve, d["mt"])
            te.wait(s_pool, d["ce"])
            te.wait(s_dve, h_valid_chunk)
            P.bank_w(te, 4 + hp)
            te.matmul(B[4 + hp][:, 0:256], v_tm[0][:, h * 128:(h + 1) * 128],
                      A["Mt0"][:, hp * 512: hp * 512 + 256], start=True, stop=False)
            te.matmul(B[4 + hp][:, 0:256], v_tm[1][:, h * 128:(h + 1) * 128],
                      A["Mt0"][:, hp * 512 + 256: hp * 512 + 512], start=False, stop=False)
            te.matmul(B[4 + hp][:, 0:256], A["h_bf16"][:, h * 128:(h + 1) * 128],
                      ce_st[gpar][:, hr * 256:(hr + 1) * 256],
                      start=False, stop=True).then_inc(s_pe, 1)
            d["y"] = s_pe.n
            P.s("p2_y_pe_%d" % hp, s_pe.n)
            P.s("p2_ce_pe_%d" % gpar, s_pe.n)
            P.bank_set_w(4 + hp, s_pe)
            te.wait(s_act, d["bw"])
            P.bank_w(te, 6 + hp)
            te.matmul(B[6 + hp][:, 0:128], A["Bw"][:, hp * 256: hp * 256 + 128],
                      v_tm[0][:, h * 128:(h + 1) * 128], start=True, stop=False)
            te.matmul(B[6 + hp][:, 0:128], A["Bw"][:, hp * 256 + 128: hp * 256 + 256],
                      v_tm[1][:, h * 128:(h + 1) * 128], start=False, stop=True).then_inc(s_pe, 1)
            d["st"] = s_pe.n
            P.s("p2_bw_pe_%d" % hp, s_pe.n)
            P.bank_set_w(6 + hp, s_pe)

        def s4(h):
            hp = h % 2
            grp = h // 8
            gpar = grp % 2
            hr = h % 8
            d = pend[h]
            ve.wait(s_pe, d["y"])
            ve.wait(s_sp, s_sp.n)
            P.bank_r(ve, 4 + hp)
            ve.scalar_tensor_tensor(yd_st[gpar][:, hr * 256:(hr + 1) * 256],
                                    vfm3[:, h, :], A["d_param"][:, h:h + 1],
                                    B[4 + hp][:, 0:256], ALU.mult, ALU.add).then_inc(s_dve, 1)
            P.bank_add_r(4 + hp, s_dve)
            P.s("p2_fm_dve", s_dve.n)
            d["yd"] = s_dve.n
            ve.wait(s_pe, d["st"])
            ve.wait(s_act, el_rdy)
            P.bank_r(ve, 6 + hp)
            ve.scalar_tensor_tensor(A["h_f32"][:, h * 128:(h + 1) * 128],
                                    A["h_f32"][:, h * 128:(h + 1) * 128],
                                    A["el_bc"][:, h:h + 1],
                                    B[6 + hp][:, 0:128], ALU.mult, ALU.add).then_inc(s_dve, 1)
            P.bank_add_r(6 + hp, s_dve)
            ve.tensor_copy(A["h_bf16"][:, h * 128:(h + 1) * 128],
                           A["h_f32"][:, h * 128:(h + 1) * 128]).then_inc(s_dve, 1)
            P.s("p2_h_valid", s_dve.n)
            if h % 8 == 7:
                sy.wait(s_dve, d["yd"])
                sy.dma_start(out=A["sp_Yd"][c * 128:(c + 1) * 128, grp * 2048:(grp + 1) * 2048],
                             in_=yd_st[gpar][:]).then_inc(s_sp, 16)
                sy.wait(s_pool, d["ce"])
                sy.wait(s_pe, d["y"])
                sy.dma_start(out=A["sp_Cexp"][c * 128:(c + 1) * 128, grp * 2048:(grp + 1) * 2048],
                             in_=ce_st[gpar][:]).then_inc(s_sp, 16)
            del pend[h]

        s1(0)
        sdiff(0)
        s1(1)
        sdiff(1)
        for p in range(H // 2):
            h = 2 * p
            s2pair(h)
            if h + 2 < H:
                s1(h + 2)
                sdiff(h + 2)
                s1(h + 3)
                sdiff(h + 3)
            s3(h)
            s4(h)
            s3(h + 1)
            s4(h + 1)

        P.s("p2_fm_pe", s_pe.n)
        P.s("p2_fm_pool", s_pool.n)
        P.s("p2_acts_prev", s_act.n)
        P.s("p2_pool_prev", s_pool.n)
    p2_spills = s_sp.n
    p2_h = P.g("p2_h_valid")

    if PHASE_LIMIT < 3:
        sy.wait(s_sp, p2_spills)
        _dump_taps(P, A, S)
        return

    ag3 = A["ag_in"].rearrange("(h n) d -> n h d", n=128)
    h3 = A["h_f32"][:].rearrange("p (h d) -> p h d", h=H)
    gp.wait(s_dve, p2_h)
    for q in range(4):
        sl = slice(q * 8, (q + 1) * 8)
        gp.dma_start(out=ag3[:, sl, :], in_=h3[:, sl, :]).then_inc(s_w, 16)
    ag_in_done = s_w.n
    gp.wait(s_w, ag_in_done)
    gp.collective_compute("AllGather", ALU.bypass,
                          replica_groups=[[0, 1], [2, 3], [4, 5], [6, 7]],
                          ins=[A["ag_in"].opt()], outs=[A["ag_out"].opt()]).then_inc(s_cc, 1)
    ago3 = A["ag_out"][0:4096, :].rearrange("(h n) d -> n h d", n=128)
    sy.wait(s_cc, 1)
    for q in range(4):
        sl = slice(q * 8, (q + 1) * 8)
        sy.dma_start(out=h3[:, sl, :], in_=ago3[:, sl, :]).then_inc(s_ld, 16)
    h8_done = s_ld.n

    yd_l = A["val_fm"]
    ce_l = A["B_fm"]
    gt_l = A["C_fm"]
    gt3 = gt_l[:].rearrange("p (i t) -> p i t", i=32)
    spg3 = A["sp_gate"].rearrange("(i p) t -> p i t", p=128)
    yg_st = [A["yd_st0"], A["ce_st0"]]
    yg_free = [0, 0]
    spyg3 = A["sp_Yg"].rearrange("(i p) t -> p i t", p=128)
    for c in range(NCH):
        sy.wait(s_sp, p2_spills)
        sy.wait(s_pe, P.g("p25_ld_pe"))
        sy.wait(s_dve, P.g("p25_ld_dve"))
        sy.wait(s_pool, P.g("p25_ld_pool"))
        sy.dma_start(out=yd_l[:], in_=A["sp_Yd"][c * 128:(c + 1) * 128, :]).then_inc(s_chl, 16)
        sy.dma_start(out=ce_l[:], in_=A["sp_Cexp"][c * 128:(c + 1) * 128, :]).then_inc(s_chl, 16)
        for q in range(4):
            sl = slice(q * 8, (q + 1) * 8)
            sy.dma_start(out=gt3[:, sl, :], in_=spg3[:, sl, c * CS:(c + 1) * CS]).then_inc(s_chl, 16)
        ld_c = s_chl.n
        ve.wait(s_dve, p2_cum)
        ve.wait(s_ld, h8_done)
        ve.wait(s_pe, P.g("p2_diag_pe"))
        ve.tensor_scalar_mul(A["dtscr"][0:32, 0:32], A["ident"][0:32, 0:32],
                             A["decays"][:, c:c + 1]).then_inc(s_dve, 1)
        diag_rdy = s_dve.n
        te.wait(s_dve, diag_rdy)
        P.bank_w(te, 7)
        te.matmul(B[7][:, 0:32], A["ones32"][:], A["dtscr"][0:32, 0:32],
                  start=True, stop=True).then_inc(s_pe, 1)
        P.s("p2_diag_pe", s_pe.n)
        P.bank_set_w(7, s_pe)
        ve.wait(s_pe, s_pe.n)
        P.bank_r(ve, 7)
        ve.wait(s_act, P.g("p25_dbc_act"))
        ve.tensor_copy(A["lastsbc"][:], B[7][:, 0:32]).then_inc(s_dve, 1)
        P.bank_add_r(7, s_dve)
        sc.wait(s_dve, s_dve.n)
        sc.wait(s_dve, P.g("p25_cl_last"))
        sc.activation(A["D_bc"][:], A["lastsbc"][:], AF.Exp).then_inc(s_act, 1)
        P.s("p25_dbc_act", s_act.n)
        dbc_rdy = s_act.n
        def cl(h):
            hp = h % 4
            ve.wait(s_act, dbc_rdy)
            ve.wait(s_ld, h8_done)
            ve.wait(s_pe, P.g("p25_cl_pe_%d" % hp))
            ve.tensor_scalar(A["clbf"][:, hp * 128:(hp + 1) * 128],
                             A["h_f32"][:, h * 128:(h + 1) * 128],
                             A["flag"][:, 0:1], A["D_bc"][:, h:h + 1],
                             ALU.mult, ALU.mult).then_inc(s_dve, 1)
            P.s("p25_cl_last", s_dve.n)
            P.s("p25_cl_rdy_%d" % hp, s_dve.n)

        def corr(h):
            hp = h % 4
            te.wait(s_dve, P.g("p25_cl_rdy_%d" % hp))
            te.wait(s_chl, ld_c)
            P.bank_w(te, hp)
            te.matmul(B[hp][:, 0:256], A["clbf"][:, hp * 128:(hp + 1) * 128],
                      ce_l[:, h * 256:(h + 1) * 256], start=True, stop=True).then_inc(s_pe, 1)
            P.s("p25_ld_pe", s_pe.n)
            P.s("p25_cl_pe_%d" % hp, s_pe.n)
            P.s("p25_corr_%d" % hp, s_pe.n)
            P.bank_set_w(hp, s_pe)

        def finish(h):
            hp = h % 4
            grp = h // 8
            gpar = grp % 2
            ve.wait(s_pe, P.g("p25_corr_%d" % hp))
            ve.wait(s_pool, P.g("p25_ct_pool_%d" % hp))
            P.bank_r(ve, hp)
            ve.tensor_add(A["ctmp"][:, hp * 256:(hp + 1) * 256],
                          yd_l[:, h * 256:(h + 1) * 256], B[hp][:, 0:256]).then_inc(s_dve, 1)
            P.bank_add_r(hp, s_dve)
            P.s("p25_ld_dve", s_dve.n)
            t_rdy = s_dve.n
            gp.wait(s_dve, t_rdy)
            gp.wait(s_sp, s_sp.n)
            gp.tensor_mul(yg_st[gpar][:, (h % 8) * 256:(h % 8 + 1) * 256],
                          A["ctmp"][:, hp * 256:(hp + 1) * 256],
                          gt3[:, h, :]).then_inc(s_pool, 1)
            P.s("p25_ct_pool_%d" % hp, s_pool.n)
            P.s("p25_ld_pool", s_pool.n)
            yg_rdy = s_pool.n
            if h % 8 == 7:
                sy.wait(s_pool, yg_rdy)
                src = yg_st[gpar][:].rearrange("p (i t) -> p i t", i=8)
                sy.dma_start(out=spyg3[:, grp * 8:(grp + 1) * 8, c * CS:(c + 1) * CS],
                             in_=src).then_inc(s_sp, 16)

        for h0 in range(4):
            cl(h0)
        for h in range(H):
            corr(h)
            finish(h)
            if h + 4 < H:
                cl(h + 4)
    p25_spills = s_sp.n

    if PHASE_LIMIT < 4:
        sy.wait(s_sp, p25_spills)
        _dump_taps(P, A, S)
        return

    sy.wait(s_sp, p25_spills)
    sy.wait(s_pool, s_pool.n)
    sy.wait(s_dve, s_dve.n)
    sy.wait(s_pe, s_pe.n)
    sy.wait(S["swc"], P.g("sw1b_rdy"))
    s_wh = [S["wh0"], S["wh1"]]
    for k in range(32):
        sy.dma_start(out=A["wot"][:, k * 2048:(k + 1) * 2048],
                     in_=A["d_woutb"][k * 128:(k + 1) * 128, :]).then_inc(s_wh[0], 16)
    wot_rdy = s_wh[0].n
    ygb = [A["yg_t"], A["yg_t2"]]
    for tt in range(16):
        par = tt % 2
        sy.wait(s_sp, p25_spills)
        sy.wait(s_pe, P.g("p3_mm_%d" % par))
        ygv = ygb[par][:].rearrange("p (i t) -> p i t", i=32)
        for q in range(4):
            sl = slice(q * 8, (q + 1) * 8)
            sy.dma_start(out=ygv[:, sl, :], in_=spyg3[:, sl, tt * 128:(tt + 1) * 128]).then_inc(s_chl, 16)
        sy.wait(s_dve, P.g("p3_x1_rdy"))
        sy.dma_start(out=A["xtm_b"][:], in_=A["in_x_tm"][tt * 128:(tt + 1) * 128, :]).then_inc(s_chl, 16)
        yg_rdy2 = s_chl.n
        te.wait(s_chl, yg_rdy2)
        te.wait(s_wh[0], wot_rdy)
        mmd = [0] * 4
        for dmg in range(4):
            P.bank_w(te, dmg)
            for k in range(32):
                mi = te.matmul(B[dmg][:], ygv[:, k, :],
                               A["wot"][:, k * 2048 + dmg * 512: k * 2048 + (dmg + 1) * 512],
                               start=(k == 0), stop=(k == 31))
            mi.then_inc(s_pe, 1)
            mmd[dmg] = s_pe.n
            P.bank_set_w(dmg, s_pe)
        P.s("p3_mm_%d" % par, s_pe.n)
        for dmg in range(4):
            ve.wait(s_pe, mmd[dmg])
            P.bank_r(ve, dmg)
            ve.scalar_tensor_tensor(A["xtm_b"][:, dmg * 512:(dmg + 1) * 512],
                                    B[dmg][:], 1.0,
                                    A["xtm_b"][:, dmg * 512:(dmg + 1) * 512],
                                    ALU.mult, ALU.add).then_inc(s_dve, 1)
            P.bank_add_r(dmg, s_dve)
        ve.reduce_sum(A["ln_stats"][:, 0:1], A["xtm_b"][:], axis=AX.X).then_inc(s_dve, 1)
        x2_rdy = s_dve.n
        sc.wait(s_dve, x2_rdy)
        sc.activation(A["sq_b"][:], A["xtm_b"][:], AF.Square,
                      accum_out=A["ln_stats"][:, 1:2]).then_inc(s_act, 1)
        sq_rdy = s_act.n
        ve.wait(s_act, sq_rdy)
        ve.tensor_scalar_mul(A["ln_stats"][:, 2:3], A["ln_stats"][:, 0:1], 1.0 / DM).then_inc(s_dve, 1)
        ve.tensor_scalar_mul(A["ln_stats"][:, 3:4], A["ln_stats"][:, 1:2], 1.0 / DM).then_inc(s_dve, 1)
        ve.tensor_mul(A["ln_stats"][:, 4:5], A["ln_stats"][:, 2:3], A["ln_stats"][:, 2:3]).then_inc(s_dve, 1)
        ve.tensor_sub(A["ln_stats"][:, 4:5], A["ln_stats"][:, 3:4], A["ln_stats"][:, 4:5]).then_inc(s_dve, 1)
        ve.tensor_scalar_add(A["ln_stats"][:, 4:5], A["ln_stats"][:, 4:5], EPS).then_inc(s_dve, 1)
        var_rdy = s_dve.n
        sc.wait(s_dve, var_rdy)
        sc.activation(A["ln_stats"][:, 5:6], A["ln_stats"][:, 4:5], AF.Ln).then_inc(s_act, 1)
        ln_rdy = s_act.n
        ve.wait(s_act, ln_rdy)
        ve.tensor_scalar_mul(A["ln_stats"][:, 5:6], A["ln_stats"][:, 5:6], -0.5).then_inc(s_dve, 1)
        nl_rdy = s_dve.n
        sc.wait(s_dve, nl_rdy)
        sc.activation(A["ln_stats"][:, 6:7], A["ln_stats"][:, 5:6], AF.Exp).then_inc(s_act, 1)
        rstd_rdy = s_act.n
        ve.wait(s_act, rstd_rdy)
        ve.wait(s_sp, s_sp.n)
        ve.wait(s_spg, s_spg.n)
        ve.tensor_scalar(A["x1stage"][:], A["xtm_b"][:], A["ln_stats"][:, 2:3],
                         A["ln_stats"][:, 6:7], ALU.subtract, ALU.mult).then_inc(s_dve, 1)
        x1_rdy = s_dve.n
        P.s("p3_x1_rdy", x1_rdy)
        sy.wait(s_dve, x1_rdy)
        sy.dma_start(out=A["sp_x1f"][tt * 128:(tt + 1) * 128, :], in_=A["x1stage"][:]).then_inc(s_sp, 16)
        gp.wait(s_dve, x1_rdy)
        gp.dma_start(out=A["sp_x1bf"][tt * 128:(tt + 1) * 128, :], in_=A["x1stage"][:]).then_inc(s_spg, 16)
        P.s("p3_x1_sp", s_sp.n)
        P.s("p3_x1_spg", s_spg.n)
    p3_done = s_sp.n
    p3_done_g = s_spg.n

    if PHASE_LIMIT < 5:
        sy.wait(s_sp, p3_done)
        _dump_taps(P, A, S)
        return

    for i in range(16):
        sy.wait(s_sp, p3_done)
        sy.wait(s_spg, p3_done_g)
        sy.dma_start(out=A["x1fm"][:, i * 2048:(i + 1) * 2048],
                     in_=A["sp_x1bf"][:, i * 128:(i + 1) * 128], transpose=True).then_inc(s_chl, 16)
    x1fm_rdy = s_chl.n
    gp.dma_start(out=A["gwbf"][:].rearrange("p (k e) -> p k e", k=16),
                 in_=A["in_gate_w"].rearrange("(k p) e -> p k e", p=128)).then_inc(s_w, 16)
    gw_rdy = s_w.n
    x1v = A["x1fm"][:].rearrange("p (k t) -> p k t", k=16)
    gwv = A["gwbf"][:].rearrange("p (k e) -> p k e", k=16)
    g = A["gtmp"]
    g2 = A["gtmp2"]
    for tt in range(16):
        te.wait(s_chl, x1fm_rdy)
        te.wait(s_w, gw_rdy)
        P.bank_w(te, 2)
        for k in range(16):
            mi = te.matmul(B[2][:, 0:8], x1v[:, k, tt * 128:(tt + 1) * 128], gwv[:, k, :],
                           start=(k == 0), stop=(k == 15))
        mi.then_inc(s_pe, 1)
        lg_done = s_pe.n
        P.bank_set_w(2, s_pe)
        ve.wait(s_pe, lg_done)
        ve.wait(s_act, P.g("p4_g_act"))
        P.bank_r(ve, 2)
        ve.tensor_copy(g[:, 0:8], B[2][:, 0:8]).then_inc(s_dve, 1)
        P.bank_add_r(2, s_dve)
        ve.reduce_max(g[:, 8:9], g[:, 0:8], axis=AX.X).then_inc(s_dve, 1)
        ve.tensor_scalar(g2[:, 0:8], g[:, 0:8], g[:, 8:9], None, ALU.is_ge).then_inc(s_dve, 1)
        ve.scalar_tensor_tensor(g2[:, 8:16], g2[:, 0:8], -1e30, g[:, 0:8],
                                ALU.mult, ALU.add).then_inc(s_dve, 1)
        ve.reduce_max(g[:, 9:10], g2[:, 8:16], axis=AX.X).then_inc(s_dve, 1)
        ve.tensor_scalar_mul(g[:, 10:11], g[:, 8:9], -1.0).then_inc(s_dve, 1)
        ve.tensor_sub(g[:, 11:12], g[:, 9:10], g[:, 8:9]).then_inc(s_dve, 1)
        pre_e = s_dve.n
        sc.wait(s_dve, pre_e)
        sc.activation(g2[:, 0:8], g[:, 0:8], AF.Exp, bias=g[:, 10:11]).then_inc(s_act, 1)
        sc.activation(g[:, 12:13], g[:, 11:12], AF.Exp).then_inc(s_act, 1)
        P.s("p4_g_act", s_act.n)
        e_done2 = s_act.n
        ve.wait(s_act, e_done2)
        ve.tensor_scalar_add(g[:, 12:13], g[:, 12:13], 1.0).then_inc(s_dve, 1)
        ve.reciprocal(g[:, 13:14], g[:, 12:13]).then_inc(s_dve, 1)
        ve.tensor_scalar(g2[:, 8:16], g[:, 0:8], g[:, 9:10], None, ALU.is_ge).then_inc(s_dve, 1)
        ve.tensor_scalar(g2[:, 0:8], g2[:, 0:8], g[:, 13:14], None, ALU.mult).then_inc(s_dve, 1)
        ve.tensor_mul(g[:, 0:8], g2[:, 0:8], g2[:, 8:16]).then_inc(s_dve, 1)
        gates_rdy = s_dve.n
        te.wait(s_dve, gates_rdy)
        P.bank_w(te, 3)
        te.transpose(B[3][0:8, 0:128], g[:, 0:8], A["ident"][:]).then_inc(s_pe, 1)
        gtr_done = s_pe.n
        P.bank_set_w(3, s_pe)
        ve.wait(s_pe, gtr_done)
        P.bank_r(ve, 3)
        ve.tensor_copy(A["g_rows"][:, tt * 128:(tt + 1) * 128], B[3][0:8, 0:128]).then_inc(s_dve, 1)
        P.bank_add_r(3, s_dve)
    grows_rdy = s_dve.n

    sy.dma_start(out=A["d_gidx"], in_=A["in_gidx_init"]).then_inc(s_gx, 16)
    for nm in ["l8u", "srow16", "iota_t", "iota_w"]:
        sy.dma_start(out=A[nm][:], in_=A["in_" + nm]).then_inc(s_gx, 16)
    rt_consts = s_gx.n
    gp.memset(A["ones1f"][:], 1.0).then_inc(s_pool, 1)
    ones1f_rdy = s_pool.n

    rr = A["rts"]
    ve.wait(s_dve, grows_rdy)
    ve.tensor_scalar(A["ind8"][:], A["g_rows"][:], 0.0, None, ALU.is_gt).then_inc(s_dve, 1)
    ve.memset(A["ones8r"][:], 1.0).then_inc(s_dve, 1)
    ve.tensor_tensor_scan(A["pos8"][:], A["ones8r"][:], A["ind8"][:], 0.0,
                          ALU.mult, ALU.add).then_inc(s_dve, 1)
    ve.tensor_copy(rr[:, 0:1], A["pos8"][:, 2047:2048]).then_inc(s_dve, 1)
    ve.wait(s_gx, rt_consts)
    ve.tensor_scalar(A["cmp16"][:], A["srow16"][:], rr[:, 0:1], None, ALU.is_lt).then_inc(s_dve, 1)
    ve.tensor_reduce(rr[:, 1:2], A["cmp16"][:], AX.X, ALU.add).then_inc(s_dve, 1)
    ve.tensor_scalar_mul(rr[:, 2:3], rr[:, 1:2], 512.0).then_inc(s_dve, 1)
    c128_rdy = s_dve.n
    te.wait(s_dve, c128_rdy)
    te.wait(s_gx, rt_consts)
    P.bank_w(te, 6)
    te.matmul(B[6][0:8, 0:1], A["l8u"][:], rr[:, 2:3], start=True, stop=True).then_inc(s_pe, 1)
    P.bank_set_w(6, s_pe)
    ve.wait(s_pe, s_pe.n)
    P.bank_r(ve, 6)
    ve.tensor_copy(rr[:, 3:4], B[6][0:8, 0:1]).then_inc(s_dve, 1)
    P.bank_add_r(6, s_dve)
    ve.tensor_add(rr[:, 4:5], rr[:, 3:4], rr[:, 2:3]).then_inc(s_dve, 1)
    ve.tensor_scalar(A["pos8"][:], A["pos8"][:], rr[:, 3:4], -1.0, ALU.add, ALU.add).then_inc(s_dve, 1)
    ve.tensor_scalar(A["ind8"][:], A["ind8"][:], -1e9, 1e9, ALU.mult, ALU.add).then_inc(s_dve, 1)
    ve.tensor_add(A["pos8"][:], A["pos8"][:], A["ind8"][:]).then_inc(s_dve, 1)
    ve.tensor_scalar(A["cmp16"][:], A["srow16"][:], rr[:, 4:5], None, ALU.is_ge).then_inc(s_dve, 1)
    slot_rdy = s_dve.n
    te.wait(s_dve, slot_rdy)
    P.bank_w(te, 6)
    te.matmul(B[6][0:1, 0:16], A["ones32"][0:8, 0:1], A["cmp16"][:],
              start=True, stop=True).then_inc(s_pe, 1)
    P.bank_set_w(6, s_pe)
    ve.wait(s_pe, s_pe.n)
    P.bank_r(ve, 6)
    ve.tensor_copy(A["egrow"][:], B[6][0:1, 0:16]).then_inc(s_dve, 1)
    P.bank_add_r(6, s_dve)
    ve.tensor_scalar_mul(A["wbrow"][:], A["egrow"][:], 128.0).then_inc(s_dve, 1)
    et_rdy = s_dve.n
    te.wait(s_dve, et_rdy)
    te.wait(s_pool, ones1f_rdy)
    P.bank_w(te, 7)
    te.matmul(B[7][:, 0:16], A["ones1f"][:], A["wbrow"][:], start=True, stop=True).then_inc(s_pe, 1)
    P.bank_set_w(7, s_pe)
    ve.wait(s_pe, s_pe.n)
    P.bank_r(ve, 7)
    ve.tensor_copy(A["wb_bc"][:], B[7][:, 0:16]).then_inc(s_dve, 1)
    P.bank_add_r(7, s_dve)
    wbbc_rdy = s_dve.n

    slot_tm = [A["slot_tm0"], A["slot_tm1"]]
    gate_tm = [A["gate_tm0"], A["gate_tm1"]]
    stage2 = [A["stage2a"], A["stage2b"], A["stage2c"], A["stage2d"]]
    s_scat = [s_ig, S["e8a"]]
    g1t = A["gtmp"]
    for tt in range(16):
        par = tt % 2
        te.wait(s_dve, slot_rdy)
        te.wait(s_dve, P.g("p4_rt_ve_%d" % par))
        P.bank_w(te, 6)
        te.transpose(B[6][0:128, 0:8], A["pos8"][:, tt * 128:(tt + 1) * 128],
                     A["ident"][0:8, 0:8]).then_inc(s_pe, 1)
        P.bank_set_w(6, s_pe)
        tr1 = s_pe.n
        P.bank_w(te, 7)
        te.transpose(B[7][0:128, 0:8], A["g_rows"][:, tt * 128:(tt + 1) * 128],
                     A["ident"][0:8, 0:8]).then_inc(s_pe, 1)
        P.bank_set_w(7, s_pe)
        tr2 = s_pe.n
        ve.wait(s_pe, tr2)
        P.bank_r(ve, 6)
        ve.tensor_copy(slot_tm[par][:], B[6][0:128, 0:8]).then_inc(s_dve, 1)
        P.bank_add_r(6, s_dve)
        P.bank_r(ve, 7)
        ve.tensor_copy(gate_tm[par][:], B[7][0:128, 0:8]).then_inc(s_dve, 1)
        P.bank_add_r(7, s_dve)
        g2t = A["gtmp2"]
        ve.tensor_reduce(g1t[:, 0:1], slot_tm[par][:], AX.X, ALU.min).then_inc(s_dve, 1)
        ve.tensor_scalar(g1t[:, 2:10], slot_tm[par][:], g1t[:, 0:1], None, ALU.is_le).then_inc(s_dve, 1)
        ve.scalar_tensor_tensor(g2t[:, 0:8], g1t[:, 2:10], 1e9, slot_tm[par][:],
                                ALU.mult, ALU.add).then_inc(s_dve, 1)
        ve.tensor_reduce(g1t[:, 1:2], g2t[:, 0:8], AX.X, ALU.min).then_inc(s_dve, 1)
        ve.tensor_copy(A["s1all"][:, tt:tt + 1], g1t[:, 0:1]).then_inc(s_dve, 1)
        ve.tensor_copy(A["s2all"][:, tt:tt + 1], g1t[:, 1:2]).then_inc(s_dve, 1)
        ve.tensor_scalar(g2t[:, 8:16], slot_tm[par][:], g1t[:, 1:2], None, ALU.is_le).then_inc(s_dve, 1)
        ve.tensor_sub(g2t[:, 8:16], g2t[:, 8:16], g1t[:, 2:10]).then_inc(s_dve, 1)
        ve.tensor_mul(g1t[:, 2:10], g1t[:, 2:10], gate_tm[par][:]).then_inc(s_dve, 1)
        ve.tensor_mul(g2t[:, 8:16], g2t[:, 8:16], gate_tm[par][:]).then_inc(s_dve, 1)
        scq = tt % 2
        ic0 = 2 * scq
        ve.wait(s_scat[scq], P.g("p4_scat_%d" % scq))
        ve.wait(s_gx, rt_consts)
        ve.tensor_copy(A["islot"][:, ic0:ic0 + 1], g1t[:, 0:1]).then_inc(s_dve, 1)
        ve.tensor_copy(A["islot"][:, ic0 + 1:ic0 + 2], g1t[:, 1:2]).then_inc(s_dve, 1)
        ve.tensor_copy(stage2[2 * scq][:, 0:1], A["iota_t"][:, tt:tt + 1]).then_inc(s_dve, 1)
        ve.tensor_reduce(stage2[2 * scq][:, 1:2], g1t[:, 2:10], AX.X, ALU.add).then_inc(s_dve, 1)
        ve.tensor_copy(stage2[2 * scq + 1][:, 0:1], A["iota_t"][:, tt:tt + 1]).then_inc(s_dve, 1)
        ve.tensor_reduce(stage2[2 * scq + 1][:, 1:2], g2t[:, 8:16], AX.X, ALU.add).then_inc(s_dve, 1)
        st_rdy = s_dve.n
        gp.wait(s_dve, st_rdy)
        gp.wait(s_gx, rt_consts)
        if tt == 0:
            gp.reg_mov(A["reg_bchk"], SLOTS - 1)
        for j in range(2):
            gp.indirect_dma_start(
                out=A["d_gidx"],
                out_offset=bass.IndirectOffsetOnAxis(A["islot"][:, ic0 + j:ic0 + j + 1], 0),
                in_=stage2[2 * scq + j][:], in_offset=None,
                bounds_check=A["reg_bchk"], oob_is_err=False).then_inc(s_scat[scq], 16)
        P.s("p4_scat_%d" % scq, s_scat[scq].n)
    scat_done = [s_scat[0].n, s_scat[1].n]

    s_gst = [S["gst"], S["gse"]]
    w1t = [A["w1t0"], A["w1t1"]]
    ghst = [A["ghe_st0"], A["ghe_st1"]]
    w1_free = [0, 0]
    for rs in range(16):
        par = rs % 2
        sy.wait(s_pe, w1_free[par])
        if rs == 0:
            sy.wait(S["swc"], P.g("sw1b_rdy"))
        sy.dma_start(out=w1t[par][:], in_=A["d_sw1b"][rs * 128:(rs + 1) * 128, :]).then_inc(s_wh[par], 16)
        w1_rdy = s_wh[par].n
        sc.wait(s_gst[par], P.g("p4_ghst_%d" % par))
        for tg in range(4):
            bk = tg % 2
            te.wait(s_wh[par], w1_rdy)
            te.wait(s_chl, x1fm_rdy)
            P.bank_w(te, bk)
            for k in range(16):
                mi = te.matmul(B[bk][:], w1t[par][:, k * 128:(k + 1) * 128],
                               x1v[:, k, tg * 512:(tg + 1) * 512], start=(k == 0), stop=(k == 15))
            mi.then_inc(s_pe, 1)
            h_done = s_pe.n
            P.bank_set_w(bk, s_pe)
            if tg == 3:
                w1_free[par] = s_pe.n
            sc.wait(s_pe, h_done)
            P.bank_r(sc, bk)
            sc.activation(ghst[par][:, tg * 512:(tg + 1) * 512], B[bk][:], AF.Gelu,
                          bias=A["sb1"][:, rs:rs + 1]).then_inc(s_act, 1)
            P.bank_add_r(bk, s_act)
        sc.wait(s_act, s_act.n)
        sc.dma_start(out=A["sp_ghe"][rs * 128:(rs + 1) * 128, :], in_=ghst[par][:]).then_inc(s_gst[par], 16)
        P.s("p4_ghst_%d" % par, s_gst[par].n)
    shared_done = [s_gst[0].n, s_gst[1].n]
    P.s("p4_wc_pe_0", s_pe.n)
    P.s("p4_wc_pe_1", s_pe.n)

    gp.wait(s_ig, scat_done[0])
    gp.wait(S["e8a"], scat_done[1])
    gp.dma_start(out=A["gixall"][:].rearrange("p (s c) -> p s c", c=2),
                 in_=bass.AP(A["d_gidx"].tensor, 0,
                             [[2, 128], [256, NST], [1, 2]])).then_inc(s_ig, 16)
    gix_rdy = s_ig.n
    ve.wait(s_ig, gix_rdy)
    ve.tensor_copy(A["giiall"][:], A["gixall"][:]).then_inc(s_dve, 1)
    ve.tensor_copy(A["gvall"][:].rearrange("p (s o) -> p s o", o=1),
                   A["gixall"][:].rearrange("p (s c) -> p s c", c=2)[:, :, 1:2]).then_inc(s_dve, 1)
    cv_rdy = s_dve.n
    P.s("p4_cv", cv_rdy)
    gp.wait(s_dve, cv_rdy)
    gp.reg_mov(A["reg_bchk"], T - 1)
    slot = [s_ig, S["e8a"], S["e8b"], S["b8b"],
            S["g4"], S["g5"], S["g6"], S["g7"]]
    ssp = [S["gt%d" % i] for i in range(NSPL)]
    assert NSPL == 10
    xg_sb = [A["xg_sb%d" % i] for i in range(8)]
    g_cnt = {}
    st_cnt = {}
    LAG = 7
    for it in range(NST + LAG):
        si = it
        if si < NST:
            q = si % 8
            if si >= 8:
                gp.wait(ssp[(si - 8) // TPS], st_cnt[si - 8])
            gp.indirect_dma_start(
                out=xg_sb[q][:], out_offset=None,
                in_=A["sp_x1bf"],
                in_offset=bass.IndirectOffsetOnAxis(A["giiall"][:, 2 * si:2 * si + 1], 0),
                bounds_check=A["reg_bchk"], oob_is_err=False).then_inc(slot[q], 16)
            g_cnt[si] = slot[q].n
        sj = it - LAG
        if 0 <= sj < NST:
            q = sj % 8
            gp.wait(slot[q], g_cnt[sj])
            gp.dma_start(out=A["d_xg"][sj * 128:(sj + 1) * 128, :],
                         in_=xg_sb[q][:]).then_inc(ssp[sj // TPS], 16)
            st_cnt[sj] = ssp[sj // TPS].n
    split_cnt = [st_cnt[s5 * TPS + TPS - 1] for s5 in range(NSPL)]
    xtb = [A["xts0a"], A["xts0b"], A["xts0c"], A["xts0d"]]
    s_xt4 = [S["xts"], S["xto"], S["xss0"], S["xss1"]]
    s_ld4 = [s_gx, S["gxo"], S["gxs0"], S["gxs1"]]
    qeng = [sy, sc]
    ld_cnt = {}
    xtr_done = [None] * NSPL

    def _xt_store(j):
        qj = j % 2
        sl = qj * 2 + (j // 2) % 2
        qeng[qj].wait(s_ld4[sl], ld_cnt[j])
        s5j, ij = j // 16, j % 16
        qeng[qj].dma_start(
            out=A["d_xgT"][ij * 128:(ij + 1) * 128, s5j * SSL:(s5j + 1) * SSL],
            in_=xtb[sl][:, 0:SSL]).then_inc(s_xt4[sl], 16)
        P.s("p4_xtb_%d" % sl, s_xt4[sl].n)
        if ij == 15:
            xtr_done[s5j] = tuple(x.n for x in s_xt4)

    for idx in range(NSPL * 16):
        s5, i = idx // 16, idx % 16
        q = idx % 2
        sl = q * 2 + (idx // 2) % 2
        qeng[q].wait(ssp[s5], split_cnt[s5])
        qeng[q].wait(s_xt4[sl], P.g("p4_xtb_%d" % sl))
        qeng[q].dma_start(out=xtb[sl][:, 0:SSL],
                          in_=A["d_xg"][s5 * SSL:(s5 + 1) * SSL, i * 128:(i + 1) * 128],
                          transpose=True).then_inc(s_ld4[sl], 16)
        ld_cnt[idx] = s_ld4[sl].n
        if idx >= 2:
            _xt_store(idx - 2)
    _xt_store(NSPL * 16 - 2)
    _xt_store(NSPL * 16 - 1)

    w1c = [[A["w1cA0"], A["w1cA1"]], [A["w1cB0"], A["w1cB1"]]]
    w2c = [[A["w2cA0"], A["w2cA1"]], [A["w2cB0"], A["w2cB1"]]]
    h_fm = [A["h_fm0"], A["h_fm1"]]
    osb = [A["osb0"], A["osb1"]]
    wii = [A["wii0"], A["wii1"]]
    s_m8 = [S["m8a"], S["m8b"]]
    s_os = [S["os0"], S["os1"]]
    s_xq = [s_gh[0], S["xq1"], S["xq2"], S["xq3"]]
    xv3 = A["xgg"][:].rearrange("p (k t) -> p k t", k=16)
    dxt3 = A["d_xgT"].rearrange("(k p) t -> p k t", p=128)
    for si in range(NST + 1):
      if si < NST:
        spar = si % 2
        grp, st = si // 4, si % 4
        gpar = grp % 2
        if st == 0:
            for _q in range(4):
                sy.wait(s_xt4[_q], xtr_done[(grp * 512 + 511) // SSL][_q])
            sy.wait(s_pe, P.g("p4_xgg_pe"))
            for q in range(4):
                sy.dma_start(out=xv3[:, q * 4:(q + 1) * 4, :],
                             in_=dxt3[:, q * 4:(q + 1) * 4, grp * 512:(grp + 1) * 512]).then_inc(s_xq[q], 16)
                P.s("p4_xgq_%d" % q, s_xq[q].n)
            ve.wait(s_dve, wbbc_rdy)
            ve.wait(s_m8[gpar], P.g("p4_m8b_%d" % gpar))
            ve.tensor_scalar_add(A["wif"][:, 0:1], A["iota_w"][:, 0:1],
                                 A["wb_bc"][:, grp:grp + 1]).then_inc(s_dve, 1)
            ve.tensor_scalar_mul(A["wif"][:, 1:2], A["wb_bc"][:, grp:grp + 1],
                                 1.0 / 128.0).then_inc(s_dve, 1)
            ve.tensor_copy(wii[gpar][:, 0:2], A["wif"][:, 0:2]).then_inc(s_dve, 1)
            wi_rdy = s_dve.n
            gp.wait(s_dve, wi_rdy)
            if si == 0:
                gp.wait(s_wc, P.g("wb_cast"))
                gp.wait(s_gst[0], shared_done[0])
                gp.wait(s_gst[1], shared_done[1])
                gp.reg_mov(A["reg_bchk"], 1023)
            gp.wait(s_pe, P.g("p4_wc_pe_%d" % gpar))
            for half in range(2):
                gp.indirect_dma_start(
                    out=bass.AP(w1c[half][gpar].tensor, 0, [[8192, 128], [1, 8192]]),
                    out_offset=None,
                    in_=A["d_w1p"], in_offset=bass.IndirectOffsetOnAxis(wii[gpar][:, 0:1], 0),
                    element_offset=half * 8192,
                    bounds_check=A["reg_bchk"], oob_is_err=False).then_inc(s_m8[gpar], 16)
                gp.indirect_dma_start(
                    out=bass.AP(w2c[half][gpar].tensor, 0, [[8192, 128], [1, 8192]]),
                    out_offset=None,
                    in_=A["d_w2p"], in_offset=bass.IndirectOffsetOnAxis(wii[gpar][:, 0:1], 0),
                    element_offset=half * 8192,
                    bounds_check=A["reg_bchk"], oob_is_err=False).then_inc(s_m8[gpar], 16)
            gp.wait(s_pe, P.g("p4_xgg_pe"))
            gp.indirect_dma_start(
                out=A["b1b"][:], out_offset=None,
                in_=A["d_b1b"], in_offset=bass.IndirectOffsetOnAxis(wii[gpar][:, 1:2], 0),
                bounds_check=A["reg_bchk"], oob_is_err=False).then_inc(s_m8[gpar], 16)
            P.s("p4_m8_%d" % gpar, s_m8[gpar].n)
        if st == 1:
            gp.wait(s_m8[gpar], P.g("p4_m8_%d" % gpar))
            gp.wait(s_pe, P.g("p4_w2pe_%d" % spar))
            gp.indirect_dma_start(
                out=A["b2b"][:], out_offset=None,
                in_=A["d_b2b"], in_offset=bass.IndirectOffsetOnAxis(wii[gpar][:, 1:2], 0),
                bounds_check=A["reg_bchk"], oob_is_err=False).then_inc(s_m8[gpar], 16)
            P.s("p4_m8b_%d" % gpar, s_m8[gpar].n)
        if st % 2 == 0:
            te.wait(s_m8[gpar], P.g(("p4_m8_%d" if st == 0 else "p4_m8b_%d") % gpar))
            for bk in range(4, 8):
                P.bank_w(te, bk)
            for k in range(16):
                if k % 4 == 0:
                    te.wait(s_xq[k // 4], P.g("p4_xgq_%d" % (k // 4)))
                for ht in range(8):
                    half, hq = ht // 4, ht % 4
                    bk = 4 + ht // 2
                    c0 = (ht % 2) * 256
                    te.matmul(B[bk][:, c0:c0 + 256],
                              w1c[half][gpar][:, hq * 2048 + k * 128: hq * 2048 + (k + 1) * 128],
                              xv3[:, k, st * 128:(st + 2) * 128],
                              start=(k == 0 and ht % 2 == 0), stop=False)
            for ht in range(8):
                bk = 4 + ht // 2
                c0 = (ht % 2) * 256
                mi = te.matmul(B[bk][:, c0:c0 + 256],
                               A["b1b"][0:1, ht * 128:(ht + 1) * 128],
                               A["ones2bf"][:], start=False, stop=(ht % 2 == 1))
            mi.then_inc(s_pe, 1)
            P.s("p4_h_all", s_pe.n)
            for bk in range(4, 8):
                P.bank_set_w(bk, s_pe)
        if st == 3:
            P.s("p4_xgg_pe", s_pe.n)
        sc.wait(s_pe, P.g("p4_w2pe_%d" % spar))
        sc.wait(s_pe, P.g("p4_h_all"))
        for ht in range(8):
            bk = 4 + ht // 2
            c0 = (ht % 2) * 256 + (st % 2) * 128
            P.bank_r(sc, bk)
            sc.activation(h_fm[spar][:, ht * 128:(ht + 1) * 128], B[bk][:, c0:c0 + 128],
                          AF.Gelu).then_inc(s_act, 1)
            P.bank_add_r(bk, s_act)
        P.s("p4_gl_%d" % spar, s_act.n)
      sj = si - 1
      if 0 <= sj < NST:
        sparj = sj % 2
        grpj, stj = sj // 4, sj % 4
        gparj = grpj % 2
        te.wait(s_act, P.g("p4_gl_%d" % sparj))
        te.wait(s_m8[gparj], P.g("p4_m8b_%d" % gparj))
        for dmg in range(4):
            P.bank_w(te, dmg)
        for ht in range(8):
            half, hq = ht // 4, ht % 4
            for dmg in range(4):
                te.matmul(B[dmg][:], h_fm[sparj][:, ht * 128:(ht + 1) * 128],
                          w2c[half][gparj][:, hq * 2048 + dmg * 512: hq * 2048 + (dmg + 1) * 512],
                          start=(ht == 0), stop=False)
        for dmg in range(4):
            mi = te.matmul(B[dmg][:], A["ones1bf"][:],
                           A["b2b"][0:1, dmg * 512:(dmg + 1) * 512], start=False, stop=True)
            mi.then_inc(s_pe, 1)
            P.s("p4_o_done_%d" % dmg, s_pe.n)
            P.bank_set_w(dmg, s_pe)
        P.s("p4_w2pe_%d" % sparj, s_pe.n)
        if stj == 3:
            P.s("p4_wc_pe_%d" % gparj, s_pe.n)
        ve.wait(s_dve, P.g("p4_cv"))
        ve.wait(s_os[sparj], P.g("p4_ost_%d" % sparj))
        for dmg in range(4):
            ve.wait(s_pe, P.g("p4_o_done_%d" % dmg))
            P.bank_r(ve, dmg)
            ve.tensor_scalar_mul(osb[sparj][:, dmg * 512:(dmg + 1) * 512], B[dmg][:],
                                 A["gvall"][:, sj:sj + 1]).then_inc(s_dve, 1)
            P.bank_add_r(dmg, s_dve)
        o_rdy = s_dve.n
        sy.wait(s_dve, o_rdy)
        sy.dma_start(out=A["d_rout2"][sj * 128:(sj + 1) * 128, :],
                     in_=osb[sparj][:]).then_inc(s_os[sparj], 16)
        P.s("p4_ost_%d" % sparj, s_os[sparj].n)
    p4_done = [s_os[0].n, s_os[1].n]
    p4_pe = s_pe.n

    if PHASE_LIMIT < 6:
        sy.wait(s_os[0], p4_done[0])
        sy.wait(s_os[1], p4_done[1])
        _dump_taps(P, A, S)
        return

    w2k = [A["w2k0"], A["w2k1"]]
    ghk = [A["ghk0"], A["ghk1"]]
    w2_free = [0, 0]
    gh2_free = [0, 0]
    spghe3 = A["sp_ghe"].rearrange("(r p) t -> p r t", p=128)
    accv = A["out_acc"][:].rearrange("p (tt d) -> p tt d", tt=8)
    rgb = [A["r1sb"], A["r2sb"]]
    for th in range(2):
        sy.wait(s_dve, P.g("p5_acc_free"))
        sy.wait(s_os[0], p4_done[0])
        sy.wait(s_os[1], p4_done[1])
        for tt in range(8):
            row = th * 1024 + tt * 128
            sy.dma_start(out=accv[:, tt, :], in_=A["sp_x1f"][row:row + 128, :]).then_inc(s_chl, 16)
        acc_init = s_chl.n
        for kb in range(NKB):
            par = kb % 2
            sy.wait(s_pe, w2_free[par] if (th > 0 or kb >= 2) else p4_pe)
            for k in range(KB):
                row = kb * 1024 + k * 128
                sy.dma_start(out=w2k[par][:, k * 2048:(k + 1) * 2048],
                             in_=A["d_w2shb"][row:row + 128, :]).then_inc(s_wh[par], 16)
            w2_rdy = s_wh[par].n
            sy.wait(s_gst[0], shared_done[0])
            sy.wait(s_gst[1], shared_done[1])
            sy.wait(s_pe, gh2_free[par])
            sy.dma_start(out=ghk[par][:].rearrange("p (k t) -> p k t", k=KB),
                         in_=spghe3[:, kb * KB:(kb + 1) * KB, th * 1024:(th + 1) * 1024]).then_inc(s_gh[par], 16)
            gh_rdy = s_gh[par].n
            ghv = ghk[par][:].rearrange("p (k t) -> p k t", k=KB)
            for tt in range(8):
                for dmg in range(4):
                    bk = (tt * 4 + dmg) % 2
                    te.wait(s_wh[par], w2_rdy)
                    te.wait(s_gh[par], gh_rdy)
                    P.bank_w(te, bk)
                    n_extra = 1 if kb == 0 else 0
                    for k in range(KB):
                        mi = te.matmul(B[bk][:], ghv[:, k, tt * 128:(tt + 1) * 128],
                                       w2k[par][:, k * 2048 + dmg * 512: k * 2048 + (dmg + 1) * 512],
                                       start=(k == 0), stop=(k == KB - 1 and n_extra == 0))
                    if kb == 0:
                        te.wait(s_w, s_w.n)
                        mi = te.matmul(B[bk][:], A["ones1bf"][:],
                                       A["sb2bf"][:, dmg * 512:(dmg + 1) * 512], start=False, stop=True)
                    mi.then_inc(s_pe, 1)
                    mm_fin = s_pe.n
                    P.bank_set_w(bk, s_pe)
                    ve.wait(s_pe, mm_fin)
                    ve.wait(s_chl, acc_init)
                    P.bank_r(ve, bk)
                    ve.scalar_tensor_tensor(accv[:, tt, dmg * 512:(dmg + 1) * 512],
                                            B[bk][:], 1.0,
                                            accv[:, tt, dmg * 512:(dmg + 1) * 512],
                                            ALU.mult, ALU.add).then_inc(s_dve, 1)
                    P.bank_add_r(bk, s_dve)
            w2_free[par] = s_pe.n
            gh2_free[par] = s_pe.n
        shared_acc = s_dve.n
        for tt in range(8):
            gtt = th * 8 + tt
            gp.wait(s_os[0], p4_done[0])
            gp.wait(s_os[1], p4_done[1])
            gp.wait(s_dve, P.g("p5_rg_ve"))
            if th == 0 and tt == 0:
                gp.reg_mov(A["reg_bchk"], SLOTS - 1)
            for j in range(2):
                idx = A["s1all"] if j == 0 else A["s2all"]
                gp.indirect_dma_start(
                    out=rgb[j][:], out_offset=None,
                    in_=A["d_rout2"], in_offset=bass.IndirectOffsetOnAxis(idx[:, gtt:gtt + 1], 0),
                    bounds_check=A["reg_bchk"], oob_is_err=False).then_inc(s_ig, 16)
            rg_rdy = s_ig.n
            ve.wait(s_ig, rg_rdy)
            ve.wait(s_dve, shared_acc)
            for j in range(2):
                ve.tensor_add(accv[:, tt, :], accv[:, tt, :], rgb[j][:]).then_inc(s_dve, 1)
            P.s("p5_rg_ve", s_dve.n)
        acc_done = s_dve.n
        ve.wait(s_dve, acc_done)
        for tt in range(8):
            ve.reduce_sum(A["ln_stats"][:, tt:tt + 1], accv[:, tt, :],
                          axis=AX.X).then_inc(s_dve, 1)
        sum_rdy = s_dve.n
        sc.wait(s_dve, sum_rdy)
        sc.wait(s_st, s_st.n)
        for tt in range(8):
            sc.activation(A["lnscr"][:], accv[:, tt, :], AF.Square,
                          accum_out=A["ln_stats"][:, 8 + tt:9 + tt]).then_inc(s_act, 1)
        sq2 = s_act.n
        ve.wait(s_act, sq2)
        ve.tensor_scalar_mul(A["ln_stats"][:, 0:8], A["ln_stats"][:, 0:8], 1.0 / DM).then_inc(s_dve, 1)
        ve.tensor_scalar_mul(A["ln_stats"][:, 8:16], A["ln_stats"][:, 8:16], 1.0 / DM).then_inc(s_dve, 1)
        ve.tensor_mul(A["lnscr"][:, 0:8], A["ln_stats"][:, 0:8], A["ln_stats"][:, 0:8]).then_inc(s_dve, 1)
        ve.tensor_sub(A["ln_stats"][:, 8:16], A["ln_stats"][:, 8:16], A["lnscr"][:, 0:8]).then_inc(s_dve, 1)
        ve.tensor_scalar_add(A["ln_stats"][:, 8:16], A["ln_stats"][:, 8:16], EPS).then_inc(s_dve, 1)
        v2 = s_dve.n
        sc.wait(s_dve, v2)
        sc.activation(A["ln_stats"][:, 8:16], A["ln_stats"][:, 8:16], AF.Ln).then_inc(s_act, 1)
        l2 = s_act.n
        ve.wait(s_act, l2)
        ve.tensor_scalar_mul(A["ln_stats"][:, 8:16], A["ln_stats"][:, 8:16], -0.5).then_inc(s_dve, 1)
        n2 = s_dve.n
        sc.wait(s_dve, n2)
        sc.activation(A["ln_stats"][:, 8:16], A["ln_stats"][:, 8:16], AF.Exp).then_inc(s_act, 1)
        rstd_rdy2 = s_act.n
        ve.wait(s_act, rstd_rdy2)
        st_prev = s_st.n
        for tt in range(8):
            row = th * 1024 + tt * 128
            ve.wait(s_st, st_prev)
            ve.tensor_scalar(A["lnscr"][:], accv[:, tt, :], A["ln_stats"][:, tt:tt + 1],
                             A["ln_stats"][:, 8 + tt:9 + tt],
                             ALU.subtract, ALU.mult).then_inc(s_dve, 1)
            fin = s_dve.n
            sy.wait(s_dve, fin)
            sy.dma_start(out=A["out"][row:row + 128, :], in_=A["lnscr"][:]).then_inc(s_st, 16)
            st_prev = s_st.n
        P.s("p5_acc_free", s_dve.n)
    _dump_taps(P, A, S)
    sy.wait(s_st, s_st.n)


def build_nc():
    nc = bass.Bass("TRN2", target_bir_lowering=False, debug=False)
    P = Prog(nc)
    A = {}

    def din(name, shape, dtype=F32):
        A["in_" + name] = nc.dram_tensor("in_" + name, list(shape), dtype,
                                         kind="ExternalInput").ap()

    din("x_fm", (DM, T + HALO))
    din("x_tm", (T, DM))
    din("w_in", (NFT * 128, 2048))
    din("conv_w", (128, 128))
    din("conv_b", (128, 32))
    din("a_dt", (32, 2))
    din("d_param", (128, 32))
    din("w_out", (DI, DM))
    din("gate_w", (DM, 8))
    din("w1", (8192, 2048))
    din("b1", (128, 64))
    din("sw1", (2048, 2048))
    din("sb1", (128, 16))
    din("w2cat", (10240, 2048))
    din("b2", (8, 2048))
    din("sb2", (1, 2048))
    din("ident", (128, 128))
    din("flag", (128, 1))
    din("lt2", (128, 512))
    din("sel32", (32, 4096))
    din("seg_mask", (32, 2048))
    din("iota_t", (128, 16))
    din("iota_w", (128, 8))
    din("srow16", (8, 16))
    din("l8u", (8, 8))
    din("gidx_init", (SLOTS, 2))
    din("b1r", (16, 1024))
    din("b2r", (16, 2048))
    A["out"] = nc.dram_tensor("out", [T, DM], F32, kind="ExternalOutput").ap()

    def internal(name, shape, dtype):
        A[name] = nc.dram_tensor(name, list(shape), dtype, kind="Internal").ap()
        if name in DEBUG_TAPS:
            A["dbg_" + name] = nc.dram_tensor("dbg_" + name, list(shape), F32,
                                              kind="ExternalOutput").ap()

    internal("sp_val", (DI, T), BF16)
    internal("sp_gate", (DI, T), BF16)
    internal("sp_B", (DI, T), BF16)
    internal("sp_C", (DI, T), BF16)
    internal("sp_Yd", (NCH * 128, 8192), BF16)
    internal("sp_Cexp", (NCH * 128, 8192), BF16)
    internal("sp_Yg", (DI, T), BF16)
    internal("sp_x1f", (T, DM), F32)
    internal("sp_x1bf", (T, DM), BF16)
    internal("sp_ghe", (2048, T), BF16)
    internal("d_w1p", (1024, 16384), BF16)
    internal("d_w2p", (1024, 16384), BF16)
    internal("d_b1b", (16, 1024), BF16)
    internal("d_b2b", (16, 2048), BF16)
    internal("d_sw1b", (2048, 2048), BF16)
    internal("d_w2shb", (2048, 2048), BF16)
    internal("d_woutb", (4096, 2048), BF16)
    internal("d_gidx", (SLOTS, 2), F32)
    internal("d_xg", (SLOTS, DM), BF16)
    internal("d_xgT", (DM, SLOTS), BF16)
    internal("d_rout2", (SLOTS, DM), BF16)
    internal("ag_in", (4096, 128), F32)
    internal("ag_out", (8192, 128), F32)

    sdefs = [
        ("ident", [128, 128], F32), ("flag", [128, 1], F32), ("conv_w", [128, 128], F32),
        ("conv_b", [128, 32], F32), ("d_param", [128, 32], F32), ("a_dt", [32, 2], F32),
        ("negA", [32, 1], F32), ("b1", [128, 64], F32), ("sb1", [128, 16], F32),
        ("b2bf", [8, 2048], BF16), ("sb2bf", [1, 2048], BF16), ("ones1bf", [1, 128], BF16),
        ("ones2bf", [1, 256], BF16),
        ("ones32", [32, 128], F32), ("gwbf", [128, 128], BF16), ("g_rows", [8, 2048], F32),
        ("ln_stats", [128, 16], F32), ("s1all", [128, 16], I32T), ("s2all", [128, 16], I32T),
        ("sel32", [32, 4096], F32), ("seg_mask", [32, 2048], F32),
        ("dt_sb", [32, 2048], F32), ("h_f32", [128, 4096], F32), ("h_bf16", [128, 4096], BF16),
        ("lasts", [32, 8], F32), ("ones8", [32, 8], F32), ("decays", [32, 8], F32),
        ("cum_tm", [128, 64], F32), ("neg_cum", [128, 64], F32), ("w_mh", [128, 64], F32),
        ("lastsbc", [128, 32], F32), ("el_bc", [128, 32], F32), ("D_bc", [128, 32], F32),
        ("cum_fm", [32, 2048], F32), ("dtscr", [32, 2048], F32),
        ("xfm", [128, 32816], BF16), ("wt0", [128, 2048], BF16), ("wt1", [128, 2048], BF16),
        ("valraw0", [128, 2056], F32), ("convacc", [128, 2048], F32),
        ("stg0", [128, 2048], BF16), ("stg1", [128, 2048], BF16), ("stg2", [128, 2048], BF16),
        ("val_fm", [128, 8192], BF16), ("B_fm", [128, 8192], BF16), ("C_fm", [128, 8192], BF16),
        ("val_tm0", [128, 4096], BF16), ("val_tm1", [128, 4096], BF16),
        ("B_tm0", [128, 4096], BF16), ("B_tm1", [128, 4096], BF16),
        ("Bw", [128, 512], BF16), ("e0", [128, 1024], F32),
        ("Mt0", [128, 1024], BF16), ("expcum", [128, 512], BF16),
        ("lt2", [128, 512], F32),
        ("yd_st0", [128, 2048], BF16), ("ce_st0", [128, 2048], BF16),
        ("clbf", [128, 512], BF16), ("ctmp", [128, 1024], F32),
        ("wot", [128, 65536], BF16), ("yg_t", [128, 4096], BF16), ("yg_t2", [128, 4096], BF16), ("x1stage", [128, 2048], F32),
        ("xtm_b", [128, 2048], F32), ("sq_b", [128, 2048], F32),
        ("x1fm", [128, 32768], BF16), ("w1t0", [128, 2048], BF16), ("w1t1", [128, 2048], BF16),
        ("ghe_st0", [128, 2048], BF16), ("ghe_st1", [128, 2048], BF16),
        ("gel0", [128, 512], F32), ("gel1", [128, 512], F32),
        ("gtmp", [128, 16], F32), ("gtmp2", [128, 16], F32),
        ("ind8", [8, 2048], F32), ("pos8", [8, 2048], F32),
        ("slot_tm0", [128, 8], F32), ("slot_tm1", [128, 8], F32),
        ("gate_tm0", [128, 8], F32), ("gate_tm1", [128, 8], F32),
        ("islot", [128, 8], I32T),
        ("stage2a", [128, 2], F32), ("stage2b", [128, 2], F32),
        ("stage2c", [128, 2], F32), ("stage2d", [128, 2], F32),
        ("gvall", [128, 60], F32),
        ("gixall", [128, 120], F32), ("giiall", [128, 120], I32T),
        ("wif", [128, 8], F32), ("wii0", [128, 8], I32T), ("wii1", [128, 8], I32T),
        ("wb_bc", [128, 16], F32), ("cmp16", [8, 16], F32), ("rts", [8, 8], F32),
        ("egrow", [1, 16], F32), ("wbrow", [1, 16], F32),
        ("ones1f", [1, 128], F32),
        ("w1cA0", [128, 8192], BF16), ("w1cA1", [128, 8192], BF16),
        ("w1cB0", [128, 8192], BF16), ("w1cB1", [128, 8192], BF16),
        ("w2cA0", [128, 8192], BF16), ("w2cA1", [128, 8192], BF16),
        ("w2cB0", [128, 8192], BF16), ("w2cB1", [128, 8192], BF16),
        ("xgg", [128, 8192], BF16),
        ("ones8r", [8, 2048], F32),
        ("xts0a", [128, 768], BF16), ("xts0b", [128, 768], BF16),
        ("xts0c", [128, 768], BF16), ("xts0d", [128, 768], BF16),
        ("r1sb", [128, 2048], BF16), ("r2sb", [128, 2048], BF16),
        ("h_fm0", [128, 1024], BF16), ("h_fm1", [128, 1024], BF16),
        ("osb0", [128, 2048], BF16), ("osb1", [128, 2048], BF16),
        ("xg_sb0", [128, 2048], BF16), ("xg_sb1", [128, 2048], BF16),
        ("xg_sb2", [128, 2048], BF16), ("xg_sb3", [128, 2048], BF16),
        ("xg_sb4", [128, 2048], BF16), ("xg_sb5", [128, 2048], BF16),
        ("xg_sb6", [128, 2048], BF16), ("xg_sb7", [128, 2048], BF16),
        ("b2b", [128, 2048], BF16), ("b1b", [128, 1024], BF16),
        ("l8u", [8, 8], F32), ("srow16", [8, 16], F32),
        ("iota_t", [128, 16], F32), ("iota_w", [128, 8], F32),
        ("w2k0", [128, 16384], BF16), ("w2k1", [128, 16384], BF16),
        ("ghk0", [128, 8192], BF16), ("ghk1", [128, 8192], BF16),
        ("out_acc", [128, 16384], F32), ("lnscr", [128, 2048], F32),
    ]
    for nm, shp, dt in sdefs:
        A[nm] = nc.alloc_sbuf_tensor_at("sb_" + nm, shp, dt, offset=_MAP[nm]).ap()

    A["reg_exp"] = nc.gpsimd.alloc_register("r_exp")
    A["reg_off"] = nc.gpsimd.alloc_register("r_off")
    A["reg_bchk"] = nc.gpsimd.alloc_register("r_bchk")
    A["reg_spares"] = [nc.gpsimd.alloc_register("r_spare%d" % i) for i in range(16)]
    with ExitStack() as stack:
        e = stack.enter_context
        S = {nm: Sem(e(nc.semaphore("s_" + nm)), nm) for nm in
             ["w", "ld", "sp", "st", "pe", "act", "dve", "pool", "cc",
              "wta", "wtb", "ch", "gha", "ghb", "spg",
              "b8a", "b8b", "e8a", "e8b", "m8a", "m8b", "ig", "gx",
              "gt0", "gt1", "gt2", "gt3", "gt4", "gt5", "gt6", "gt7", "gt8", "gt9", "gst", "gse", "xts", "xto", "wh0", "wh1", "swc", "gxo", "gxs0", "gxs1", "xss0", "xss1", "xq1", "xq2", "xq3", "os0", "os1", "g4", "g5", "g6", "g7"]}
        Bh = [e(nc.psum_tensor("pb%d" % i, [128, 512], F32)) for i in range(8)]
        B = [b[:] for b in Bh]
        P.sems = S
        block = e(nc.Block())
        for en in ["sync", "tensor", "scalar", "vector", "gpsimd"]:
            P.reset()
            P.active = en

            def run(eng, en=en):
                P.real[en] = eng
                emit(P, A, S, B)

            getattr(block, en)(run)
    return nc


def pack_inputs(inputs):
    f32 = np.float32
    x = np.asarray(inputs["L0_attn_x"], f32)
    W_in = np.asarray(inputs["L0_attn_W_in"], f32)
    conv_w = np.asarray(inputs["L0_attn_conv_w"], f32)
    conv_b = np.asarray(inputs["L0_attn_conv_b"], f32)
    A_log = np.asarray(inputs["L0_attn_A_log"], f32)
    D_param = np.asarray(inputs["L0_attn_D_param"], f32)
    dt_bias = np.asarray(inputs["L0_attn_dt_bias"], f32)
    W_out = np.asarray(inputs["L0_attn_W_out"], f32)
    gate_w = np.asarray(inputs["L0_ffn_gate_w"], f32)
    w1 = np.asarray(inputs["L0_ffn_w1"], f32)
    b1 = np.asarray(inputs["L0_ffn_b1"], f32)
    w2 = np.asarray(inputs["L0_ffn_w2"], f32)
    b2 = np.asarray(inputs["L0_ffn_b2"], f32)
    sw1 = np.asarray(inputs["L0_ffn_shared_w1"], f32)
    sb1 = np.asarray(inputs["L0_ffn_shared_b1"], f32)
    sw2 = np.asarray(inputs["L0_ffn_shared_w2"], f32)
    sb2 = np.asarray(inputs["L0_ffn_shared_b2"], f32)

    W_pad = np.zeros((NFT * 128, 2048), f32)
    W_pad[:16416] = W_in
    w_in_p = np.ascontiguousarray(
        W_pad.reshape(NFT, 128, 16, 128).transpose(0, 3, 2, 1).reshape(NFT * 128, 2048))
    conv_w_p = np.ascontiguousarray(conv_w.reshape(32, 128, 4).transpose(1, 0, 2).reshape(128, 128))
    conv_b_p = np.ascontiguousarray(conv_b.reshape(32, 128).T)
    d_p = np.ascontiguousarray(D_param.reshape(32, 128).T)
    a_dt = np.stack([A_log, dt_bias], axis=1).astype(f32)
    w_out_p = np.ascontiguousarray(W_out.T)
    w1_p = np.ascontiguousarray(
        w1.reshape(8, 16, 128, 8, 128).transpose(0, 3, 2, 1, 4).reshape(8192, 2048))
    b1_p = np.ascontiguousarray(b1.reshape(8, 8, 128).transpose(2, 0, 1).reshape(128, 64))
    sw1_p = np.ascontiguousarray(
        sw1.reshape(16, 128, 16, 128).transpose(2, 1, 0, 3).reshape(2048, 2048))
    sb1_p = np.ascontiguousarray(sb1.reshape(16, 128).T)
    w2cat = np.ascontiguousarray(np.concatenate([w2.reshape(8192, 2048), sw2], axis=0))
    sb2_p = np.ascontiguousarray(sb2.reshape(1, 2048))
    ident = np.eye(128, dtype=f32)
    lt2 = np.zeros((128, 512), f32)
    ll = np.arange(256)[None, :]
    pp = np.arange(128)[:, None]
    lt2[:, 0:256] = np.where(ll >= pp, 0.0, -60000.0)
    lt2[:, 256:512] = np.where(ll >= 128 + pp, 0.0, -60000.0)
    sel32 = (np.eye(32, dtype=f32)[:, :, None] * np.ones(128, f32)).reshape(32, 4096)
    iota_t = (np.arange(16)[None, :] * 128 + np.arange(128)[:, None]).astype(f32)
    iota_w = (np.arange(8)[None, :] * 128 + np.arange(128)[:, None]).astype(f32)
    srow16 = np.broadcast_to((np.arange(16) * 512).astype(f32), (8, 16)).copy()
    l8u = np.triu(np.ones((8, 8), f32), 1)
    gidx_init = np.zeros((SLOTS, 2), f32)
    seg = np.ones((32, 2048), f32)
    seg[:, ::CS] = 0.0

    common = {
        "in_w_in": w_in_p, "in_conv_w": conv_w_p, "in_conv_b": conv_b_p,
        "in_a_dt": a_dt, "in_d_param": d_p, "in_w_out": w_out_p,
        "in_gate_w": gate_w, "in_w1": w1_p, "in_b1": b1_p, "in_sw1": sw1_p,
        "in_sb1": sb1_p, "in_w2cat": w2cat, "in_b2": b2, "in_sb2": sb2_p,
        "in_ident": ident, "in_lt2": lt2, "in_sel32": sel32, "in_seg_mask": seg,
        "in_iota_t": iota_t, "in_iota_w": iota_w, "in_srow16": srow16, "in_l8u": l8u,
        "in_gidx_init": gidx_init,
        "in_b1r": np.concatenate([b1.reshape(8, 1024), np.zeros((8, 1024), f32)], axis=0),
        "in_b2r": np.concatenate([b2, np.zeros((8, 2048), f32)], axis=0),
    }
    maps = []
    for core in range(8):
        b_i = core // 2
        half = core % 2
        t0 = half * T
        xs = x[b_i, t0:t0 + T]
        if half == 0:
            xh = np.concatenate([np.zeros((HALO, DM), f32), xs], axis=0)
        else:
            xh = x[b_i, t0 - HALO:t0 + T]
        m = dict(common)
        m["in_x_fm"] = np.ascontiguousarray(xh.T)
        m["in_x_tm"] = np.ascontiguousarray(xs)
        m["in_flag"] = np.full((128, 1), float(half), f32)
        maps.append(m)
    return maps


def kernel(**inputs):
    maps = pack_inputs(inputs)
    nc = build_nc()
    res = run_bass_kernel_spmd(nc, maps, core_ids=list(range(8)))
    out = np.zeros((4, 4096, 2048), np.float32)
    for core in range(8):
        b_i = core // 2
        half = core % 2
        out[b_i, half * T:(half + 1) * T] = np.asarray(res.results[core]["out"], np.float32)
    if DEBUG_TAPS:
        kernel.debug = res.results
    return out


# version 27
# speedup vs baseline: 1.0749x; 1.0451x over previous
import numpy as np
import ml_dtypes
from contextlib import ExitStack

import concourse.bass as bass
import concourse.mybir as mybir
from concourse.bass_utils import run_bass_kernel_spmd

F32 = mybir.dt.float32
BF16 = mybir.dt.bfloat16
I32T = mybir.dt.int32
AF = mybir.ActivationFunctionType
ALU = mybir.AluOpType
AX = mybir.AxisListType

DM = 2048
T = 2048
HALO = 3
NFT = 129
KD = 16
CS = 256
NCH = 8
H = 32
DI = 4096
E = 8
NKR = 80
KB = 8
NKB = 2
SLOTS = 7680
NST = 60
NGRP = 15
NSPL = 10
TPS = 6
SSL = 768
EPS = 1e-5

DEBUG_TAPS = set()
PHASE_LIMIT = 99
FT_LIMIT = NFT

SB_BASE = 16512
SB_LIMIT = 229344 - SB_BASE


def _build_map():
    m = {}

    def seq(start, items):
        o = start
        for name, byts in items:
            o = (o + 31) & ~31
            m[name] = o
            o += byts
        return o

    end0 = seq(0, [
        ("ident", 512), ("flag", 32), ("conv_w", 512), ("conv_b", 128),
        ("d_param", 128), ("a_dt", 32), ("negA", 32), ("b1", 256), ("sb1", 64),
        ("b2bf", 4096), ("sb2bf", 4096), ("ones1bf", 256), ("ones32", 512),
        ("gwbf", 256), ("g_rows", 8192), ("ln_stats", 64),
        ("lt2", 2048), ("s1all", 64), ("s2all", 64),
    ])
    assert end0 <= 25600, end0
    end1 = seq(25632, [
        ("sel32", 16384), ("sel8", 4096), ("seg_mask", 8192), ("dt_sb", 8192),
        ("h_f32", 16384), ("h_bf16", 8192),
        ("lasts", 64), ("ones8", 32), ("decays", 32),
        ("cum_tm", 256), ("neg_cum", 256), ("w_mh", 256),
        ("lastsbc", 128), ("el_bc", 128), ("D_bc", 128),
        ("cum_fm", 8192), ("dtscr", 8192),
    ])
    P2S = (end1 + 31) & ~31
    endp1 = seq(P2S, [
        ("xfm", 65632), ("wt0", 4096), ("wt1", 4096),
        ("valraw0", 8224), ("convacc", 8192),
        ("stg0", 4096), ("stg1", 4096), ("stg2", 4096),
    ])
    assert endp1 <= SB_LIMIT, endp1
    endp2 = seq(P2S, [
        ("val_fm", 16384), ("B_fm", 16384), ("C_fm", 16384),
        ("val_tm0", 8192), ("val_tm1", 8192), ("B_tm0", 8192), ("B_tm1", 8192),
        ("Bw", 1024), ("e0", 8192), ("Mt0", 4096), ("expcum", 2048),
        ("yd_st0", 4096), ("ce_st0", 4096),
    ])
    assert endp2 <= SB_LIMIT, endp2
    m["clbf"] = m["sel32"]
    m["ctmp"] = m["sel32"] + 1024
    m["wot"] = 49152
    assert 49152 + 131072 <= 180256
    m["yg_t"] = 180256
    m["yg_t2"] = 188448
    m["xtm_b"] = 196640
    m["sq_b"] = 25632
    m["x1stage"] = 33824
    m["x1fm"] = 49152
    m["w1t0"] = 114688
    m["w1t1"] = 118784
    m["ghe_st0"] = 155648
    m["ghe_st1"] = 159744
    m["gel0"] = 163840
    m["gel1"] = 165888
    m["gtmp"] = 167936
    m["gtmp2"] = 168000
    m["ind8"] = 25632
    m["pos8"] = 33824
    m["slot_tm0"] = 198656
    m["slot_tm1"] = 198720
    m["gate_tm0"] = 198784
    m["gate_tm1"] = 198848
    m["islot"] = 198912
    m["stage2a"] = 198976
    m["stage2b"] = 199008
    m["stage2c"] = 201344
    m["stage2d"] = 201376
    m["gvall"] = 199040
    m["gixall"] = 199296
    m["giiall"] = 199776
    m["wif"] = 200256
    m["wii0"] = 200288
    m["wii1"] = 200320
    m["wb_bc"] = 200352
    m["cmp16"] = 200416
    m["rts"] = 200480
    m["egrow"] = 200512
    m["wbrow"] = 200576
    m["ones1f"] = 200640
    m["l8u"] = 201152
    m["srow16"] = 201184
    m["iota_t"] = 201248
    m["iota_w"] = 201312
    m["w1cA0"] = 49152
    m["w1cA1"] = 65536
    m["w1cB0"] = 81920
    m["w1cB1"] = 98304
    m["w2cA0"] = 114688
    m["w2cA1"] = 131072
    m["w2cB0"] = 147456
    m["w2cB1"] = 163840
    m["xgg"] = 180224
    m["ones8r"] = 180224
    m["r1sb"] = 33824
    m["r2sb"] = 204800
    m["h_fm0"] = 204832
    m["h_fm1"] = 206880
    m["osb0"] = 33824
    m["osb1"] = 37920
    m["xg_sb0"] = 25632
    m["xg_sb1"] = 29728
    m["xg_sb2"] = 33824
    m["xg_sb3"] = 37920
    m["xg_sb4"] = 180224
    m["xg_sb5"] = 184320
    m["xg_sb6"] = 188416
    m["xg_sb7"] = 192512
    m["b2b"] = 42016
    m["b1b"] = 46112
    m["w2k0"] = 40960
    m["w2k1"] = 73728
    m["ghk0"] = 106496
    m["ghk1"] = 122880
    m["out_acc"] = 139264
    m["lnscr"] = 25632
    m["xts0a"] = m["g_rows"]
    m["xts0b"] = m["g_rows"] + 1536
    m["xts0c"] = m["g_rows"] + 3072
    m["xts0d"] = m["g_rows"] + 4608
    for k in m:
        m[k] += SB_BASE
    return m


_MAP = _build_map()


class Sem:
    __slots__ = ("h", "n", "name")

    def __init__(self, h, name):
        self.h = h
        self.n = 0
        self.name = name


class _I:
    __slots__ = ("r", "auto")

    def __init__(self, r, auto=None):
        self.r = r
        self.auto = auto

    def then_inc(self, sem, k=1):
        if sem is self.auto:
            return self
        sem.n += k
        if self.r is not None:
            self.r.then_inc(sem.h, k)
        return self


_NO_CHAIN = {"wait_ge", "dma_start", "dma_start_transpose", "collective_compute",
             "indirect_dma_start", "reg_load", "reg_mul", "reg_add", "reg_mov",
             "reg_sub", "nop", "sem_inc", "drain"}


class Eng:
    def __init__(self, prog, name, auto_sem_name=None):
        self._P = prog
        self._name = name
        self._auto_name = auto_sem_name

    def __getattr__(self, attr):
        P = self._P
        name = self._name
        auto_name = self._auto_name if attr not in _NO_CHAIN else None

        def call(*a, **kw):
            auto = P.sems[auto_name] if auto_name else None
            if P.active == name:
                real_eng = P.real[name]
                if auto is not None and auto.n > 0:
                    real_eng.wait_ge(auto.h, auto.n)
                ri = getattr(real_eng, attr)(*a, **kw)
                if auto is not None:
                    ri.then_inc(auto.h, 1)
                    auto.n += 1
                return _I(ri, auto)
            if auto is not None:
                auto.n += 1
            return _I(None, auto)

        return call

    def wait(self, sem, val=None):
        if val is None:
            val = sem.n
        if val <= 0:
            return
        if self._P.active == self._name:
            self._P.real[self._name].wait_ge(sem.h, val)


class Prog:
    def __init__(self, nc):
        self.nc = nc
        self.active = None
        self.real = {}
        self.sy = Eng(self, "sync")
        self.te = Eng(self, "tensor")
        self.sc = Eng(self, "scalar", "act")
        self.ve = Eng(self, "vector", "dve")
        self.gp = Eng(self, "gpsimd", "pool")

    def reset(self):
        for s in self.sems.values():
            s.n = 0
        self.bank = [{"wr": None, "rd": {}} for _ in range(8)]
        self.snap = {}

    def g(self, key):
        return self.snap.get(key, 0)

    def s(self, key, val):
        self.snap[key] = val

    def bank_w(self, eng, i):
        for s, v in self.bank[i]["rd"].items():
            eng.wait(s, v)

    def bank_r(self, eng, i):
        wr = self.bank[i]["wr"]
        if wr is not None:
            eng.wait(wr[0], wr[1])

    def bank_set_w(self, i, sem):
        self.bank[i]["wr"] = (sem, sem.n)
        self.bank[i]["rd"] = {}

    def bank_add_r(self, i, sem):
        self.bank[i]["rd"][sem] = sem.n


def _dump_taps(P, A, S):
    if not DEBUG_TAPS:
        return
    gp, sy = P.gp, P.sy
    for nm in ["w", "ld", "sp", "spg", "wta", "wtb", "ch", "gha", "ghb", "ig", "gx", "pe", "act", "dve", "pool"]:
        gp.wait(S[nm], S[nm].n)
    for name in sorted(DEBUG_TAPS):
        src = A[name]
        rows = src.shape[0]
        step = max(rows // 8, 1)
        for r0 in range(0, rows, step):
            r1 = min(r0 + step, rows)
            gp.dma_start(out=A["dbg_" + name][r0:r1, :], in_=src[r0:r1, :]).then_inc(S["st"], 16)
    sy.wait(S["st"], S["st"].n)


def emit(P, A, S, B):
    sy, te, sc, ve, gp = P.sy, P.te, P.sc, P.ve, P.gp
    if P.active == "gpsimd":
        for _r in A["reg_spares"]:
            P.real["gpsimd"].free_register(_r)
    s_w, s_ld, s_sp, s_st = S["w"], S["ld"], S["sp"], S["st"]
    s_pe, s_act, s_dve, s_pool, s_cc = S["pe"], S["act"], S["dve"], S["pool"], S["cc"]
    s_wt = [S["wta"], S["wtb"]]
    s_chl = S["ch"]
    s_gh = [S["gha"], S["ghb"]]
    s_spg = S["spg"]
    s_ig = S["ig"]
    s_gx = S["gx"]
    s_wc = S["b8a"]

    for nm in ["ident", "flag", "conv_w", "conv_b", "a_dt", "d_param", "b1", "sb1",
               "lt2", "sel32", "seg_mask"]:
        sy.dma_start(out=A[nm][:], in_=A["in_" + nm]).then_inc(s_ld, 16)
    p0_consts = s_ld.n
    gp.dma_start(out=A["b2bf"][:], in_=A["in_b2"]).then_inc(s_w, 16)
    gp.dma_start(out=A["sb2bf"][:], in_=A["in_sb2"]).then_inc(s_w, 16)
    p0_bias = s_w.n
    xfm3 = A["xfm"][:].rearrange("p (k t) -> p k t", k=KD)
    src3 = A["in_x_fm"].rearrange("(k p) t -> p k t", p=128)
    for i in range(4):
        gp.dma_start(out=xfm3[:, i * 4:(i + 1) * 4, :],
                     in_=src3[:, i * 4:(i + 1) * 4, :]).then_inc(s_w, 16)
    p0_xfm = s_w.n
    gp.memset(A["ones1bf"][:], 1.0).then_inc(s_pool, 1)
    gp.memset(A["ones32"][:], 1.0).then_inc(s_pool, 1)
    gp.memset(A["ones8"][:], 1.0).then_inc(s_pool, 1)
    p0_pool = s_pool.n
    ve.memset(A["h_f32"][:], 0.0).then_inc(s_dve, 1)
    ve.memset(A["h_bf16"][:], 0.0).then_inc(s_dve, 1)
    p0_h = s_dve.n
    sc.wait(s_ld, p0_consts)
    sc.activation(A["negA"][:], A["a_dt"][:, 0:1], AF.Exp).then_inc(s_act, 1)
    ve.wait(s_act, s_act.n)
    ve.tensor_scalar_mul(A["negA"][:], A["negA"][:], -1.0).then_inc(s_dve, 1)
    p0_negA = s_dve.n

    if PHASE_LIMIT < 1:
        return

    _pc = []
    for hh in range(2):
        _pc.append((bass.AP(A["d_sw1b"].tensor, hh * 1024 * 2048,
                            [[2048, 1024], [1, 2048]]),
                    bass.AP(A["in_sw1"].tensor, hh * 1024 * 2048,
                            [[2048, 1024], [1, 2048]]), S["swc"]))
    for hh in range(4):
        _pc.append((bass.AP(A["d_woutb"].tensor, hh * 1024 * 2048,
                            [[2048, 1024], [1, 2048]]),
                    bass.AP(A["in_w_out"].tensor, hh * 1024 * 2048,
                            [[2048, 1024], [1, 2048]]), S["swc"]))
    for i in range(8):
        _pc.append((bass.AP(A["d_w1p"].tensor, i * 128 * 16384,
                            [[16384, 128], [2048, 8], [1, 2048]]),
                    bass.AP(A["in_w1"].tensor, i * 8 * 128 * 2048,
                            [[2048, 128], [128 * 2048, 8], [1, 2048]]), s_wc))
        _pc.append((bass.AP(A["d_w2p"].tensor, i * 128 * 16384,
                            [[16384, 128], [2048, 8], [1, 2048]]),
                    bass.AP(A["in_w2cat"].tensor, i * 8 * 128 * 2048,
                            [[2048, 128], [128 * 2048, 8], [1, 2048]]), s_wc))
    _pc.append((A["d_b1b"], A["in_b1r"], s_wc))
    _pc.append((A["d_b2b"], A["in_b2r"], s_wc))
    for hh in range(2):
        _pc.append((bass.AP(A["d_w2shb"].tensor, hh * 1024 * 2048,
                            [[2048, 1024], [1, 2048]]),
                    bass.AP(A["in_w2cat"].tensor, (8192 + hh * 1024) * 2048,
                            [[2048, 1024], [1, 2048]]), s_wc))
    _pci = [0]

    def _emit_pc():
        if _pci[0] < len(_pc):
            d, sr, sem = _pc[_pci[0]]
            gp.dma_start(out=d, in_=sr).then_inc(sem, 16)
            _pci[0] += 1

    wt = [A["wt0"], A["wt1"]]
    vr = [A["valraw0"]]
    stg = [A["stg0"], A["stg1"], A["stg2"]]
    wt_free = [0, 0]
    vr_free = [0]
    stg_free = [0, 0, 0]
    conv_act = 0

    for ft in range(min(NFT, FT_LIMIT)):
        typ = ("val" if ft < 32 else "gate" if ft < 64 else "B" if ft < 96 else
               "C" if ft < 128 else "dt")
        par = ft % 2
        vp = 0
        sg = ft % 3
        gp.wait(s_pe, wt_free[par])
        gp.dma_start(out=wt[par][:], in_=A["in_w_in"][ft * 128:(ft + 1) * 128, :]).then_inc(s_wt[par], 16)
        w_rdy = s_wt[par].n
        if ft >= 10 and ft % 5 == 0:
            _emit_pc()
        te.wait(s_wt[par], w_rdy)
        if ft == 0:
            te.wait(s_w, p0_xfm)
        halo_done = 0
        if typ == "val":
            P.bank_w(te, 2)
            for k in range(KD):
                mi = te.matmul(B[2][:, 0:HALO], wt[par][:, k * 128:(k + 1) * 128],
                               xfm3[:, k, 0:HALO], start=(k == 0), stop=(k == KD - 1))
            mi.then_inc(s_pe, 1)
            halo_done = s_pe.n
            P.bank_set_w(2, s_pe)
        if typ == "val":
            sc.wait(s_dve, vr_free[vp])
            sc.wait(s_pe, halo_done)
            P.bank_r(sc, 2)
            sc.copy(vr[vp][:, 0:HALO], B[2][:, 0:HALO]).then_inc(s_act, 1)
            P.bank_add_r(2, s_act)
        if typ in ("gate",) or typ == "val":
            sc.wait(s_sp, s_sp.n)
        if typ in ("B", "C"):
            ve.wait(s_sp, s_sp.n)
        for tg in range(4):
            bk = tg % 2
            P.bank_w(te, bk)
            for k in range(KD):
                mi = te.matmul(B[bk][:], wt[par][:, k * 128:(k + 1) * 128],
                               xfm3[:, k, HALO + tg * 512: HALO + (tg + 1) * 512],
                               start=(k == 0), stop=(k == KD - 1))
            mi.then_inc(s_pe, 1)
            mm_done = s_pe.n
            P.bank_set_w(bk, s_pe)
            if typ == "val":
                if tg == 0:
                    sc.wait(s_dve, vr_free[vp])
                sc.wait(s_pe, mm_done)
                P.bank_r(sc, bk)
                sc.copy(vr[vp][:, HALO + tg * 512: HALO + (tg + 1) * 512],
                        B[bk][:]).then_inc(s_act, 1)
                P.bank_add_r(bk, s_act)
            elif typ == "gate":
                sc.wait(s_pe, mm_done)
                P.bank_r(sc, bk)
                sc.activation(stg[sg][:, tg * 512:(tg + 1) * 512], B[bk][:],
                              AF.Silu).then_inc(s_act, 1)
                P.bank_add_r(bk, s_act)
            elif typ in ("B", "C"):
                ve.wait(s_pe, mm_done)
                P.bank_r(ve, bk)
                ve.tensor_copy(stg[sg][:, tg * 512:(tg + 1) * 512], B[bk][:]).then_inc(s_dve, 1)
                P.bank_add_r(bk, s_dve)
            else:
                sc.wait(s_pe, mm_done)
                P.bank_r(sc, bk)
                sc.copy(A["dt_sb"][:, tg * 512:(tg + 1) * 512], B[bk][0:32, :]).then_inc(s_act, 1)
                P.bank_add_r(bk, s_act)
        wt_free[par] = s_pe.n

        if typ == "val":
            evac_done = s_act.n
            ve.wait(s_act, max(evac_done, conv_act))
            ve.tensor_scalar_mul(A["convacc"][:], vr[vp][:, 0:T],
                                 A["conv_w"][:, ft * 4: ft * 4 + 1]).then_inc(s_dve, 1)
            for k in range(1, 4):
                ve.scalar_tensor_tensor(A["convacc"][:], vr[vp][:, k:k + T],
                                        A["conv_w"][:, ft * 4 + k: ft * 4 + k + 1],
                                        A["convacc"][:], ALU.mult, ALU.add).then_inc(s_dve, 1)
            vr_free[vp] = s_dve.n
            conv_done = s_dve.n
            sc.wait(s_dve, conv_done)
            sc.wait(s_sp, s_sp.n)
            sc.activation(stg[sg][:], A["convacc"][:], AF.Silu,
                          bias=A["conv_b"][:, ft:ft + 1]).then_inc(s_act, 1)
            conv_act = s_act.n
            sy.wait(s_act, s_act.n)
            sy.dma_start(out=A["sp_val"][ft * 128:(ft + 1) * 128, :], in_=stg[sg][:]).then_inc(s_sp, 16)
            stg_free[sg] = s_sp.n
        elif typ == "gate":
            sy.wait(s_act, s_act.n)
            sy.dma_start(out=A["sp_gate"][(ft - 32) * 128:(ft - 31) * 128, :],
                         in_=stg[sg][:]).then_inc(s_sp, 16)
            stg_free[sg] = s_sp.n
        elif typ in ("B", "C"):
            sy.wait(s_dve, s_dve.n)
            dst = A["sp_B"] if typ == "B" else A["sp_C"]
            base = (ft - 64) if typ == "B" else (ft - 96)
            sy.dma_start(out=dst[base * 128:(base + 1) * 128, :], in_=stg[sg][:]).then_inc(s_sp, 16)
            stg_free[sg] = s_sp.n
    p1_spills = s_sp.n
    p1_dt = s_act.n
    p1_pe = s_pe.n
    while _pci[0] < len(_pc):
        _emit_pc()
    P.s("sw1b_rdy", S["swc"].n)
    P.s("wb_cast", s_wc.n)

    if PHASE_LIMIT < 2:
        sy.wait(s_sp, p1_spills)
        _dump_taps(P, A, S)
        return

    ve.wait(s_act, p1_dt)
    ve.wait(s_ld, p0_consts)
    ve.wait(s_dve, p0_negA)
    ve.tensor_scalar_add(A["dt_sb"][:], A["dt_sb"][:], A["a_dt"][:, 1:2]).then_inc(s_dve, 1)
    dt_abs = s_dve.n
    sc.wait(s_dve, dt_abs)
    sc.activation(A["dtscr"][:], A["dt_sb"][:], AF.Abs).then_inc(s_act, 1)
    sc.activation(A["dtscr"][:], A["dtscr"][:], AF.Exp, scale=-1.0).then_inc(s_act, 1)
    sc.activation(A["dtscr"][:], A["dtscr"][:], AF.Ln, bias=1.0).then_inc(s_act, 1)
    dt_ln = s_act.n
    ve.wait(s_act, dt_ln)
    ve.tensor_scalar_max(A["dt_sb"][:], A["dt_sb"][:], 0.0).then_inc(s_dve, 1)
    ve.tensor_add(A["dt_sb"][:], A["dt_sb"][:], A["dtscr"][:]).then_inc(s_dve, 1)
    ve.tensor_scalar_mul(A["dt_sb"][:], A["dt_sb"][:], A["negA"][:, 0:1]).then_inc(s_dve, 1)
    ve.tensor_tensor_scan(A["cum_fm"][:], A["seg_mask"][:], A["dt_sb"][:], 0.0,
                          ALU.mult, ALU.add).then_inc(s_dve, 1)
    for c in range(NCH):
        ve.tensor_copy(A["lasts"][:, c:c + 1],
                       A["cum_fm"][:, c * CS + CS - 1: c * CS + CS]).then_inc(s_dve, 1)
    ve.wait(s_pool, p0_pool)
    ve.tensor_tensor_scan(A["decays"][:], A["ones8"][:], A["lasts"][:], 0.0,
                          ALU.mult, ALU.add).then_inc(s_dve, 1)
    ve.tensor_sub(A["decays"][:], A["decays"][:], A["lasts"][:]).then_inc(s_dve, 1)
    p2_cum = s_dve.n
    v_tm = [A["val_tm0"], A["val_tm1"]]
    b_tm = [A["B_tm0"], A["B_tm1"]]
    yd_st = [A["yd_st0"], A["yd_st0"]]
    ce_st = [A["ce_st0"], A["ce_st0"]]
    spl_v3 = A["sp_val"].rearrange("(i p) t -> p i t", p=128)
    spl_b3 = A["sp_B"].rearrange("(i p) t -> p i t", p=128)
    spl_c3 = A["sp_C"].rearrange("(i p) t -> p i t", p=128)
    vfm3 = A["val_fm"][:].rearrange("p (i t) -> p i t", i=32)
    bfm3 = A["B_fm"][:].rearrange("p (i t) -> p i t", i=32)
    cfm3 = A["C_fm"][:].rearrange("p (i t) -> p i t", i=32)
    yd_free = [0, 0]
    ce_free = [0, 0]

    for c in range(NCH):
        sy.wait(s_sp, p1_spills)
        sy.wait(s_pe, max(P.g("p2_fm_pe"), p1_pe))
        sy.wait(s_dve, P.g("p2_fm_dve"))
        sy.wait(s_pool, P.g("p2_fm_pool"))
        for q in range(4):
            sl = slice(q * 8, (q + 1) * 8)
            sy.dma_start(out=vfm3[:, sl, :], in_=spl_v3[:, sl, c * CS:(c + 1) * CS]).then_inc(s_chl, 16)
            sy.dma_start(out=bfm3[:, sl, :], in_=spl_b3[:, sl, c * CS:(c + 1) * CS]).then_inc(s_chl, 16)
            sy.dma_start(out=cfm3[:, sl, :], in_=spl_c3[:, sl, c * CS:(c + 1) * CS]).then_inc(s_chl, 16)
        for mh in range(2):
            col = c * CS + mh * 128
            sy.dma_start(out=v_tm[mh][:], in_=A["sp_val"][:, col:col + 128], transpose=True).then_inc(s_chl, 16)
            sy.dma_start(out=b_tm[mh][:], in_=A["sp_B"][:, col:col + 128], transpose=True).then_inc(s_chl, 16)
        ld_chunk = s_chl.n

        cum_c = A["cum_fm"][:, c * CS:(c + 1) * CS]
        te.wait(s_dve, p2_cum)
        for mh in range(2):
            P.bank_w(te, 6)
            te.transpose(B[6][0:128, 0:32], cum_c[:, mh * 128:(mh + 1) * 128],
                         A["ident"][0:32, 0:32]).then_inc(s_pe, 1)
            P.bank_set_w(6, s_pe)
            ve.wait(s_pe, s_pe.n)
            P.bank_r(ve, 6)
            ve.wait(s_act, P.g("p2_acts_prev"))
            ve.tensor_copy(A["cum_tm"][:, mh * 32:(mh + 1) * 32], B[6][0:128, 0:32]).then_inc(s_dve, 1)
            P.bank_add_r(6, s_dve)
        ve.tensor_scalar_mul(A["neg_cum"][:], A["cum_tm"][:], -1.0).then_inc(s_dve, 1)
        ve.wait(s_pe, P.g("p2_diag_pe"))
        ve.tensor_scalar_mul(A["dtscr"][0:32, 0:32], A["ident"][0:32, 0:32],
                             A["lasts"][:, c:c + 1]).then_inc(s_dve, 1)
        diag_rdy = s_dve.n
        te.wait(s_dve, diag_rdy)
        P.bank_w(te, 7)
        te.matmul(B[7][:, 0:32], A["ones32"][:], A["dtscr"][0:32, 0:32],
                  start=True, stop=True).then_inc(s_pe, 1)
        P.s("p2_diag_pe", s_pe.n)
        P.bank_set_w(7, s_pe)
        ve.wait(s_pe, s_pe.n)
        P.bank_r(ve, 7)
        ve.tensor_copy(A["lastsbc"][:], B[7][:, 0:32]).then_inc(s_dve, 1)
        P.bank_add_r(7, s_dve)
        lbc_rdy = s_dve.n
        sc.wait(s_dve, lbc_rdy)
        sc.wait(s_dve, P.g("p2_h_valid"))
        sc.activation(A["el_bc"][:], A["lastsbc"][:], AF.Exp).then_inc(s_act, 1)
        el_rdy = s_act.n
        for mh in range(2):
            ve.tensor_sub(A["w_mh"][:, mh * 32:(mh + 1) * 32], A["lastsbc"][:],
                          A["cum_tm"][:, mh * 32:(mh + 1) * 32]).then_inc(s_dve, 1)
        w_sub = s_dve.n
        sc.wait(s_dve, w_sub)
        sc.activation(A["w_mh"][:], A["w_mh"][:], AF.Exp).then_inc(s_act, 1)
        w_rdy2 = s_act.n
        pre_dve = s_dve.n

        h_valid_chunk = P.g("p2_h_valid")
        pend = {}

        def s1(h):
            hp = h % 2
            d = pend.setdefault(h, {})
            te.wait(s_dve, pre_dve)
            te.wait(s_chl, ld_chunk)
            P.bank_w(te, hp)
            te.matmul(B[hp][:, 0:256],
                      A["sel32"][:, h * 128:(h + 1) * 128], cum_c,
                      start=True, stop=True).then_inc(s_pe, 1)
            d["bc"] = s_pe.n
            P.bank_set_w(hp, s_pe)
            P.bank_w(te, 2 + hp)
            for mh in range(2):
                mi = te.matmul(B[2 + hp][:, mh * 256:(mh + 1) * 256],
                               bfm3[:, h, mh * 128:(mh + 1) * 128],
                               cfm3[:, h, :], start=True, stop=True)
            mi.then_inc(s_pe, 1)
            d["cbt"] = s_pe.n
            P.bank_set_w(2 + hp, s_pe)

        def sdiff(h):
            hp = h % 2
            d = pend[h]
            pb = (h // 2) % 2
            ve.wait(s_pe, d["bc"])
            if hp == 0:
                ve.wait(s_act, P.g("p2_e_act_%d" % pb))
                ve.wait(s_dve, P.g("p2_mt_%d" % pb))
            P.bank_r(ve, hp)
            for mh in range(2):
                ve.scalar_tensor_tensor(
                    A["e0"][:, pb * 1024 + hp * 512 + mh * 256: pb * 1024 + hp * 512 + (mh + 1) * 256],
                    B[hp][:, 0:256],
                    A["neg_cum"][:, mh * 32 + h: mh * 32 + h + 1],
                    A["lt2"][:, mh * 256:(mh + 1) * 256],
                    ALU.add, ALU.add).then_inc(s_dve, 1)
            P.bank_add_r(hp, s_dve)
            d["diff"] = s_dve.n

        def s2pair(h):
            grp = h // 8
            gpar = grp % 2
            hr = h % 8
            pb = (h // 2) % 2
            d = pend[h]
            d1 = pend[h + 1]
            sc.wait(s_dve, d1["diff"])
            sc.activation(A["e0"][:, pb * 1024:(pb + 1) * 1024],
                          A["e0"][:, pb * 1024:(pb + 1) * 1024], AF.Exp).then_inc(s_act, 1)
            P.s("p2_e_act_%d" % pb, s_act.n)
            d["e"] = s_act.n
            sc.wait(s_pool, P.g("p2_expcum_pool_%d" % pb))
            for hh in range(2):
                sc.wait(s_pe, pend[h + hh]["bc"])
                P.bank_r(sc, hh)
                sc.activation(A["expcum"][:, pb * 512 + hh * 256: pb * 512 + (hh + 1) * 256],
                              B[hh][:, 0:256], AF.Exp).then_inc(s_act, 1)
                P.bank_add_r(hh, s_act)
            d["ec"] = s_act.n
            sc.wait(s_chl, ld_chunk)
            sc.wait(s_act, w_rdy2)
            for hh in range(2):
                hp = hh
                sc.wait(s_pe, P.g("p2_bw_pe_%d" % hp))
                for mh in range(2):
                    sc.activation(A["Bw"][:, hp * 256 + mh * 128: hp * 256 + (mh + 1) * 128],
                                  b_tm[mh][:, (h + hh) * 128:(h + hh + 1) * 128], AF.Copy,
                                  scale=A["w_mh"][:, mh * 32 + h + hh: mh * 32 + h + hh + 1]).then_inc(s_act, 1)
                pend[h + hh]["bw"] = s_act.n
            gp.wait(s_chl, ld_chunk)
            gp.wait(s_act, d["ec"])
            gp.wait(s_sp, s_sp.n)
            gp.wait(s_pe, P.g("p2_ce_pe_%d" % gpar))
            for hh in range(2):
                gp.tensor_mul(ce_st[gpar][:, (hr + hh) * 256:(hr + hh + 1) * 256],
                              cfm3[:, h + hh, :],
                              A["expcum"][:, pb * 512 + hh * 256: pb * 512 + (hh + 1) * 256]).then_inc(s_pool, 1)
            P.s("p2_expcum_pool_%d" % pb, s_pool.n)
            d["ce"] = s_pool.n
            d1["ce"] = s_pool.n
            for hh in range(2):
                hp = hh
                dd = pend[h + hh]
                ve.wait(s_pe, dd["cbt"])
                ve.wait(s_act, d["e"])
                ve.wait(s_pe, P.g("p2_y_pe_%d" % hp))
                P.bank_r(ve, 2 + hp)
                ve.tensor_mul(A["Mt0"][:, pb * 1024 + hp * 512: pb * 1024 + (hp + 1) * 512],
                              B[2 + hp][:, 0:512],
                              A["e0"][:, pb * 1024 + hp * 512: pb * 1024 + (hp + 1) * 512]).then_inc(s_dve, 1)
                P.bank_add_r(2 + hp, s_dve)
                dd["mt"] = s_dve.n
            P.s("p2_mt_%d" % pb, s_dve.n)

        def s3(h):
            hp = h % 2
            grp = h // 8
            gpar = grp % 2
            hr = h % 8
            d = pend[h]
            te.wait(s_dve, d["mt"])
            te.wait(s_pool, d["ce"])
            te.wait(s_dve, h_valid_chunk)
            pb = (h // 2) % 2
            P.bank_w(te, 4 + hp)
            te.matmul(B[4 + hp][:, 0:256], v_tm[0][:, h * 128:(h + 1) * 128],
                      A["Mt0"][:, pb * 1024 + hp * 512: pb * 1024 + hp * 512 + 256],
                      start=True, stop=False)
            te.matmul(B[4 + hp][:, 0:256], v_tm[1][:, h * 128:(h + 1) * 128],
                      A["Mt0"][:, pb * 1024 + hp * 512 + 256: pb * 1024 + hp * 512 + 512],
                      start=False, stop=False)
            te.matmul(B[4 + hp][:, 0:256], A["h_bf16"][:, h * 128:(h + 1) * 128],
                      ce_st[gpar][:, hr * 256:(hr + 1) * 256],
                      start=False, stop=True).then_inc(s_pe, 1)
            d["y"] = s_pe.n
            P.s("p2_y_pe_%d" % hp, s_pe.n)
            P.s("p2_ce_pe_%d" % gpar, s_pe.n)
            P.bank_set_w(4 + hp, s_pe)
            te.wait(s_act, d["bw"])
            P.bank_w(te, 6 + hp)
            te.matmul(B[6 + hp][:, 0:128], A["Bw"][:, hp * 256: hp * 256 + 128],
                      v_tm[0][:, h * 128:(h + 1) * 128], start=True, stop=False)
            te.matmul(B[6 + hp][:, 0:128], A["Bw"][:, hp * 256 + 128: hp * 256 + 256],
                      v_tm[1][:, h * 128:(h + 1) * 128], start=False, stop=True).then_inc(s_pe, 1)
            d["st"] = s_pe.n
            P.s("p2_bw_pe_%d" % hp, s_pe.n)
            P.bank_set_w(6 + hp, s_pe)

        def s4(h):
            hp = h % 2
            grp = h // 8
            gpar = grp % 2
            hr = h % 8
            d = pend[h]
            ve.wait(s_pe, d["y"])
            ve.wait(s_sp, s_sp.n)
            P.bank_r(ve, 4 + hp)
            ve.scalar_tensor_tensor(yd_st[gpar][:, hr * 256:(hr + 1) * 256],
                                    vfm3[:, h, :], A["d_param"][:, h:h + 1],
                                    B[4 + hp][:, 0:256], ALU.mult, ALU.add).then_inc(s_dve, 1)
            P.bank_add_r(4 + hp, s_dve)
            P.s("p2_fm_dve", s_dve.n)
            d["yd"] = s_dve.n
            ve.wait(s_pe, d["st"])
            ve.wait(s_act, el_rdy)
            P.bank_r(ve, 6 + hp)
            ve.scalar_tensor_tensor(A["h_f32"][:, h * 128:(h + 1) * 128],
                                    A["h_f32"][:, h * 128:(h + 1) * 128],
                                    A["el_bc"][:, h:h + 1],
                                    B[6 + hp][:, 0:128], ALU.mult, ALU.add).then_inc(s_dve, 1)
            P.bank_add_r(6 + hp, s_dve)
            ve.tensor_copy(A["h_bf16"][:, h * 128:(h + 1) * 128],
                           A["h_f32"][:, h * 128:(h + 1) * 128]).then_inc(s_dve, 1)
            P.s("p2_h_valid", s_dve.n)
            if h % 8 == 7:
                sy.wait(s_dve, d["yd"])
                sy.dma_start(out=A["sp_Yd"][c * 128:(c + 1) * 128, grp * 2048:(grp + 1) * 2048],
                             in_=yd_st[gpar][:]).then_inc(s_sp, 16)
                sy.wait(s_pool, d["ce"])
                sy.wait(s_pe, d["y"])
                sy.dma_start(out=A["sp_Cexp"][c * 128:(c + 1) * 128, grp * 2048:(grp + 1) * 2048],
                             in_=ce_st[gpar][:]).then_inc(s_sp, 16)
            del pend[h]

        s1(0)
        sdiff(0)
        s1(1)
        sdiff(1)
        for p in range(H // 2):
            h = 2 * p
            s2pair(h)
            if h + 2 < H:
                s1(h + 2)
                sdiff(h + 2)
                s1(h + 3)
                sdiff(h + 3)
            s3(h)
            s4(h)
            s3(h + 1)
            s4(h + 1)

        P.s("p2_fm_pe", s_pe.n)
        P.s("p2_fm_pool", s_pool.n)
        P.s("p2_acts_prev", s_act.n)
        P.s("p2_pool_prev", s_pool.n)
    p2_spills = s_sp.n
    p2_h = P.g("p2_h_valid")

    if PHASE_LIMIT < 3:
        sy.wait(s_sp, p2_spills)
        _dump_taps(P, A, S)
        return

    ag3 = A["ag_in"].rearrange("(h n) d -> n h d", n=128)
    h3 = A["h_f32"][:].rearrange("p (h d) -> p h d", h=H)
    gp.wait(s_dve, p2_h)
    for q in range(4):
        sl = slice(q * 8, (q + 1) * 8)
        gp.dma_start(out=ag3[:, sl, :], in_=h3[:, sl, :]).then_inc(s_w, 16)
    ag_in_done = s_w.n
    gp.wait(s_w, ag_in_done)
    gp.collective_compute("AllGather", ALU.bypass,
                          replica_groups=[[0, 1], [2, 3], [4, 5], [6, 7]],
                          ins=[A["ag_in"].opt()], outs=[A["ag_out"].opt()]).then_inc(s_cc, 1)
    ago3 = A["ag_out"][0:4096, :].rearrange("(h n) d -> n h d", n=128)
    sy.wait(s_cc, 1)
    for q in range(4):
        sl = slice(q * 8, (q + 1) * 8)
        sy.dma_start(out=h3[:, sl, :], in_=ago3[:, sl, :]).then_inc(s_ld, 16)
    h8_done = s_ld.n

    yd_l = A["val_fm"]
    ce_l = A["B_fm"]
    gt_l = A["C_fm"]
    gt3 = gt_l[:].rearrange("p (i t) -> p i t", i=32)
    spg3 = A["sp_gate"].rearrange("(i p) t -> p i t", p=128)
    yg_st = [A["yd_st0"], A["ce_st0"]]
    yg_free = [0, 0]
    spyg3 = A["sp_Yg"].rearrange("(i p) t -> p i t", p=128)
    for c in range(NCH):
        sy.wait(s_sp, p2_spills)
        sy.wait(s_pe, P.g("p25_ld_pe"))
        sy.wait(s_dve, P.g("p25_ld_dve"))
        sy.wait(s_pool, P.g("p25_ld_pool"))
        sy.dma_start(out=yd_l[:], in_=A["sp_Yd"][c * 128:(c + 1) * 128, :]).then_inc(s_chl, 16)
        sy.dma_start(out=ce_l[:], in_=A["sp_Cexp"][c * 128:(c + 1) * 128, :]).then_inc(s_chl, 16)
        for q in range(4):
            sl = slice(q * 8, (q + 1) * 8)
            sy.dma_start(out=gt3[:, sl, :], in_=spg3[:, sl, c * CS:(c + 1) * CS]).then_inc(s_chl, 16)
        ld_c = s_chl.n
        ve.wait(s_dve, p2_cum)
        ve.wait(s_ld, h8_done)
        ve.wait(s_pe, P.g("p2_diag_pe"))
        ve.tensor_scalar_mul(A["dtscr"][0:32, 0:32], A["ident"][0:32, 0:32],
                             A["decays"][:, c:c + 1]).then_inc(s_dve, 1)
        diag_rdy = s_dve.n
        te.wait(s_dve, diag_rdy)
        P.bank_w(te, 7)
        te.matmul(B[7][:, 0:32], A["ones32"][:], A["dtscr"][0:32, 0:32],
                  start=True, stop=True).then_inc(s_pe, 1)
        P.s("p2_diag_pe", s_pe.n)
        P.bank_set_w(7, s_pe)
        ve.wait(s_pe, s_pe.n)
        P.bank_r(ve, 7)
        ve.wait(s_act, P.g("p25_dbc_act"))
        ve.tensor_copy(A["lastsbc"][:], B[7][:, 0:32]).then_inc(s_dve, 1)
        P.bank_add_r(7, s_dve)
        sc.wait(s_dve, s_dve.n)
        sc.wait(s_dve, P.g("p25_cl_last"))
        sc.activation(A["D_bc"][:], A["lastsbc"][:], AF.Exp).then_inc(s_act, 1)
        P.s("p25_dbc_act", s_act.n)
        dbc_rdy = s_act.n
        def cl(h):
            hp = h % 4
            ve.wait(s_act, dbc_rdy)
            ve.wait(s_ld, h8_done)
            ve.wait(s_pe, P.g("p25_cl_pe_%d" % hp))
            ve.tensor_scalar(A["clbf"][:, hp * 128:(hp + 1) * 128],
                             A["h_f32"][:, h * 128:(h + 1) * 128],
                             A["flag"][:, 0:1], A["D_bc"][:, h:h + 1],
                             ALU.mult, ALU.mult).then_inc(s_dve, 1)
            P.s("p25_cl_last", s_dve.n)
            P.s("p25_cl_rdy_%d" % hp, s_dve.n)

        def corr(h):
            hp = h % 4
            te.wait(s_dve, P.g("p25_cl_rdy_%d" % hp))
            te.wait(s_chl, ld_c)
            P.bank_w(te, hp)
            te.matmul(B[hp][:, 0:256], A["clbf"][:, hp * 128:(hp + 1) * 128],
                      ce_l[:, h * 256:(h + 1) * 256], start=True, stop=True).then_inc(s_pe, 1)
            P.s("p25_ld_pe", s_pe.n)
            P.s("p25_cl_pe_%d" % hp, s_pe.n)
            P.s("p25_corr_%d" % hp, s_pe.n)
            P.bank_set_w(hp, s_pe)

        def finish(h):
            hp = h % 4
            grp = h // 8
            gpar = grp % 2
            ve.wait(s_pe, P.g("p25_corr_%d" % hp))
            ve.wait(s_pool, P.g("p25_ct_pool_%d" % hp))
            P.bank_r(ve, hp)
            ve.tensor_add(A["ctmp"][:, hp * 256:(hp + 1) * 256],
                          yd_l[:, h * 256:(h + 1) * 256], B[hp][:, 0:256]).then_inc(s_dve, 1)
            P.bank_add_r(hp, s_dve)
            P.s("p25_ld_dve", s_dve.n)
            t_rdy = s_dve.n
            gp.wait(s_dve, t_rdy)
            gp.wait(s_sp, s_sp.n)
            gp.tensor_mul(yg_st[gpar][:, (h % 8) * 256:(h % 8 + 1) * 256],
                          A["ctmp"][:, hp * 256:(hp + 1) * 256],
                          gt3[:, h, :]).then_inc(s_pool, 1)
            P.s("p25_ct_pool_%d" % hp, s_pool.n)
            P.s("p25_ld_pool", s_pool.n)
            yg_rdy = s_pool.n
            if h % 8 == 7:
                sy.wait(s_pool, yg_rdy)
                src = yg_st[gpar][:].rearrange("p (i t) -> p i t", i=8)
                sy.dma_start(out=spyg3[:, grp * 8:(grp + 1) * 8, c * CS:(c + 1) * CS],
                             in_=src).then_inc(s_sp, 16)

        for h0 in range(4):
            cl(h0)
        for h in range(H):
            corr(h)
            finish(h)
            if h + 4 < H:
                cl(h + 4)
    p25_spills = s_sp.n

    if PHASE_LIMIT < 4:
        sy.wait(s_sp, p25_spills)
        _dump_taps(P, A, S)
        return

    sy.wait(s_sp, p25_spills)
    sy.wait(s_pool, s_pool.n)
    sy.wait(s_dve, s_dve.n)
    sy.wait(s_pe, s_pe.n)
    sy.wait(S["swc"], P.g("sw1b_rdy"))
    s_wh = [S["wh0"], S["wh1"]]
    for k in range(32):
        sy.dma_start(out=A["wot"][:, k * 2048:(k + 1) * 2048],
                     in_=A["d_woutb"][k * 128:(k + 1) * 128, :]).then_inc(s_wh[0], 16)
    wot_rdy = s_wh[0].n
    ygb = [A["yg_t"], A["yg_t2"]]
    for tt in range(16):
        par = tt % 2
        sy.wait(s_sp, p25_spills)
        sy.wait(s_pe, P.g("p3_mm_%d" % par))
        ygv = ygb[par][:].rearrange("p (i t) -> p i t", i=32)
        for q in range(4):
            sl = slice(q * 8, (q + 1) * 8)
            sy.dma_start(out=ygv[:, sl, :], in_=spyg3[:, sl, tt * 128:(tt + 1) * 128]).then_inc(s_chl, 16)
        sy.wait(s_dve, P.g("p3_x1_rdy"))
        sy.dma_start(out=A["xtm_b"][:], in_=A["in_x_tm"][tt * 128:(tt + 1) * 128, :]).then_inc(s_chl, 16)
        yg_rdy2 = s_chl.n
        te.wait(s_chl, yg_rdy2)
        te.wait(s_wh[0], wot_rdy)
        mmd = [0] * 4
        for dmg in range(4):
            P.bank_w(te, dmg)
            for k in range(32):
                mi = te.matmul(B[dmg][:], ygv[:, k, :],
                               A["wot"][:, k * 2048 + dmg * 512: k * 2048 + (dmg + 1) * 512],
                               start=(k == 0), stop=(k == 31))
            mi.then_inc(s_pe, 1)
            mmd[dmg] = s_pe.n
            P.bank_set_w(dmg, s_pe)
        P.s("p3_mm_%d" % par, s_pe.n)
        for dmg in range(4):
            ve.wait(s_pe, mmd[dmg])
            P.bank_r(ve, dmg)
            ve.scalar_tensor_tensor(A["xtm_b"][:, dmg * 512:(dmg + 1) * 512],
                                    B[dmg][:], 1.0,
                                    A["xtm_b"][:, dmg * 512:(dmg + 1) * 512],
                                    ALU.mult, ALU.add).then_inc(s_dve, 1)
            P.bank_add_r(dmg, s_dve)
        ve.reduce_sum(A["ln_stats"][:, 0:1], A["xtm_b"][:], axis=AX.X).then_inc(s_dve, 1)
        x2_rdy = s_dve.n
        sc.wait(s_dve, x2_rdy)
        sc.activation(A["sq_b"][:], A["xtm_b"][:], AF.Square,
                      accum_out=A["ln_stats"][:, 1:2]).then_inc(s_act, 1)
        sq_rdy = s_act.n
        ve.wait(s_act, sq_rdy)
        ve.tensor_scalar_mul(A["ln_stats"][:, 2:3], A["ln_stats"][:, 0:1], 1.0 / DM).then_inc(s_dve, 1)
        ve.tensor_scalar_mul(A["ln_stats"][:, 3:4], A["ln_stats"][:, 1:2], 1.0 / DM).then_inc(s_dve, 1)
        ve.tensor_mul(A["ln_stats"][:, 4:5], A["ln_stats"][:, 2:3], A["ln_stats"][:, 2:3]).then_inc(s_dve, 1)
        ve.tensor_sub(A["ln_stats"][:, 4:5], A["ln_stats"][:, 3:4], A["ln_stats"][:, 4:5]).then_inc(s_dve, 1)
        ve.tensor_scalar_add(A["ln_stats"][:, 4:5], A["ln_stats"][:, 4:5], EPS).then_inc(s_dve, 1)
        var_rdy = s_dve.n
        sc.wait(s_dve, var_rdy)
        sc.activation(A["ln_stats"][:, 5:6], A["ln_stats"][:, 4:5], AF.Ln).then_inc(s_act, 1)
        ln_rdy = s_act.n
        ve.wait(s_act, ln_rdy)
        ve.tensor_scalar_mul(A["ln_stats"][:, 5:6], A["ln_stats"][:, 5:6], -0.5).then_inc(s_dve, 1)
        nl_rdy = s_dve.n
        sc.wait(s_dve, nl_rdy)
        sc.activation(A["ln_stats"][:, 6:7], A["ln_stats"][:, 5:6], AF.Exp).then_inc(s_act, 1)
        rstd_rdy = s_act.n
        ve.wait(s_act, rstd_rdy)
        ve.wait(s_sp, s_sp.n)
        ve.wait(s_spg, s_spg.n)
        ve.tensor_scalar(A["x1stage"][:], A["xtm_b"][:], A["ln_stats"][:, 2:3],
                         A["ln_stats"][:, 6:7], ALU.subtract, ALU.mult).then_inc(s_dve, 1)
        x1_rdy = s_dve.n
        P.s("p3_x1_rdy", x1_rdy)
        sy.wait(s_dve, x1_rdy)
        sy.dma_start(out=A["sp_x1f"][tt * 128:(tt + 1) * 128, :], in_=A["x1stage"][:]).then_inc(s_sp, 16)
        gp.wait(s_dve, x1_rdy)
        gp.dma_start(out=A["sp_x1bf"][tt * 128:(tt + 1) * 128, :], in_=A["x1stage"][:]).then_inc(s_spg, 16)
        P.s("p3_x1_sp", s_sp.n)
        P.s("p3_x1_spg", s_spg.n)
    p3_done = s_sp.n
    p3_done_g = s_spg.n

    if PHASE_LIMIT < 5:
        sy.wait(s_sp, p3_done)
        _dump_taps(P, A, S)
        return

    for i in range(16):
        sy.wait(s_sp, p3_done)
        sy.wait(s_spg, p3_done_g)
        sy.dma_start(out=A["x1fm"][:, i * 2048:(i + 1) * 2048],
                     in_=A["sp_x1bf"][:, i * 128:(i + 1) * 128], transpose=True).then_inc(s_chl, 16)
    x1fm_rdy = s_chl.n
    gp.dma_start(out=A["gwbf"][:].rearrange("p (k e) -> p k e", k=16),
                 in_=A["in_gate_w"].rearrange("(k p) e -> p k e", p=128)).then_inc(s_w, 16)
    gw_rdy = s_w.n
    x1v = A["x1fm"][:].rearrange("p (k t) -> p k t", k=16)
    gwv = A["gwbf"][:].rearrange("p (k e) -> p k e", k=16)
    g = A["gtmp"]
    g2 = A["gtmp2"]
    for tt in range(16):
        te.wait(s_chl, x1fm_rdy)
        te.wait(s_w, gw_rdy)
        P.bank_w(te, 2)
        for k in range(16):
            mi = te.matmul(B[2][:, 0:8], x1v[:, k, tt * 128:(tt + 1) * 128], gwv[:, k, :],
                           start=(k == 0), stop=(k == 15))
        mi.then_inc(s_pe, 1)
        lg_done = s_pe.n
        P.bank_set_w(2, s_pe)
        ve.wait(s_pe, lg_done)
        ve.wait(s_act, P.g("p4_g_act"))
        P.bank_r(ve, 2)
        ve.tensor_copy(g[:, 0:8], B[2][:, 0:8]).then_inc(s_dve, 1)
        P.bank_add_r(2, s_dve)
        ve.reduce_max(g[:, 8:9], g[:, 0:8], axis=AX.X).then_inc(s_dve, 1)
        ve.tensor_scalar(g2[:, 0:8], g[:, 0:8], g[:, 8:9], None, ALU.is_ge).then_inc(s_dve, 1)
        ve.scalar_tensor_tensor(g2[:, 8:16], g2[:, 0:8], -1e30, g[:, 0:8],
                                ALU.mult, ALU.add).then_inc(s_dve, 1)
        ve.reduce_max(g[:, 9:10], g2[:, 8:16], axis=AX.X).then_inc(s_dve, 1)
        ve.tensor_scalar_mul(g[:, 10:11], g[:, 8:9], -1.0).then_inc(s_dve, 1)
        ve.tensor_sub(g[:, 11:12], g[:, 9:10], g[:, 8:9]).then_inc(s_dve, 1)
        pre_e = s_dve.n
        sc.wait(s_dve, pre_e)
        sc.activation(g2[:, 0:8], g[:, 0:8], AF.Exp, bias=g[:, 10:11]).then_inc(s_act, 1)
        sc.activation(g[:, 12:13], g[:, 11:12], AF.Exp).then_inc(s_act, 1)
        P.s("p4_g_act", s_act.n)
        e_done2 = s_act.n
        ve.wait(s_act, e_done2)
        ve.tensor_scalar_add(g[:, 12:13], g[:, 12:13], 1.0).then_inc(s_dve, 1)
        ve.reciprocal(g[:, 13:14], g[:, 12:13]).then_inc(s_dve, 1)
        ve.tensor_scalar(g2[:, 8:16], g[:, 0:8], g[:, 9:10], None, ALU.is_ge).then_inc(s_dve, 1)
        ve.tensor_scalar(g2[:, 0:8], g2[:, 0:8], g[:, 13:14], None, ALU.mult).then_inc(s_dve, 1)
        ve.tensor_mul(g[:, 0:8], g2[:, 0:8], g2[:, 8:16]).then_inc(s_dve, 1)
        gates_rdy = s_dve.n
        te.wait(s_dve, gates_rdy)
        P.bank_w(te, 3)
        te.transpose(B[3][0:8, 0:128], g[:, 0:8], A["ident"][:]).then_inc(s_pe, 1)
        gtr_done = s_pe.n
        P.bank_set_w(3, s_pe)
        ve.wait(s_pe, gtr_done)
        P.bank_r(ve, 3)
        ve.tensor_copy(A["g_rows"][:, tt * 128:(tt + 1) * 128], B[3][0:8, 0:128]).then_inc(s_dve, 1)
        P.bank_add_r(3, s_dve)
    grows_rdy = s_dve.n

    sy.dma_start(out=A["d_gidx"], in_=A["in_gidx_init"]).then_inc(s_gx, 16)
    for nm in ["l8u", "srow16", "iota_t", "iota_w"]:
        sy.dma_start(out=A[nm][:], in_=A["in_" + nm]).then_inc(s_gx, 16)
    rt_consts = s_gx.n
    gp.memset(A["ones1f"][:], 1.0).then_inc(s_pool, 1)
    ones1f_rdy = s_pool.n

    rr = A["rts"]
    ve.wait(s_dve, grows_rdy)
    ve.tensor_scalar(A["ind8"][:], A["g_rows"][:], 0.0, None, ALU.is_gt).then_inc(s_dve, 1)
    ve.memset(A["ones8r"][:], 1.0).then_inc(s_dve, 1)
    ve.tensor_tensor_scan(A["pos8"][:], A["ones8r"][:], A["ind8"][:], 0.0,
                          ALU.mult, ALU.add).then_inc(s_dve, 1)
    ve.tensor_copy(rr[:, 0:1], A["pos8"][:, 2047:2048]).then_inc(s_dve, 1)
    ve.wait(s_gx, rt_consts)
    ve.tensor_scalar(A["cmp16"][:], A["srow16"][:], rr[:, 0:1], None, ALU.is_lt).then_inc(s_dve, 1)
    ve.tensor_reduce(rr[:, 1:2], A["cmp16"][:], AX.X, ALU.add).then_inc(s_dve, 1)
    ve.tensor_scalar_mul(rr[:, 2:3], rr[:, 1:2], 512.0).then_inc(s_dve, 1)
    c128_rdy = s_dve.n
    te.wait(s_dve, c128_rdy)
    te.wait(s_gx, rt_consts)
    P.bank_w(te, 6)
    te.matmul(B[6][0:8, 0:1], A["l8u"][:], rr[:, 2:3], start=True, stop=True).then_inc(s_pe, 1)
    P.bank_set_w(6, s_pe)
    ve.wait(s_pe, s_pe.n)
    P.bank_r(ve, 6)
    ve.tensor_copy(rr[:, 3:4], B[6][0:8, 0:1]).then_inc(s_dve, 1)
    P.bank_add_r(6, s_dve)
    ve.tensor_add(rr[:, 4:5], rr[:, 3:4], rr[:, 2:3]).then_inc(s_dve, 1)
    ve.tensor_scalar(A["pos8"][:], A["pos8"][:], rr[:, 3:4], -1.0, ALU.add, ALU.add).then_inc(s_dve, 1)
    ve.tensor_scalar(A["ind8"][:], A["ind8"][:], -1e9, 1e9, ALU.mult, ALU.add).then_inc(s_dve, 1)
    ve.tensor_add(A["pos8"][:], A["pos8"][:], A["ind8"][:]).then_inc(s_dve, 1)
    ve.tensor_scalar(A["cmp16"][:], A["srow16"][:], rr[:, 4:5], None, ALU.is_ge).then_inc(s_dve, 1)
    slot_rdy = s_dve.n
    te.wait(s_dve, slot_rdy)
    P.bank_w(te, 6)
    te.matmul(B[6][0:1, 0:16], A["ones32"][0:8, 0:1], A["cmp16"][:],
              start=True, stop=True).then_inc(s_pe, 1)
    P.bank_set_w(6, s_pe)
    ve.wait(s_pe, s_pe.n)
    P.bank_r(ve, 6)
    ve.tensor_copy(A["egrow"][:], B[6][0:1, 0:16]).then_inc(s_dve, 1)
    P.bank_add_r(6, s_dve)
    ve.tensor_scalar_mul(A["wbrow"][:], A["egrow"][:], 128.0).then_inc(s_dve, 1)
    et_rdy = s_dve.n
    te.wait(s_dve, et_rdy)
    te.wait(s_pool, ones1f_rdy)
    P.bank_w(te, 7)
    te.matmul(B[7][:, 0:16], A["ones1f"][:], A["wbrow"][:], start=True, stop=True).then_inc(s_pe, 1)
    P.bank_set_w(7, s_pe)
    ve.wait(s_pe, s_pe.n)
    P.bank_r(ve, 7)
    ve.tensor_copy(A["wb_bc"][:], B[7][:, 0:16]).then_inc(s_dve, 1)
    P.bank_add_r(7, s_dve)
    wbbc_rdy = s_dve.n

    slot_tm = [A["slot_tm0"], A["slot_tm1"]]
    gate_tm = [A["gate_tm0"], A["gate_tm1"]]
    stage2 = [A["stage2a"], A["stage2b"], A["stage2c"], A["stage2d"]]
    s_scat = [s_ig, S["e8a"]]
    g1t = A["gtmp"]
    for tt in range(16):
        par = tt % 2
        te.wait(s_dve, slot_rdy)
        te.wait(s_dve, P.g("p4_rt_ve_%d" % par))
        P.bank_w(te, 6)
        te.transpose(B[6][0:128, 0:8], A["pos8"][:, tt * 128:(tt + 1) * 128],
                     A["ident"][0:8, 0:8]).then_inc(s_pe, 1)
        P.bank_set_w(6, s_pe)
        tr1 = s_pe.n
        P.bank_w(te, 7)
        te.transpose(B[7][0:128, 0:8], A["g_rows"][:, tt * 128:(tt + 1) * 128],
                     A["ident"][0:8, 0:8]).then_inc(s_pe, 1)
        P.bank_set_w(7, s_pe)
        tr2 = s_pe.n
        ve.wait(s_pe, tr2)
        P.bank_r(ve, 6)
        ve.tensor_copy(slot_tm[par][:], B[6][0:128, 0:8]).then_inc(s_dve, 1)
        P.bank_add_r(6, s_dve)
        P.bank_r(ve, 7)
        ve.tensor_copy(gate_tm[par][:], B[7][0:128, 0:8]).then_inc(s_dve, 1)
        P.bank_add_r(7, s_dve)
        g2t = A["gtmp2"]
        ve.tensor_reduce(g1t[:, 0:1], slot_tm[par][:], AX.X, ALU.min).then_inc(s_dve, 1)
        ve.tensor_scalar(g1t[:, 2:10], slot_tm[par][:], g1t[:, 0:1], None, ALU.is_le).then_inc(s_dve, 1)
        ve.scalar_tensor_tensor(g2t[:, 0:8], g1t[:, 2:10], 1e9, slot_tm[par][:],
                                ALU.mult, ALU.add).then_inc(s_dve, 1)
        ve.tensor_reduce(g1t[:, 1:2], g2t[:, 0:8], AX.X, ALU.min).then_inc(s_dve, 1)
        ve.tensor_copy(A["s1all"][:, tt:tt + 1], g1t[:, 0:1]).then_inc(s_dve, 1)
        ve.tensor_copy(A["s2all"][:, tt:tt + 1], g1t[:, 1:2]).then_inc(s_dve, 1)
        ve.tensor_scalar(g2t[:, 8:16], slot_tm[par][:], g1t[:, 1:2], None, ALU.is_le).then_inc(s_dve, 1)
        ve.tensor_sub(g2t[:, 8:16], g2t[:, 8:16], g1t[:, 2:10]).then_inc(s_dve, 1)
        ve.tensor_mul(g1t[:, 2:10], g1t[:, 2:10], gate_tm[par][:]).then_inc(s_dve, 1)
        ve.tensor_mul(g2t[:, 8:16], g2t[:, 8:16], gate_tm[par][:]).then_inc(s_dve, 1)
        scq = tt % 2
        ic0 = 2 * scq
        ve.wait(s_scat[scq], P.g("p4_scat_%d" % scq))
        ve.wait(s_gx, rt_consts)
        ve.tensor_copy(A["islot"][:, ic0:ic0 + 1], g1t[:, 0:1]).then_inc(s_dve, 1)
        ve.tensor_copy(A["islot"][:, ic0 + 1:ic0 + 2], g1t[:, 1:2]).then_inc(s_dve, 1)
        ve.tensor_copy(stage2[2 * scq][:, 0:1], A["iota_t"][:, tt:tt + 1]).then_inc(s_dve, 1)
        ve.tensor_reduce(stage2[2 * scq][:, 1:2], g1t[:, 2:10], AX.X, ALU.add).then_inc(s_dve, 1)
        ve.tensor_copy(stage2[2 * scq + 1][:, 0:1], A["iota_t"][:, tt:tt + 1]).then_inc(s_dve, 1)
        ve.tensor_reduce(stage2[2 * scq + 1][:, 1:2], g2t[:, 8:16], AX.X, ALU.add).then_inc(s_dve, 1)
        st_rdy = s_dve.n
        gp.wait(s_dve, st_rdy)
        gp.wait(s_gx, rt_consts)
        if tt == 0:
            gp.reg_mov(A["reg_bchk"], SLOTS - 1)
        for j in range(2):
            gp.indirect_dma_start(
                out=A["d_gidx"],
                out_offset=bass.IndirectOffsetOnAxis(A["islot"][:, ic0 + j:ic0 + j + 1], 0),
                in_=stage2[2 * scq + j][:], in_offset=None,
                bounds_check=A["reg_bchk"], oob_is_err=False).then_inc(s_scat[scq], 16)
        P.s("p4_scat_%d" % scq, s_scat[scq].n)
    scat_done = [s_scat[0].n, s_scat[1].n]

    s_gst = [S["gst"], S["gse"]]
    w1t = [A["w1t0"], A["w1t1"]]
    ghst = [A["ghe_st0"], A["ghe_st1"]]
    w1_free = [0, 0]
    for rs in range(16):
        par = rs % 2
        sy.wait(s_pe, w1_free[par])
        if rs == 0:
            sy.wait(S["swc"], P.g("sw1b_rdy"))
        sy.dma_start(out=w1t[par][:], in_=A["d_sw1b"][rs * 128:(rs + 1) * 128, :]).then_inc(s_wh[par], 16)
        w1_rdy = s_wh[par].n
        sc.wait(s_gst[par], P.g("p4_ghst_%d" % par))
        for tg in range(4):
            bk = tg % 2
            te.wait(s_wh[par], w1_rdy)
            te.wait(s_chl, x1fm_rdy)
            P.bank_w(te, bk)
            for k in range(16):
                mi = te.matmul(B[bk][:], w1t[par][:, k * 128:(k + 1) * 128],
                               x1v[:, k, tg * 512:(tg + 1) * 512], start=(k == 0), stop=(k == 15))
            mi.then_inc(s_pe, 1)
            h_done = s_pe.n
            P.bank_set_w(bk, s_pe)
            if tg == 3:
                w1_free[par] = s_pe.n
            sc.wait(s_pe, h_done)
            P.bank_r(sc, bk)
            sc.activation(ghst[par][:, tg * 512:(tg + 1) * 512], B[bk][:], AF.Gelu,
                          bias=A["sb1"][:, rs:rs + 1]).then_inc(s_act, 1)
            P.bank_add_r(bk, s_act)
        sc.wait(s_act, s_act.n)
        sc.dma_start(out=A["sp_ghe"][rs * 128:(rs + 1) * 128, :], in_=ghst[par][:]).then_inc(s_gst[par], 16)
        P.s("p4_ghst_%d" % par, s_gst[par].n)
    shared_done = [s_gst[0].n, s_gst[1].n]
    P.s("p4_wc_pe_0", s_pe.n)
    P.s("p4_wc_pe_1", s_pe.n)

    gp.wait(s_ig, scat_done[0])
    gp.wait(S["e8a"], scat_done[1])
    gp.dma_start(out=A["gixall"][:].rearrange("p (s c) -> p s c", c=2),
                 in_=bass.AP(A["d_gidx"].tensor, 0,
                             [[2, 128], [256, NST], [1, 2]])).then_inc(s_ig, 16)
    gix_rdy = s_ig.n
    ve.wait(s_ig, gix_rdy)
    ve.tensor_copy(A["giiall"][:], A["gixall"][:]).then_inc(s_dve, 1)
    ve.tensor_copy(A["gvall"][:].rearrange("p (s o) -> p s o", o=1),
                   A["gixall"][:].rearrange("p (s c) -> p s c", c=2)[:, :, 1:2]).then_inc(s_dve, 1)
    cv_rdy = s_dve.n
    P.s("p4_cv", cv_rdy)
    gp.wait(s_dve, cv_rdy)
    gp.reg_mov(A["reg_bchk"], T - 1)
    slot = [s_ig, S["e8a"], S["e8b"], S["b8b"],
            S["g4"], S["g5"], S["g6"], S["g7"]]
    ssp = [S["gt%d" % i] for i in range(NSPL)]
    assert NSPL == 10
    xg_sb = [A["xg_sb%d" % i] for i in range(8)]
    g_cnt = {}
    st_cnt = {}
    LAG = 7
    for it in range(NST + LAG):
        si = it
        if si < NST:
            q = si % 8
            if si >= 8:
                gp.wait(ssp[(si - 8) // TPS], st_cnt[si - 8])
            gp.indirect_dma_start(
                out=xg_sb[q][:], out_offset=None,
                in_=A["sp_x1bf"],
                in_offset=bass.IndirectOffsetOnAxis(A["giiall"][:, 2 * si:2 * si + 1], 0),
                bounds_check=A["reg_bchk"], oob_is_err=False).then_inc(slot[q], 16)
            g_cnt[si] = slot[q].n
        sj = it - LAG
        if 0 <= sj < NST:
            q = sj % 8
            gp.wait(slot[q], g_cnt[sj])
            gp.dma_start(out=A["d_xg"][sj * 128:(sj + 1) * 128, :],
                         in_=xg_sb[q][:]).then_inc(ssp[sj // TPS], 16)
            st_cnt[sj] = ssp[sj // TPS].n
    split_cnt = [st_cnt[s5 * TPS + TPS - 1] for s5 in range(NSPL)]
    xtb = [A["xts0a"], A["xts0b"], A["xts0c"], A["xts0d"]]
    s_xt4 = [S["xts"], S["xto"], S["xss0"], S["xss1"]]
    s_ld4 = [s_gx, S["gxo"], S["gxs0"], S["gxs1"]]
    qeng = [sy, sc]
    ld_cnt = {}
    xtr_done = [None] * NSPL

    def _xt_store(j):
        qj = j % 2
        sl = qj * 2 + (j // 2) % 2
        qeng[qj].wait(s_ld4[sl], ld_cnt[j])
        s5j, ij = j // 16, j % 16
        qeng[qj].dma_start(
            out=A["d_xgT"][ij * 128:(ij + 1) * 128, s5j * SSL:(s5j + 1) * SSL],
            in_=xtb[sl][:, 0:SSL]).then_inc(s_xt4[sl], 16)
        P.s("p4_xtb_%d" % sl, s_xt4[sl].n)
        if ij == 15:
            xtr_done[s5j] = tuple(x.n for x in s_xt4)

    for idx in range(NSPL * 16):
        s5, i = idx // 16, idx % 16
        q = idx % 2
        sl = q * 2 + (idx // 2) % 2
        qeng[q].wait(ssp[s5], split_cnt[s5])
        qeng[q].wait(s_xt4[sl], P.g("p4_xtb_%d" % sl))
        qeng[q].dma_start(out=xtb[sl][:, 0:SSL],
                          in_=A["d_xg"][s5 * SSL:(s5 + 1) * SSL, i * 128:(i + 1) * 128],
                          transpose=True).then_inc(s_ld4[sl], 16)
        ld_cnt[idx] = s_ld4[sl].n
        if idx >= 2:
            _xt_store(idx - 2)
    _xt_store(NSPL * 16 - 2)
    _xt_store(NSPL * 16 - 1)

    w1c = [[A["w1cA0"], A["w1cA1"]], [A["w1cB0"], A["w1cB1"]]]
    w2c = [[A["w2cA0"], A["w2cA1"]], [A["w2cB0"], A["w2cB1"]]]
    h_fm = [A["h_fm0"], A["h_fm1"]]
    osb = [A["osb0"], A["osb1"]]
    wii = [A["wii0"], A["wii1"]]
    s_m8 = [S["m8a"], S["m8b"]]
    s_os = [S["os0"], S["os1"]]
    s_xq = [s_gh[0], S["xq1"], S["xq2"], S["xq3"]]
    xv3 = A["xgg"][:].rearrange("p (k t) -> p k t", k=16)
    dxt3 = A["d_xgT"].rearrange("(k p) t -> p k t", p=128)
    for si in range(NST + 1):
      if si < NST:
        spar = si % 2
        grp, st = si // 4, si % 4
        gpar = grp % 2
        if st == 0:
            for _q in range(4):
                sy.wait(s_xt4[_q], xtr_done[(grp * 512 + 511) // SSL][_q])
            sy.wait(s_pe, P.g("p4_xgg_pe"))
            for q in range(4):
                sy.dma_start(out=xv3[:, q * 4:(q + 1) * 4, :],
                             in_=dxt3[:, q * 4:(q + 1) * 4, grp * 512:(grp + 1) * 512]).then_inc(s_xq[q], 16)
                P.s("p4_xgq_%d" % q, s_xq[q].n)
            ve.wait(s_dve, wbbc_rdy)
            ve.wait(s_m8[gpar], P.g("p4_m8b_%d" % gpar))
            ve.tensor_scalar_add(A["wif"][:, 0:1], A["iota_w"][:, 0:1],
                                 A["wb_bc"][:, grp:grp + 1]).then_inc(s_dve, 1)
            ve.tensor_scalar_mul(A["wif"][:, 1:2], A["wb_bc"][:, grp:grp + 1],
                                 1.0 / 128.0).then_inc(s_dve, 1)
            ve.tensor_copy(wii[gpar][:, 0:2], A["wif"][:, 0:2]).then_inc(s_dve, 1)
            wi_rdy = s_dve.n
            gp.wait(s_dve, wi_rdy)
            if si == 0:
                gp.wait(s_wc, P.g("wb_cast"))
                gp.wait(s_gst[0], shared_done[0])
                gp.wait(s_gst[1], shared_done[1])
                gp.reg_mov(A["reg_bchk"], 1023)
            gp.wait(s_pe, P.g("p4_wc_pe_%d" % gpar))
            for half in range(2):
                gp.indirect_dma_start(
                    out=bass.AP(w1c[half][gpar].tensor, 0, [[8192, 128], [1, 8192]]),
                    out_offset=None,
                    in_=A["d_w1p"], in_offset=bass.IndirectOffsetOnAxis(wii[gpar][:, 0:1], 0),
                    element_offset=half * 8192,
                    bounds_check=A["reg_bchk"], oob_is_err=False).then_inc(s_m8[gpar], 16)
                gp.indirect_dma_start(
                    out=bass.AP(w2c[half][gpar].tensor, 0, [[8192, 128], [1, 8192]]),
                    out_offset=None,
                    in_=A["d_w2p"], in_offset=bass.IndirectOffsetOnAxis(wii[gpar][:, 0:1], 0),
                    element_offset=half * 8192,
                    bounds_check=A["reg_bchk"], oob_is_err=False).then_inc(s_m8[gpar], 16)
            gp.wait(s_pe, P.g("p4_xgg_pe"))
            gp.indirect_dma_start(
                out=A["b1b"][:], out_offset=None,
                in_=A["d_b1b"], in_offset=bass.IndirectOffsetOnAxis(wii[gpar][:, 1:2], 0),
                bounds_check=A["reg_bchk"], oob_is_err=False).then_inc(s_m8[gpar], 16)
            P.s("p4_m8_%d" % gpar, s_m8[gpar].n)
        if st == 1:
            gp.wait(s_m8[gpar], P.g("p4_m8_%d" % gpar))
            gp.wait(s_pe, P.g("p4_w2pe_%d" % spar))
            gp.indirect_dma_start(
                out=A["b2b"][:], out_offset=None,
                in_=A["d_b2b"], in_offset=bass.IndirectOffsetOnAxis(wii[gpar][:, 1:2], 0),
                bounds_check=A["reg_bchk"], oob_is_err=False).then_inc(s_m8[gpar], 16)
            P.s("p4_m8b_%d" % gpar, s_m8[gpar].n)
        te.wait(s_m8[gpar], P.g("p4_m8_%d" % gpar))
        for bk in range(4, 8):
            P.bank_w(te, bk)
        for k in range(16):
            if k % 4 == 0:
                te.wait(s_xq[k // 4], P.g("p4_xgq_%d" % (k // 4)))
            for ht in range(8):
                half, hq = ht // 4, ht % 4
                bk = 4 + ht // 2
                c0 = (ht % 2) * 256
                te.matmul(B[bk][:, c0:c0 + 128],
                          w1c[half][gpar][:, hq * 2048 + k * 128: hq * 2048 + (k + 1) * 128],
                          xv3[:, k, st * 128:(st + 1) * 128],
                          start=(k == 0 and ht % 2 == 0), stop=False)
        for ht in range(8):
            bk = 4 + ht // 2
            c0 = (ht % 2) * 256
            mi = te.matmul(B[bk][:, c0:c0 + 128],
                           A["b1b"][0:1, ht * 128:(ht + 1) * 128],
                           A["ones1bf"][:], start=False, stop=(ht % 2 == 1))
        mi.then_inc(s_pe, 1)
        h_all = s_pe.n
        for bk in range(4, 8):
            P.bank_set_w(bk, s_pe)
        if st == 3:
            P.s("p4_xgg_pe", s_pe.n)
        sc.wait(s_pe, P.g("p4_w2pe_%d" % spar))
        sc.wait(s_pe, h_all)
        for ht in range(8):
            bk = 4 + ht // 2
            c0 = (ht % 2) * 256
            P.bank_r(sc, bk)
            sc.activation(h_fm[spar][:, ht * 128:(ht + 1) * 128], B[bk][:, c0:c0 + 128],
                          AF.Gelu).then_inc(s_act, 1)
            P.bank_add_r(bk, s_act)
        P.s("p4_gl_%d" % spar, s_act.n)
      sj = si - 1
      if 0 <= sj < NST:
        sparj = sj % 2
        grpj, stj = sj // 4, sj % 4
        gparj = grpj % 2
        te.wait(s_act, P.g("p4_gl_%d" % sparj))
        te.wait(s_m8[gparj], P.g("p4_m8b_%d" % gparj))
        for dmg in range(4):
            P.bank_w(te, dmg)
        for ht in range(8):
            half, hq = ht // 4, ht % 4
            for dmg in range(4):
                te.matmul(B[dmg][:], h_fm[sparj][:, ht * 128:(ht + 1) * 128],
                          w2c[half][gparj][:, hq * 2048 + dmg * 512: hq * 2048 + (dmg + 1) * 512],
                          start=(ht == 0), stop=False)
        for dmg in range(4):
            mi = te.matmul(B[dmg][:], A["ones1bf"][:],
                           A["b2b"][0:1, dmg * 512:(dmg + 1) * 512], start=False, stop=True)
            mi.then_inc(s_pe, 1)
            P.s("p4_o_done_%d" % dmg, s_pe.n)
            P.bank_set_w(dmg, s_pe)
        P.s("p4_w2pe_%d" % sparj, s_pe.n)
        if stj == 3:
            P.s("p4_wc_pe_%d" % gparj, s_pe.n)
        ve.wait(s_dve, P.g("p4_cv"))
        ve.wait(s_os[sparj], P.g("p4_ost_%d" % sparj))
        for dmg in range(4):
            ve.wait(s_pe, P.g("p4_o_done_%d" % dmg))
            P.bank_r(ve, dmg)
            ve.tensor_scalar_mul(osb[sparj][:, dmg * 512:(dmg + 1) * 512], B[dmg][:],
                                 A["gvall"][:, sj:sj + 1]).then_inc(s_dve, 1)
            P.bank_add_r(dmg, s_dve)
        o_rdy = s_dve.n
        sy.wait(s_dve, o_rdy)
        sy.dma_start(out=A["d_rout2"][sj * 128:(sj + 1) * 128, :],
                     in_=osb[sparj][:]).then_inc(s_os[sparj], 16)
        P.s("p4_ost_%d" % sparj, s_os[sparj].n)
    p4_done = [s_os[0].n, s_os[1].n]
    p4_pe = s_pe.n

    if PHASE_LIMIT < 6:
        sy.wait(s_os[0], p4_done[0])
        sy.wait(s_os[1], p4_done[1])
        _dump_taps(P, A, S)
        return

    w2k = [A["w2k0"], A["w2k1"]]
    ghk = [A["ghk0"], A["ghk1"]]
    w2_free = [0, 0]
    gh2_free = [0, 0]
    spghe3 = A["sp_ghe"].rearrange("(r p) t -> p r t", p=128)
    accv = A["out_acc"][:].rearrange("p (tt d) -> p tt d", tt=8)
    rgb = [A["r1sb"], A["r2sb"]]
    for th in range(2):
        sy.wait(s_dve, P.g("p5_acc_free"))
        sy.wait(s_os[0], p4_done[0])
        sy.wait(s_os[1], p4_done[1])
        for tt in range(8):
            row = th * 1024 + tt * 128
            sy.dma_start(out=accv[:, tt, :], in_=A["sp_x1f"][row:row + 128, :]).then_inc(s_chl, 16)
        acc_init = s_chl.n
        for kb in range(NKB):
            par = kb % 2
            sy.wait(s_pe, w2_free[par] if (th > 0 or kb >= 2) else p4_pe)
            for k in range(KB):
                row = kb * 1024 + k * 128
                sy.dma_start(out=w2k[par][:, k * 2048:(k + 1) * 2048],
                             in_=A["d_w2shb"][row:row + 128, :]).then_inc(s_wh[par], 16)
            w2_rdy = s_wh[par].n
            sy.wait(s_gst[0], shared_done[0])
            sy.wait(s_gst[1], shared_done[1])
            sy.wait(s_pe, gh2_free[par])
            sy.dma_start(out=ghk[par][:].rearrange("p (k t) -> p k t", k=KB),
                         in_=spghe3[:, kb * KB:(kb + 1) * KB, th * 1024:(th + 1) * 1024]).then_inc(s_gh[par], 16)
            gh_rdy = s_gh[par].n
            ghv = ghk[par][:].rearrange("p (k t) -> p k t", k=KB)
            for tt in range(8):
                for dmg in range(4):
                    bk = (tt * 4 + dmg) % 2
                    te.wait(s_wh[par], w2_rdy)
                    te.wait(s_gh[par], gh_rdy)
                    P.bank_w(te, bk)
                    n_extra = 1 if kb == 0 else 0
                    for k in range(KB):
                        mi = te.matmul(B[bk][:], ghv[:, k, tt * 128:(tt + 1) * 128],
                                       w2k[par][:, k * 2048 + dmg * 512: k * 2048 + (dmg + 1) * 512],
                                       start=(k == 0), stop=(k == KB - 1 and n_extra == 0))
                    if kb == 0:
                        te.wait(s_w, s_w.n)
                        mi = te.matmul(B[bk][:], A["ones1bf"][:],
                                       A["sb2bf"][:, dmg * 512:(dmg + 1) * 512], start=False, stop=True)
                    mi.then_inc(s_pe, 1)
                    mm_fin = s_pe.n
                    P.bank_set_w(bk, s_pe)
                    ve.wait(s_pe, mm_fin)
                    ve.wait(s_chl, acc_init)
                    P.bank_r(ve, bk)
                    ve.scalar_tensor_tensor(accv[:, tt, dmg * 512:(dmg + 1) * 512],
                                            B[bk][:], 1.0,
                                            accv[:, tt, dmg * 512:(dmg + 1) * 512],
                                            ALU.mult, ALU.add).then_inc(s_dve, 1)
                    P.bank_add_r(bk, s_dve)
            w2_free[par] = s_pe.n
            gh2_free[par] = s_pe.n
        shared_acc = s_dve.n
        for tt in range(8):
            gtt = th * 8 + tt
            gp.wait(s_os[0], p4_done[0])
            gp.wait(s_os[1], p4_done[1])
            gp.wait(s_dve, P.g("p5_rg_ve"))
            if th == 0 and tt == 0:
                gp.reg_mov(A["reg_bchk"], SLOTS - 1)
            for j in range(2):
                idx = A["s1all"] if j == 0 else A["s2all"]
                gp.indirect_dma_start(
                    out=rgb[j][:], out_offset=None,
                    in_=A["d_rout2"], in_offset=bass.IndirectOffsetOnAxis(idx[:, gtt:gtt + 1], 0),
                    bounds_check=A["reg_bchk"], oob_is_err=False).then_inc(s_ig, 16)
            rg_rdy = s_ig.n
            ve.wait(s_ig, rg_rdy)
            ve.wait(s_dve, shared_acc)
            for j in range(2):
                ve.tensor_add(accv[:, tt, :], accv[:, tt, :], rgb[j][:]).then_inc(s_dve, 1)
            P.s("p5_rg_ve", s_dve.n)
        acc_done = s_dve.n
        ve.wait(s_dve, acc_done)
        for tt in range(8):
            ve.reduce_sum(A["ln_stats"][:, tt:tt + 1], accv[:, tt, :],
                          axis=AX.X).then_inc(s_dve, 1)
        sum_rdy = s_dve.n
        sc.wait(s_dve, sum_rdy)
        sc.wait(s_st, s_st.n)
        for tt in range(8):
            sc.activation(A["lnscr"][:], accv[:, tt, :], AF.Square,
                          accum_out=A["ln_stats"][:, 8 + tt:9 + tt]).then_inc(s_act, 1)
        sq2 = s_act.n
        ve.wait(s_act, sq2)
        ve.tensor_scalar_mul(A["ln_stats"][:, 0:8], A["ln_stats"][:, 0:8], 1.0 / DM).then_inc(s_dve, 1)
        ve.tensor_scalar_mul(A["ln_stats"][:, 8:16], A["ln_stats"][:, 8:16], 1.0 / DM).then_inc(s_dve, 1)
        ve.tensor_mul(A["lnscr"][:, 0:8], A["ln_stats"][:, 0:8], A["ln_stats"][:, 0:8]).then_inc(s_dve, 1)
        ve.tensor_sub(A["ln_stats"][:, 8:16], A["ln_stats"][:, 8:16], A["lnscr"][:, 0:8]).then_inc(s_dve, 1)
        ve.tensor_scalar_add(A["ln_stats"][:, 8:16], A["ln_stats"][:, 8:16], EPS).then_inc(s_dve, 1)
        v2 = s_dve.n
        sc.wait(s_dve, v2)
        sc.activation(A["ln_stats"][:, 8:16], A["ln_stats"][:, 8:16], AF.Ln).then_inc(s_act, 1)
        l2 = s_act.n
        ve.wait(s_act, l2)
        ve.tensor_scalar_mul(A["ln_stats"][:, 8:16], A["ln_stats"][:, 8:16], -0.5).then_inc(s_dve, 1)
        n2 = s_dve.n
        sc.wait(s_dve, n2)
        sc.activation(A["ln_stats"][:, 8:16], A["ln_stats"][:, 8:16], AF.Exp).then_inc(s_act, 1)
        rstd_rdy2 = s_act.n
        ve.wait(s_act, rstd_rdy2)
        st_prev = s_st.n
        for tt in range(8):
            row = th * 1024 + tt * 128
            ve.wait(s_st, st_prev)
            ve.tensor_scalar(A["lnscr"][:], accv[:, tt, :], A["ln_stats"][:, tt:tt + 1],
                             A["ln_stats"][:, 8 + tt:9 + tt],
                             ALU.subtract, ALU.mult).then_inc(s_dve, 1)
            fin = s_dve.n
            sy.wait(s_dve, fin)
            sy.dma_start(out=A["out"][row:row + 128, :], in_=A["lnscr"][:]).then_inc(s_st, 16)
            st_prev = s_st.n
        P.s("p5_acc_free", s_dve.n)
    _dump_taps(P, A, S)
    sy.wait(s_st, s_st.n)


def build_nc():
    nc = bass.Bass("TRN2", target_bir_lowering=False, debug=False)
    P = Prog(nc)
    A = {}

    def din(name, shape, dtype=F32):
        A["in_" + name] = nc.dram_tensor("in_" + name, list(shape), dtype,
                                         kind="ExternalInput").ap()

    din("x_fm", (DM, T + HALO))
    din("x_tm", (T, DM))
    din("w_in", (NFT * 128, 2048))
    din("conv_w", (128, 128))
    din("conv_b", (128, 32))
    din("a_dt", (32, 2))
    din("d_param", (128, 32))
    din("w_out", (DI, DM))
    din("gate_w", (DM, 8))
    din("w1", (8192, 2048))
    din("b1", (128, 64))
    din("sw1", (2048, 2048))
    din("sb1", (128, 16))
    din("w2cat", (10240, 2048))
    din("b2", (8, 2048))
    din("sb2", (1, 2048))
    din("ident", (128, 128))
    din("flag", (128, 1))
    din("lt2", (128, 512))
    din("sel32", (32, 4096))
    din("seg_mask", (32, 2048))
    din("iota_t", (128, 16))
    din("iota_w", (128, 8))
    din("srow16", (8, 16))
    din("l8u", (8, 8))
    din("gidx_init", (SLOTS, 2))
    din("b1r", (16, 1024))
    din("b2r", (16, 2048))
    A["out"] = nc.dram_tensor("out", [T, DM], F32, kind="ExternalOutput").ap()

    def internal(name, shape, dtype):
        A[name] = nc.dram_tensor(name, list(shape), dtype, kind="Internal").ap()
        if name in DEBUG_TAPS:
            A["dbg_" + name] = nc.dram_tensor("dbg_" + name, list(shape), F32,
                                              kind="ExternalOutput").ap()

    internal("sp_val", (DI, T), BF16)
    internal("sp_gate", (DI, T), BF16)
    internal("sp_B", (DI, T), BF16)
    internal("sp_C", (DI, T), BF16)
    internal("sp_Yd", (NCH * 128, 8192), BF16)
    internal("sp_Cexp", (NCH * 128, 8192), BF16)
    internal("sp_Yg", (DI, T), BF16)
    internal("sp_x1f", (T, DM), F32)
    internal("sp_x1bf", (T, DM), BF16)
    internal("sp_ghe", (2048, T), BF16)
    internal("d_w1p", (1024, 16384), BF16)
    internal("d_w2p", (1024, 16384), BF16)
    internal("d_b1b", (16, 1024), BF16)
    internal("d_b2b", (16, 2048), BF16)
    internal("d_sw1b", (2048, 2048), BF16)
    internal("d_w2shb", (2048, 2048), BF16)
    internal("d_woutb", (4096, 2048), BF16)
    internal("d_gidx", (SLOTS, 2), F32)
    internal("d_xg", (SLOTS, DM), BF16)
    internal("d_xgT", (DM, SLOTS), BF16)
    internal("d_rout2", (SLOTS, DM), BF16)
    internal("ag_in", (4096, 128), F32)
    internal("ag_out", (8192, 128), F32)

    sdefs = [
        ("ident", [128, 128], F32), ("flag", [128, 1], F32), ("conv_w", [128, 128], F32),
        ("conv_b", [128, 32], F32), ("d_param", [128, 32], F32), ("a_dt", [32, 2], F32),
        ("negA", [32, 1], F32), ("b1", [128, 64], F32), ("sb1", [128, 16], F32),
        ("b2bf", [8, 2048], BF16), ("sb2bf", [1, 2048], BF16), ("ones1bf", [1, 128], BF16),
        ("ones32", [32, 128], F32), ("gwbf", [128, 128], BF16), ("g_rows", [8, 2048], F32),
        ("ln_stats", [128, 16], F32), ("s1all", [128, 16], I32T), ("s2all", [128, 16], I32T),
        ("sel32", [32, 4096], F32), ("seg_mask", [32, 2048], F32),
        ("dt_sb", [32, 2048], F32), ("h_f32", [128, 4096], F32), ("h_bf16", [128, 4096], BF16),
        ("lasts", [32, 8], F32), ("ones8", [32, 8], F32), ("decays", [32, 8], F32),
        ("cum_tm", [128, 64], F32), ("neg_cum", [128, 64], F32), ("w_mh", [128, 64], F32),
        ("lastsbc", [128, 32], F32), ("el_bc", [128, 32], F32), ("D_bc", [128, 32], F32),
        ("cum_fm", [32, 2048], F32), ("dtscr", [32, 2048], F32),
        ("xfm", [128, 32816], BF16), ("wt0", [128, 2048], BF16), ("wt1", [128, 2048], BF16),
        ("valraw0", [128, 2056], F32), ("convacc", [128, 2048], F32),
        ("stg0", [128, 2048], BF16), ("stg1", [128, 2048], BF16), ("stg2", [128, 2048], BF16),
        ("val_fm", [128, 8192], BF16), ("B_fm", [128, 8192], BF16), ("C_fm", [128, 8192], BF16),
        ("val_tm0", [128, 4096], BF16), ("val_tm1", [128, 4096], BF16),
        ("B_tm0", [128, 4096], BF16), ("B_tm1", [128, 4096], BF16),
        ("Bw", [128, 512], BF16), ("e0", [128, 2048], F32),
        ("Mt0", [128, 2048], BF16), ("expcum", [128, 1024], BF16),
        ("lt2", [128, 512], F32),
        ("yd_st0", [128, 2048], BF16), ("ce_st0", [128, 2048], BF16),
        ("clbf", [128, 512], BF16), ("ctmp", [128, 1024], F32),
        ("wot", [128, 65536], BF16), ("yg_t", [128, 4096], BF16), ("yg_t2", [128, 4096], BF16), ("x1stage", [128, 2048], F32),
        ("xtm_b", [128, 2048], F32), ("sq_b", [128, 2048], F32),
        ("x1fm", [128, 32768], BF16), ("w1t0", [128, 2048], BF16), ("w1t1", [128, 2048], BF16),
        ("ghe_st0", [128, 2048], BF16), ("ghe_st1", [128, 2048], BF16),
        ("gel0", [128, 512], F32), ("gel1", [128, 512], F32),
        ("gtmp", [128, 16], F32), ("gtmp2", [128, 16], F32),
        ("ind8", [8, 2048], F32), ("pos8", [8, 2048], F32),
        ("slot_tm0", [128, 8], F32), ("slot_tm1", [128, 8], F32),
        ("gate_tm0", [128, 8], F32), ("gate_tm1", [128, 8], F32),
        ("islot", [128, 8], I32T),
        ("stage2a", [128, 2], F32), ("stage2b", [128, 2], F32),
        ("stage2c", [128, 2], F32), ("stage2d", [128, 2], F32),
        ("gvall", [128, 60], F32),
        ("gixall", [128, 120], F32), ("giiall", [128, 120], I32T),
        ("wif", [128, 8], F32), ("wii0", [128, 8], I32T), ("wii1", [128, 8], I32T),
        ("wb_bc", [128, 16], F32), ("cmp16", [8, 16], F32), ("rts", [8, 8], F32),
        ("egrow", [1, 16], F32), ("wbrow", [1, 16], F32),
        ("ones1f", [1, 128], F32),
        ("w1cA0", [128, 8192], BF16), ("w1cA1", [128, 8192], BF16),
        ("w1cB0", [128, 8192], BF16), ("w1cB1", [128, 8192], BF16),
        ("w2cA0", [128, 8192], BF16), ("w2cA1", [128, 8192], BF16),
        ("w2cB0", [128, 8192], BF16), ("w2cB1", [128, 8192], BF16),
        ("xgg", [128, 8192], BF16),
        ("ones8r", [8, 2048], F32),
        ("xts0a", [128, 768], BF16), ("xts0b", [128, 768], BF16),
        ("xts0c", [128, 768], BF16), ("xts0d", [128, 768], BF16),
        ("r1sb", [128, 2048], BF16), ("r2sb", [128, 2048], BF16),
        ("h_fm0", [128, 1024], BF16), ("h_fm1", [128, 1024], BF16),
        ("osb0", [128, 2048], BF16), ("osb1", [128, 2048], BF16),
        ("xg_sb0", [128, 2048], BF16), ("xg_sb1", [128, 2048], BF16),
        ("xg_sb2", [128, 2048], BF16), ("xg_sb3", [128, 2048], BF16),
        ("xg_sb4", [128, 2048], BF16), ("xg_sb5", [128, 2048], BF16),
        ("xg_sb6", [128, 2048], BF16), ("xg_sb7", [128, 2048], BF16),
        ("b2b", [128, 2048], BF16), ("b1b", [128, 1024], BF16),
        ("l8u", [8, 8], F32), ("srow16", [8, 16], F32),
        ("iota_t", [128, 16], F32), ("iota_w", [128, 8], F32),
        ("w2k0", [128, 16384], BF16), ("w2k1", [128, 16384], BF16),
        ("ghk0", [128, 8192], BF16), ("ghk1", [128, 8192], BF16),
        ("out_acc", [128, 16384], F32), ("lnscr", [128, 2048], F32),
    ]
    for nm, shp, dt in sdefs:
        A[nm] = nc.alloc_sbuf_tensor_at("sb_" + nm, shp, dt, offset=_MAP[nm]).ap()

    A["reg_exp"] = nc.gpsimd.alloc_register("r_exp")
    A["reg_off"] = nc.gpsimd.alloc_register("r_off")
    A["reg_bchk"] = nc.gpsimd.alloc_register("r_bchk")
    A["reg_spares"] = [nc.gpsimd.alloc_register("r_spare%d" % i) for i in range(16)]
    with ExitStack() as stack:
        e = stack.enter_context
        S = {nm: Sem(e(nc.semaphore("s_" + nm)), nm) for nm in
             ["w", "ld", "sp", "st", "pe", "act", "dve", "pool", "cc",
              "wta", "wtb", "ch", "gha", "ghb", "spg",
              "b8a", "b8b", "e8a", "e8b", "m8a", "m8b", "ig", "gx",
              "gt0", "gt1", "gt2", "gt3", "gt4", "gt5", "gt6", "gt7", "gt8", "gt9", "gst", "gse", "xts", "xto", "wh0", "wh1", "swc", "gxo", "gxs0", "gxs1", "xss0", "xss1", "xq1", "xq2", "xq3", "os0", "os1", "g4", "g5", "g6", "g7"]}
        Bh = [e(nc.psum_tensor("pb%d" % i, [128, 512], F32)) for i in range(8)]
        B = [b[:] for b in Bh]
        P.sems = S
        block = e(nc.Block())
        for en in ["sync", "tensor", "scalar", "vector", "gpsimd"]:
            P.reset()
            P.active = en

            def run(eng, en=en):
                P.real[en] = eng
                emit(P, A, S, B)

            getattr(block, en)(run)
    return nc


def pack_inputs(inputs):
    f32 = np.float32
    x = np.asarray(inputs["L0_attn_x"], f32)
    W_in = np.asarray(inputs["L0_attn_W_in"], f32)
    conv_w = np.asarray(inputs["L0_attn_conv_w"], f32)
    conv_b = np.asarray(inputs["L0_attn_conv_b"], f32)
    A_log = np.asarray(inputs["L0_attn_A_log"], f32)
    D_param = np.asarray(inputs["L0_attn_D_param"], f32)
    dt_bias = np.asarray(inputs["L0_attn_dt_bias"], f32)
    W_out = np.asarray(inputs["L0_attn_W_out"], f32)
    gate_w = np.asarray(inputs["L0_ffn_gate_w"], f32)
    w1 = np.asarray(inputs["L0_ffn_w1"], f32)
    b1 = np.asarray(inputs["L0_ffn_b1"], f32)
    w2 = np.asarray(inputs["L0_ffn_w2"], f32)
    b2 = np.asarray(inputs["L0_ffn_b2"], f32)
    sw1 = np.asarray(inputs["L0_ffn_shared_w1"], f32)
    sb1 = np.asarray(inputs["L0_ffn_shared_b1"], f32)
    sw2 = np.asarray(inputs["L0_ffn_shared_w2"], f32)
    sb2 = np.asarray(inputs["L0_ffn_shared_b2"], f32)

    W_pad = np.zeros((NFT * 128, 2048), f32)
    W_pad[:16416] = W_in
    w_in_p = np.ascontiguousarray(
        W_pad.reshape(NFT, 128, 16, 128).transpose(0, 3, 2, 1).reshape(NFT * 128, 2048))
    conv_w_p = np.ascontiguousarray(conv_w.reshape(32, 128, 4).transpose(1, 0, 2).reshape(128, 128))
    conv_b_p = np.ascontiguousarray(conv_b.reshape(32, 128).T)
    d_p = np.ascontiguousarray(D_param.reshape(32, 128).T)
    a_dt = np.stack([A_log, dt_bias], axis=1).astype(f32)
    w_out_p = np.ascontiguousarray(W_out.T)
    w1_p = np.ascontiguousarray(
        w1.reshape(8, 16, 128, 8, 128).transpose(0, 3, 2, 1, 4).reshape(8192, 2048))
    b1_p = np.ascontiguousarray(b1.reshape(8, 8, 128).transpose(2, 0, 1).reshape(128, 64))
    sw1_p = np.ascontiguousarray(
        sw1.reshape(16, 128, 16, 128).transpose(2, 1, 0, 3).reshape(2048, 2048))
    sb1_p = np.ascontiguousarray(sb1.reshape(16, 128).T)
    w2cat = np.ascontiguousarray(np.concatenate([w2.reshape(8192, 2048), sw2], axis=0))
    sb2_p = np.ascontiguousarray(sb2.reshape(1, 2048))
    ident = np.eye(128, dtype=f32)
    lt2 = np.zeros((128, 512), f32)
    ll = np.arange(256)[None, :]
    pp = np.arange(128)[:, None]
    lt2[:, 0:256] = np.where(ll >= pp, 0.0, -60000.0)
    lt2[:, 256:512] = np.where(ll >= 128 + pp, 0.0, -60000.0)
    sel32 = (np.eye(32, dtype=f32)[:, :, None] * np.ones(128, f32)).reshape(32, 4096)
    iota_t = (np.arange(16)[None, :] * 128 + np.arange(128)[:, None]).astype(f32)
    iota_w = (np.arange(8)[None, :] * 128 + np.arange(128)[:, None]).astype(f32)
    srow16 = np.broadcast_to((np.arange(16) * 512).astype(f32), (8, 16)).copy()
    l8u = np.triu(np.ones((8, 8), f32), 1)
    gidx_init = np.zeros((SLOTS, 2), f32)
    seg = np.ones((32, 2048), f32)
    seg[:, ::CS] = 0.0

    common = {
        "in_w_in": w_in_p, "in_conv_w": conv_w_p, "in_conv_b": conv_b_p,
        "in_a_dt": a_dt, "in_d_param": d_p, "in_w_out": w_out_p,
        "in_gate_w": gate_w, "in_w1": w1_p, "in_b1": b1_p, "in_sw1": sw1_p,
        "in_sb1": sb1_p, "in_w2cat": w2cat, "in_b2": b2, "in_sb2": sb2_p,
        "in_ident": ident, "in_lt2": lt2, "in_sel32": sel32, "in_seg_mask": seg,
        "in_iota_t": iota_t, "in_iota_w": iota_w, "in_srow16": srow16, "in_l8u": l8u,
        "in_gidx_init": gidx_init,
        "in_b1r": np.concatenate([b1.reshape(8, 1024), np.zeros((8, 1024), f32)], axis=0),
        "in_b2r": np.concatenate([b2, np.zeros((8, 2048), f32)], axis=0),
    }
    maps = []
    for core in range(8):
        b_i = core // 2
        half = core % 2
        t0 = half * T
        xs = x[b_i, t0:t0 + T]
        if half == 0:
            xh = np.concatenate([np.zeros((HALO, DM), f32), xs], axis=0)
        else:
            xh = x[b_i, t0 - HALO:t0 + T]
        m = dict(common)
        m["in_x_fm"] = np.ascontiguousarray(xh.T)
        m["in_x_tm"] = np.ascontiguousarray(xs)
        m["in_flag"] = np.full((128, 1), float(half), f32)
        maps.append(m)
    return maps


def kernel(**inputs):
    maps = pack_inputs(inputs)
    nc = build_nc()
    res = run_bass_kernel_spmd(nc, maps, core_ids=list(range(8)))
    out = np.zeros((4, 4096, 2048), np.float32)
    for core in range(8):
        b_i = core // 2
        half = core % 2
        out[b_i, half * T:(half + 1) * T] = np.asarray(res.results[core]["out"], np.float32)
    if DEBUG_TAPS:
        kernel.debug = res.results
    return out


# version 28 (final)
# speedup vs baseline: 1.0967x; 1.0203x over previous
import numpy as np
import ml_dtypes
from contextlib import ExitStack

import concourse.bass as bass
import concourse.mybir as mybir
from concourse.bass_utils import run_bass_kernel_spmd

F32 = mybir.dt.float32
BF16 = mybir.dt.bfloat16
I32T = mybir.dt.int32
AF = mybir.ActivationFunctionType
ALU = mybir.AluOpType
AX = mybir.AxisListType

DM = 2048
T = 2048
HALO = 3
NFT = 129
KD = 16
CS = 256
NCH = 8
H = 32
DI = 4096
E = 8
NKR = 80
KB = 8
NKB = 2
SLOTS = 7680
NST = 60
NGRP = 15
NSPL = 10
TPS = 6
SSL = 768
EPS = 1e-5

DEBUG_TAPS = set()
PHASE_LIMIT = 99
FT_LIMIT = NFT

SB_BASE = 16512
SB_LIMIT = 229344 - SB_BASE


def _build_map():
    m = {}

    def seq(start, items):
        o = start
        for name, byts in items:
            o = (o + 31) & ~31
            m[name] = o
            o += byts
        return o

    end0 = seq(0, [
        ("ident", 512), ("flag", 32), ("conv_w", 512), ("conv_b", 128),
        ("d_param", 128), ("a_dt", 32), ("negA", 32), ("b1", 256), ("sb1", 64),
        ("b2bf", 4096), ("sb2bf", 4096), ("ones1bf", 256), ("ones32", 512),
        ("gwbf", 256), ("g_rows", 8192), ("ln_stats", 64),
        ("lt2", 2048), ("s1all", 64), ("s2all", 64),
    ])
    assert end0 <= 25600, end0
    end1 = seq(25632, [
        ("sel32", 16384), ("sel8", 4096), ("seg_mask", 8192), ("dt_sb", 8192),
        ("h_f32", 16384), ("h_bf16", 8192),
        ("lasts", 64), ("ones8", 32), ("decays", 32),
        ("cum_tm", 256), ("neg_cum", 256), ("w_mh", 256),
        ("lastsbc", 128), ("el_bc", 128), ("D_bc", 128),
        ("cum_fm", 8192), ("dtscr", 8192),
    ])
    P2S = (end1 + 31) & ~31
    endp1 = seq(P2S, [
        ("xfm", 65632), ("wt0", 4096), ("wt1", 4096),
        ("valraw0", 8224), ("convacc", 8192),
        ("stg0", 4096), ("stg1", 4096), ("stg2", 4096),
    ])
    assert endp1 <= SB_LIMIT, endp1
    endp2 = seq(P2S, [
        ("val_fm", 16384), ("B_fm", 16384), ("C_fm", 16384),
        ("val_tm0", 8192), ("val_tm1", 8192), ("B_tm0", 8192), ("B_tm1", 8192),
        ("Bw", 1024), ("e0", 4096), ("Mt0", 2048), ("expcum", 1024),
        ("yd_st0", 4096), ("ce_st0", 4096),
    ])
    assert endp2 <= SB_LIMIT, endp2
    m["clbf"] = m["sel32"]
    m["ctmp"] = m["sel32"] + 1024
    m["wot"] = 49152
    assert 49152 + 131072 <= 180256
    m["yg_t"] = 180256
    m["yg_t2"] = 188448
    m["xtm_b"] = 196640
    m["sq_b"] = 25632
    m["x1stage"] = 33824
    m["x1fm"] = 49152
    m["w1t0"] = 114688
    m["w1t1"] = 118784
    m["ghe_st0"] = 155648
    m["ghe_st1"] = 159744
    m["gel0"] = 163840
    m["gel1"] = 165888
    m["gtmp"] = 167936
    m["gtmp2"] = 168000
    m["ind8"] = 25632
    m["pos8"] = 33824
    m["slot_tm0"] = 198656
    m["slot_tm1"] = 198720
    m["gate_tm0"] = 198784
    m["gate_tm1"] = 198848
    m["islot"] = 198912
    m["stage2a"] = 198976
    m["stage2b"] = 199008
    m["stage2c"] = 201344
    m["stage2d"] = 201376
    m["gvall"] = 199040
    m["gixall"] = 199296
    m["giiall"] = 199776
    m["wif"] = 200256
    m["wii0"] = 200288
    m["wii1"] = 200320
    m["wb_bc"] = 200352
    m["cmp16"] = 200416
    m["rts"] = 200480
    m["egrow"] = 200512
    m["wbrow"] = 200576
    m["ones1f"] = 200640
    m["l8u"] = 201152
    m["srow16"] = 201184
    m["iota_t"] = 201248
    m["iota_w"] = 201312
    m["w1cA0"] = 49152
    m["w1cA1"] = 65536
    m["w1cB0"] = 81920
    m["w1cB1"] = 98304
    m["w2cA0"] = 114688
    m["w2cA1"] = 131072
    m["w2cB0"] = 147456
    m["w2cB1"] = 163840
    m["xgg"] = 180224
    m["ones8r"] = 180224
    m["r1sb"] = 33824
    m["r2sb"] = 204800
    m["h_fm0"] = 204832
    m["h_fm1"] = 206880
    m["osb0"] = 33824
    m["osb1"] = 37920
    m["xg_sb0"] = 25632
    m["xg_sb1"] = 29728
    m["xg_sb2"] = 33824
    m["xg_sb3"] = 37920
    m["xg_sb4"] = 180224
    m["xg_sb5"] = 184320
    m["xg_sb6"] = 188416
    m["xg_sb7"] = 192512
    m["b2b"] = 42016
    m["b1b"] = 46112
    m["w2k0"] = 40960
    m["w2k1"] = 73728
    m["ghk0"] = 106496
    m["ghk1"] = 122880
    m["out_acc"] = 139264
    m["lnscr"] = 25632
    m["xts0a"] = m["g_rows"]
    m["xts0b"] = m["g_rows"] + 1536
    m["xts0c"] = m["g_rows"] + 3072
    m["xts0d"] = m["g_rows"] + 4608
    for k in m:
        m[k] += SB_BASE
    return m


_MAP = _build_map()


class Sem:
    __slots__ = ("h", "n", "name")

    def __init__(self, h, name):
        self.h = h
        self.n = 0
        self.name = name


class _I:
    __slots__ = ("r", "auto")

    def __init__(self, r, auto=None):
        self.r = r
        self.auto = auto

    def then_inc(self, sem, k=1):
        if sem is self.auto:
            return self
        sem.n += k
        if self.r is not None:
            self.r.then_inc(sem.h, k)
        return self


_NO_CHAIN = {"wait_ge", "dma_start", "dma_start_transpose", "collective_compute",
             "indirect_dma_start", "reg_load", "reg_mul", "reg_add", "reg_mov",
             "reg_sub", "nop", "sem_inc", "drain"}


class Eng:
    def __init__(self, prog, name, auto_sem_name=None):
        self._P = prog
        self._name = name
        self._auto_name = auto_sem_name

    def __getattr__(self, attr):
        P = self._P
        name = self._name
        auto_name = self._auto_name if attr not in _NO_CHAIN else None

        def call(*a, **kw):
            auto = P.sems[auto_name] if auto_name else None
            if P.active == name:
                real_eng = P.real[name]
                if auto is not None and auto.n > 0:
                    real_eng.wait_ge(auto.h, auto.n)
                ri = getattr(real_eng, attr)(*a, **kw)
                if auto is not None:
                    ri.then_inc(auto.h, 1)
                    auto.n += 1
                return _I(ri, auto)
            if auto is not None:
                auto.n += 1
            return _I(None, auto)

        return call

    def wait(self, sem, val=None):
        if val is None:
            val = sem.n
        if val <= 0:
            return
        if self._P.active == self._name:
            self._P.real[self._name].wait_ge(sem.h, val)


class Prog:
    def __init__(self, nc):
        self.nc = nc
        self.active = None
        self.real = {}
        self.sy = Eng(self, "sync")
        self.te = Eng(self, "tensor")
        self.sc = Eng(self, "scalar", "act")
        self.ve = Eng(self, "vector", "dve")
        self.gp = Eng(self, "gpsimd", "pool")

    def reset(self):
        for s in self.sems.values():
            s.n = 0
        self.bank = [{"wr": None, "rd": {}} for _ in range(8)]
        self.snap = {}

    def g(self, key):
        return self.snap.get(key, 0)

    def s(self, key, val):
        self.snap[key] = val

    def bank_w(self, eng, i):
        for s, v in self.bank[i]["rd"].items():
            eng.wait(s, v)

    def bank_r(self, eng, i):
        wr = self.bank[i]["wr"]
        if wr is not None:
            eng.wait(wr[0], wr[1])

    def bank_set_w(self, i, sem):
        self.bank[i]["wr"] = (sem, sem.n)
        self.bank[i]["rd"] = {}

    def bank_add_r(self, i, sem):
        self.bank[i]["rd"][sem] = sem.n


def _dump_taps(P, A, S):
    if not DEBUG_TAPS:
        return
    gp, sy = P.gp, P.sy
    for nm in ["w", "ld", "sp", "spg", "wta", "wtb", "ch", "gha", "ghb", "ig", "gx", "pe", "act", "dve", "pool"]:
        gp.wait(S[nm], S[nm].n)
    for name in sorted(DEBUG_TAPS):
        src = A[name]
        rows = src.shape[0]
        step = max(rows // 8, 1)
        for r0 in range(0, rows, step):
            r1 = min(r0 + step, rows)
            gp.dma_start(out=A["dbg_" + name][r0:r1, :], in_=src[r0:r1, :]).then_inc(S["st"], 16)
    sy.wait(S["st"], S["st"].n)


def emit(P, A, S, B):
    sy, te, sc, ve, gp = P.sy, P.te, P.sc, P.ve, P.gp
    if P.active == "gpsimd":
        for _r in A["reg_spares"]:
            P.real["gpsimd"].free_register(_r)
    s_w, s_ld, s_sp, s_st = S["w"], S["ld"], S["sp"], S["st"]
    s_pe, s_act, s_dve, s_pool, s_cc = S["pe"], S["act"], S["dve"], S["pool"], S["cc"]
    s_wt = [S["wta"], S["wtb"]]
    s_chl = S["ch"]
    s_gh = [S["gha"], S["ghb"]]
    s_spg = S["spg"]
    s_ig = S["ig"]
    s_gx = S["gx"]
    s_wc = S["b8a"]

    for nm in ["ident", "flag", "conv_w", "conv_b", "a_dt", "d_param", "b1", "sb1",
               "lt2", "sel32", "seg_mask"]:
        sy.dma_start(out=A[nm][:], in_=A["in_" + nm]).then_inc(s_ld, 16)
    p0_consts = s_ld.n
    gp.dma_start(out=A["b2bf"][:], in_=A["in_b2"]).then_inc(s_w, 16)
    gp.dma_start(out=A["sb2bf"][:], in_=A["in_sb2"]).then_inc(s_w, 16)
    p0_bias = s_w.n
    xfm3 = A["xfm"][:].rearrange("p (k t) -> p k t", k=KD)
    src3 = A["in_x_fm"].rearrange("(k p) t -> p k t", p=128)
    for i in range(4):
        gp.dma_start(out=xfm3[:, i * 4:(i + 1) * 4, :],
                     in_=src3[:, i * 4:(i + 1) * 4, :]).then_inc(s_w, 16)
    p0_xfm = s_w.n
    gp.memset(A["ones1bf"][:], 1.0).then_inc(s_pool, 1)
    gp.memset(A["ones32"][:], 1.0).then_inc(s_pool, 1)
    gp.memset(A["ones8"][:], 1.0).then_inc(s_pool, 1)
    p0_pool = s_pool.n
    ve.memset(A["h_f32"][:], 0.0).then_inc(s_dve, 1)
    ve.memset(A["h_bf16"][:], 0.0).then_inc(s_dve, 1)
    p0_h = s_dve.n
    sc.wait(s_ld, p0_consts)
    sc.activation(A["negA"][:], A["a_dt"][:, 0:1], AF.Exp).then_inc(s_act, 1)
    ve.wait(s_act, s_act.n)
    ve.tensor_scalar_mul(A["negA"][:], A["negA"][:], -1.0).then_inc(s_dve, 1)
    p0_negA = s_dve.n

    if PHASE_LIMIT < 1:
        return

    _pc = []
    for hh in range(2):
        _pc.append((bass.AP(A["d_sw1b"].tensor, hh * 1024 * 2048,
                            [[2048, 1024], [1, 2048]]),
                    bass.AP(A["in_sw1"].tensor, hh * 1024 * 2048,
                            [[2048, 1024], [1, 2048]]), S["swc"]))
    for hh in range(4):
        _pc.append((bass.AP(A["d_woutb"].tensor, hh * 1024 * 2048,
                            [[2048, 1024], [1, 2048]]),
                    bass.AP(A["in_w_out"].tensor, hh * 1024 * 2048,
                            [[2048, 1024], [1, 2048]]), S["swc"]))
    for i in range(8):
        _pc.append((bass.AP(A["d_w1p"].tensor, i * 128 * 16384,
                            [[16384, 128], [2048, 8], [1, 2048]]),
                    bass.AP(A["in_w1"].tensor, i * 8 * 128 * 2048,
                            [[2048, 128], [128 * 2048, 8], [1, 2048]]), s_wc))
        _pc.append((bass.AP(A["d_w2p"].tensor, i * 128 * 16384,
                            [[16384, 128], [2048, 8], [1, 2048]]),
                    bass.AP(A["in_w2cat"].tensor, i * 8 * 128 * 2048,
                            [[2048, 128], [128 * 2048, 8], [1, 2048]]), s_wc))
    _pc.append((A["d_b1b"], A["in_b1r"], s_wc))
    _pc.append((A["d_b2b"], A["in_b2r"], s_wc))
    for hh in range(2):
        _pc.append((bass.AP(A["d_w2shb"].tensor, hh * 1024 * 2048,
                            [[2048, 1024], [1, 2048]]),
                    bass.AP(A["in_w2cat"].tensor, (8192 + hh * 1024) * 2048,
                            [[2048, 1024], [1, 2048]]), s_wc))
    _pci = [0]

    def _emit_pc():
        if _pci[0] < len(_pc):
            d, sr, sem = _pc[_pci[0]]
            gp.dma_start(out=d, in_=sr).then_inc(sem, 16)
            _pci[0] += 1

    wt = [A["wt0"], A["wt1"]]
    vr = [A["valraw0"]]
    stg = [A["stg0"], A["stg1"], A["stg2"]]
    wt_free = [0, 0]
    vr_free = [0]
    stg_free = [0, 0, 0]
    conv_act = 0

    for ft in range(min(NFT, FT_LIMIT)):
        typ = ("val" if ft < 32 else "gate" if ft < 64 else "B" if ft < 96 else
               "C" if ft < 128 else "dt")
        par = ft % 2
        vp = 0
        sg = ft % 3
        gp.wait(s_pe, wt_free[par])
        gp.dma_start(out=wt[par][:], in_=A["in_w_in"][ft * 128:(ft + 1) * 128, :]).then_inc(s_wt[par], 16)
        w_rdy = s_wt[par].n
        if ft >= 10 and ft % 5 == 0:
            _emit_pc()
        te.wait(s_wt[par], w_rdy)
        if ft == 0:
            te.wait(s_w, p0_xfm)
        bset = 4 * (ft % 2)
        halo_done = 0
        if typ == "val":
            P.bank_w(te, bset + 2)
            for k in range(KD):
                mi = te.matmul(B[bset + 2][:, 0:HALO], wt[par][:, k * 128:(k + 1) * 128],
                               xfm3[:, k, 0:HALO], start=(k == 0), stop=(k == KD - 1))
            mi.then_inc(s_pe, 1)
            halo_done = s_pe.n
            P.bank_set_w(bset + 2, s_pe)
        if typ == "val":
            sc.wait(s_dve, vr_free[vp])
            sc.wait(s_pe, halo_done)
            P.bank_r(sc, bset + 2)
            sc.copy(vr[vp][:, 0:HALO], B[bset + 2][:, 0:HALO]).then_inc(s_act, 1)
            P.bank_add_r(bset + 2, s_act)
        if typ in ("gate",) or typ == "val":
            sc.wait(s_sp, s_sp.n)
        if typ in ("B", "C"):
            ve.wait(s_sp, s_sp.n)
        for tg in range(4):
            P.bank_w(te, bset + tg)
        for k in range(KD):
            for tg in range(4):
                mi = te.matmul(B[bset + tg][:], wt[par][:, k * 128:(k + 1) * 128],
                               xfm3[:, k, HALO + tg * 512: HALO + (tg + 1) * 512],
                               start=(k == 0), stop=(k == KD - 1))
        mi.then_inc(s_pe, 1)
        mm_done = s_pe.n
        for tg in range(4):
            P.bank_set_w(bset + tg, s_pe)
        for tg in range(4):
            bk = bset + tg
            if typ == "val":
                if tg == 0:
                    sc.wait(s_dve, vr_free[vp])
                sc.wait(s_pe, mm_done)
                P.bank_r(sc, bk)
                sc.copy(vr[vp][:, HALO + tg * 512: HALO + (tg + 1) * 512],
                        B[bk][:]).then_inc(s_act, 1)
                P.bank_add_r(bk, s_act)
            elif typ == "gate":
                sc.wait(s_pe, mm_done)
                P.bank_r(sc, bk)
                sc.activation(stg[sg][:, tg * 512:(tg + 1) * 512], B[bk][:],
                              AF.Silu).then_inc(s_act, 1)
                P.bank_add_r(bk, s_act)
            elif typ in ("B", "C"):
                ve.wait(s_pe, mm_done)
                P.bank_r(ve, bk)
                ve.tensor_copy(stg[sg][:, tg * 512:(tg + 1) * 512], B[bk][:]).then_inc(s_dve, 1)
                P.bank_add_r(bk, s_dve)
            else:
                sc.wait(s_pe, mm_done)
                P.bank_r(sc, bk)
                sc.copy(A["dt_sb"][:, tg * 512:(tg + 1) * 512], B[bk][0:32, :]).then_inc(s_act, 1)
                P.bank_add_r(bk, s_act)
        wt_free[par] = s_pe.n

        if typ == "val":
            evac_done = s_act.n
            ve.wait(s_act, max(evac_done, conv_act))
            ve.tensor_scalar_mul(A["convacc"][:], vr[vp][:, 0:T],
                                 A["conv_w"][:, ft * 4: ft * 4 + 1]).then_inc(s_dve, 1)
            for k in range(1, 4):
                ve.scalar_tensor_tensor(A["convacc"][:], vr[vp][:, k:k + T],
                                        A["conv_w"][:, ft * 4 + k: ft * 4 + k + 1],
                                        A["convacc"][:], ALU.mult, ALU.add).then_inc(s_dve, 1)
            vr_free[vp] = s_dve.n
            conv_done = s_dve.n
            sc.wait(s_dve, conv_done)
            sc.wait(s_sp, s_sp.n)
            sc.activation(stg[sg][:], A["convacc"][:], AF.Silu,
                          bias=A["conv_b"][:, ft:ft + 1]).then_inc(s_act, 1)
            conv_act = s_act.n
            sy.wait(s_act, s_act.n)
            sy.dma_start(out=A["sp_val"][ft * 128:(ft + 1) * 128, :], in_=stg[sg][:]).then_inc(s_sp, 16)
            stg_free[sg] = s_sp.n
        elif typ == "gate":
            sy.wait(s_act, s_act.n)
            sy.dma_start(out=A["sp_gate"][(ft - 32) * 128:(ft - 31) * 128, :],
                         in_=stg[sg][:]).then_inc(s_sp, 16)
            stg_free[sg] = s_sp.n
        elif typ in ("B", "C"):
            sy.wait(s_dve, s_dve.n)
            dst = A["sp_B"] if typ == "B" else A["sp_C"]
            base = (ft - 64) if typ == "B" else (ft - 96)
            sy.dma_start(out=dst[base * 128:(base + 1) * 128, :], in_=stg[sg][:]).then_inc(s_sp, 16)
            stg_free[sg] = s_sp.n
    p1_spills = s_sp.n
    p1_dt = s_act.n
    p1_pe = s_pe.n
    while _pci[0] < len(_pc):
        _emit_pc()
    P.s("sw1b_rdy", S["swc"].n)
    P.s("wb_cast", s_wc.n)

    if PHASE_LIMIT < 2:
        sy.wait(s_sp, p1_spills)
        _dump_taps(P, A, S)
        return

    ve.wait(s_act, p1_dt)
    ve.wait(s_ld, p0_consts)
    ve.wait(s_dve, p0_negA)
    ve.tensor_scalar_add(A["dt_sb"][:], A["dt_sb"][:], A["a_dt"][:, 1:2]).then_inc(s_dve, 1)
    dt_abs = s_dve.n
    sc.wait(s_dve, dt_abs)
    sc.activation(A["dtscr"][:], A["dt_sb"][:], AF.Abs).then_inc(s_act, 1)
    sc.activation(A["dtscr"][:], A["dtscr"][:], AF.Exp, scale=-1.0).then_inc(s_act, 1)
    sc.activation(A["dtscr"][:], A["dtscr"][:], AF.Ln, bias=1.0).then_inc(s_act, 1)
    dt_ln = s_act.n
    ve.wait(s_act, dt_ln)
    ve.tensor_scalar_max(A["dt_sb"][:], A["dt_sb"][:], 0.0).then_inc(s_dve, 1)
    ve.tensor_add(A["dt_sb"][:], A["dt_sb"][:], A["dtscr"][:]).then_inc(s_dve, 1)
    ve.tensor_scalar_mul(A["dt_sb"][:], A["dt_sb"][:], A["negA"][:, 0:1]).then_inc(s_dve, 1)
    ve.tensor_tensor_scan(A["cum_fm"][:], A["seg_mask"][:], A["dt_sb"][:], 0.0,
                          ALU.mult, ALU.add).then_inc(s_dve, 1)
    for c in range(NCH):
        ve.tensor_copy(A["lasts"][:, c:c + 1],
                       A["cum_fm"][:, c * CS + CS - 1: c * CS + CS]).then_inc(s_dve, 1)
    ve.wait(s_pool, p0_pool)
    ve.tensor_tensor_scan(A["decays"][:], A["ones8"][:], A["lasts"][:], 0.0,
                          ALU.mult, ALU.add).then_inc(s_dve, 1)
    ve.tensor_sub(A["decays"][:], A["decays"][:], A["lasts"][:]).then_inc(s_dve, 1)
    p2_cum = s_dve.n
    v_tm = [A["val_tm0"], A["val_tm1"]]
    b_tm = [A["B_tm0"], A["B_tm1"]]
    yd_st = [A["yd_st0"], A["yd_st0"]]
    ce_st = [A["ce_st0"], A["ce_st0"]]
    spl_v3 = A["sp_val"].rearrange("(i p) t -> p i t", p=128)
    spl_b3 = A["sp_B"].rearrange("(i p) t -> p i t", p=128)
    spl_c3 = A["sp_C"].rearrange("(i p) t -> p i t", p=128)
    vfm3 = A["val_fm"][:].rearrange("p (i t) -> p i t", i=32)
    bfm3 = A["B_fm"][:].rearrange("p (i t) -> p i t", i=32)
    cfm3 = A["C_fm"][:].rearrange("p (i t) -> p i t", i=32)
    yd_free = [0, 0]
    ce_free = [0, 0]

    for c in range(NCH):
        sy.wait(s_sp, p1_spills)
        sy.wait(s_pe, max(P.g("p2_fm_pe"), p1_pe))
        sy.wait(s_dve, P.g("p2_fm_dve"))
        sy.wait(s_pool, P.g("p2_fm_pool"))
        for q in range(4):
            sl = slice(q * 8, (q + 1) * 8)
            sy.dma_start(out=vfm3[:, sl, :], in_=spl_v3[:, sl, c * CS:(c + 1) * CS]).then_inc(s_chl, 16)
            sy.dma_start(out=bfm3[:, sl, :], in_=spl_b3[:, sl, c * CS:(c + 1) * CS]).then_inc(s_chl, 16)
            sy.dma_start(out=cfm3[:, sl, :], in_=spl_c3[:, sl, c * CS:(c + 1) * CS]).then_inc(s_chl, 16)
        for mh in range(2):
            col = c * CS + mh * 128
            sy.dma_start(out=v_tm[mh][:], in_=A["sp_val"][:, col:col + 128], transpose=True).then_inc(s_chl, 16)
            sy.dma_start(out=b_tm[mh][:], in_=A["sp_B"][:, col:col + 128], transpose=True).then_inc(s_chl, 16)
        ld_chunk = s_chl.n

        cum_c = A["cum_fm"][:, c * CS:(c + 1) * CS]
        te.wait(s_dve, p2_cum)
        for mh in range(2):
            P.bank_w(te, 6)
            te.transpose(B[6][0:128, 0:32], cum_c[:, mh * 128:(mh + 1) * 128],
                         A["ident"][0:32, 0:32]).then_inc(s_pe, 1)
            P.bank_set_w(6, s_pe)
            ve.wait(s_pe, s_pe.n)
            P.bank_r(ve, 6)
            ve.wait(s_act, P.g("p2_acts_prev"))
            ve.tensor_copy(A["cum_tm"][:, mh * 32:(mh + 1) * 32], B[6][0:128, 0:32]).then_inc(s_dve, 1)
            P.bank_add_r(6, s_dve)
        ve.tensor_scalar_mul(A["neg_cum"][:], A["cum_tm"][:], -1.0).then_inc(s_dve, 1)
        ve.wait(s_pe, P.g("p2_diag_pe"))
        ve.tensor_scalar_mul(A["dtscr"][0:32, 0:32], A["ident"][0:32, 0:32],
                             A["lasts"][:, c:c + 1]).then_inc(s_dve, 1)
        diag_rdy = s_dve.n
        te.wait(s_dve, diag_rdy)
        P.bank_w(te, 7)
        te.matmul(B[7][:, 0:32], A["ones32"][:], A["dtscr"][0:32, 0:32],
                  start=True, stop=True).then_inc(s_pe, 1)
        P.s("p2_diag_pe", s_pe.n)
        P.bank_set_w(7, s_pe)
        ve.wait(s_pe, s_pe.n)
        P.bank_r(ve, 7)
        ve.tensor_copy(A["lastsbc"][:], B[7][:, 0:32]).then_inc(s_dve, 1)
        P.bank_add_r(7, s_dve)
        lbc_rdy = s_dve.n
        sc.wait(s_dve, lbc_rdy)
        sc.wait(s_dve, P.g("p2_h_valid"))
        sc.activation(A["el_bc"][:], A["lastsbc"][:], AF.Exp).then_inc(s_act, 1)
        el_rdy = s_act.n
        for mh in range(2):
            ve.tensor_sub(A["w_mh"][:, mh * 32:(mh + 1) * 32], A["lastsbc"][:],
                          A["cum_tm"][:, mh * 32:(mh + 1) * 32]).then_inc(s_dve, 1)
        w_sub = s_dve.n
        sc.wait(s_dve, w_sub)
        sc.activation(A["w_mh"][:], A["w_mh"][:], AF.Exp).then_inc(s_act, 1)
        w_rdy2 = s_act.n
        pre_dve = s_dve.n

        h_valid_chunk = P.g("p2_h_valid")
        pend = {}

        def s1(h):
            hp = h % 2
            d = pend.setdefault(h, {})
            te.wait(s_dve, pre_dve)
            te.wait(s_chl, ld_chunk)
            P.bank_w(te, hp)
            te.matmul(B[hp][:, 0:256],
                      A["sel32"][:, h * 128:(h + 1) * 128], cum_c,
                      start=True, stop=True).then_inc(s_pe, 1)
            d["bc"] = s_pe.n
            P.bank_set_w(hp, s_pe)
            P.bank_w(te, 2 + hp)
            for mh in range(2):
                mi = te.matmul(B[2 + hp][:, mh * 256:(mh + 1) * 256],
                               bfm3[:, h, mh * 128:(mh + 1) * 128],
                               cfm3[:, h, :], start=True, stop=True)
            mi.then_inc(s_pe, 1)
            d["cbt"] = s_pe.n
            P.bank_set_w(2 + hp, s_pe)

        def sdiff(h):
            hp = h % 2
            d = pend[h]
            ve.wait(s_pe, d["bc"])
            if hp == 0:
                ve.wait(s_act, P.g("p2_e_act"))
                ve.wait(s_dve, P.g("p2_mt"))
            P.bank_r(ve, hp)
            for mh in range(2):
                ve.scalar_tensor_tensor(
                    A["e0"][:, hp * 512 + mh * 256: hp * 512 + (mh + 1) * 256],
                    B[hp][:, 0:256],
                    A["neg_cum"][:, mh * 32 + h: mh * 32 + h + 1],
                    A["lt2"][:, mh * 256:(mh + 1) * 256],
                    ALU.add, ALU.add).then_inc(s_dve, 1)
            P.bank_add_r(hp, s_dve)
            d["diff"] = s_dve.n

        def s2pair(h):
            grp = h // 8
            gpar = grp % 2
            hr = h % 8
            d = pend[h]
            d1 = pend[h + 1]
            sc.wait(s_dve, d1["diff"])
            sc.activation(A["e0"][:], A["e0"][:], AF.Exp).then_inc(s_act, 1)
            P.s("p2_e_act", s_act.n)
            d["e"] = s_act.n
            sc.wait(s_pool, P.g("p2_expcum_pool"))
            for hh in range(2):
                sc.wait(s_pe, pend[h + hh]["bc"])
                P.bank_r(sc, hh)
                sc.activation(A["expcum"][:, hh * 256:(hh + 1) * 256],
                              B[hh][:, 0:256], AF.Exp).then_inc(s_act, 1)
                P.bank_add_r(hh, s_act)
            d["ec"] = s_act.n
            sc.wait(s_chl, ld_chunk)
            sc.wait(s_act, w_rdy2)
            for hh in range(2):
                hp = hh
                sc.wait(s_pe, P.g("p2_bw_pe_%d" % hp))
                for mh in range(2):
                    sc.activation(A["Bw"][:, hp * 256 + mh * 128: hp * 256 + (mh + 1) * 128],
                                  b_tm[mh][:, (h + hh) * 128:(h + hh + 1) * 128], AF.Copy,
                                  scale=A["w_mh"][:, mh * 32 + h + hh: mh * 32 + h + hh + 1]).then_inc(s_act, 1)
                pend[h + hh]["bw"] = s_act.n
            gp.wait(s_chl, ld_chunk)
            gp.wait(s_act, d["ec"])
            gp.wait(s_sp, s_sp.n)
            gp.wait(s_pe, P.g("p2_ce_pe_%d" % gpar))
            for hh in range(2):
                gp.tensor_mul(ce_st[gpar][:, (hr + hh) * 256:(hr + hh + 1) * 256],
                              cfm3[:, h + hh, :],
                              A["expcum"][:, hh * 256:(hh + 1) * 256]).then_inc(s_pool, 1)
            P.s("p2_expcum_pool", s_pool.n)
            d["ce"] = s_pool.n
            d1["ce"] = s_pool.n
            for hh in range(2):
                hp = hh
                dd = pend[h + hh]
                ve.wait(s_pe, dd["cbt"])
                ve.wait(s_act, d["e"])
                ve.wait(s_pe, P.g("p2_y_pe_%d" % hp))
                P.bank_r(ve, 2 + hp)
                ve.tensor_mul(A["Mt0"][:, hp * 512:(hp + 1) * 512], B[2 + hp][:, 0:512],
                              A["e0"][:, hp * 512:(hp + 1) * 512]).then_inc(s_dve, 1)
                P.bank_add_r(2 + hp, s_dve)
                dd["mt"] = s_dve.n
            P.s("p2_mt", s_dve.n)

        def s3(h):
            hp = h % 2
            grp = h // 8
            gpar = grp % 2
            hr = h % 8
            d = pend[h]
            te.wait(s_dve, d["mt"])
            te.wait(s_pool, d["ce"])
            te.wait(s_dve, h_valid_chunk)
            P.bank_w(te, 4 + hp)
            te.matmul(B[4 + hp][:, 0:256], v_tm[0][:, h * 128:(h + 1) * 128],
                      A["Mt0"][:, hp * 512: hp * 512 + 256], start=True, stop=False)
            te.matmul(B[4 + hp][:, 0:256], v_tm[1][:, h * 128:(h + 1) * 128],
                      A["Mt0"][:, hp * 512 + 256: hp * 512 + 512], start=False, stop=False)
            te.matmul(B[4 + hp][:, 0:256], A["h_bf16"][:, h * 128:(h + 1) * 128],
                      ce_st[gpar][:, hr * 256:(hr + 1) * 256],
                      start=False, stop=True).then_inc(s_pe, 1)
            d["y"] = s_pe.n
            P.s("p2_y_pe_%d" % hp, s_pe.n)
            P.s("p2_ce_pe_%d" % gpar, s_pe.n)
            P.bank_set_w(4 + hp, s_pe)
            te.wait(s_act, d["bw"])
            P.bank_w(te, 6 + hp)
            te.matmul(B[6 + hp][:, 0:128], A["Bw"][:, hp * 256: hp * 256 + 128],
                      v_tm[0][:, h * 128:(h + 1) * 128], start=True, stop=False)
            te.matmul(B[6 + hp][:, 0:128], A["Bw"][:, hp * 256 + 128: hp * 256 + 256],
                      v_tm[1][:, h * 128:(h + 1) * 128], start=False, stop=True).then_inc(s_pe, 1)
            d["st"] = s_pe.n
            P.s("p2_bw_pe_%d" % hp, s_pe.n)
            P.bank_set_w(6 + hp, s_pe)

        def s4(h):
            hp = h % 2
            grp = h // 8
            gpar = grp % 2
            hr = h % 8
            d = pend[h]
            ve.wait(s_pe, d["y"])
            ve.wait(s_sp, s_sp.n)
            P.bank_r(ve, 4 + hp)
            ve.scalar_tensor_tensor(yd_st[gpar][:, hr * 256:(hr + 1) * 256],
                                    vfm3[:, h, :], A["d_param"][:, h:h + 1],
                                    B[4 + hp][:, 0:256], ALU.mult, ALU.add).then_inc(s_dve, 1)
            P.bank_add_r(4 + hp, s_dve)
            P.s("p2_fm_dve", s_dve.n)
            d["yd"] = s_dve.n
            ve.wait(s_pe, d["st"])
            ve.wait(s_act, el_rdy)
            P.bank_r(ve, 6 + hp)
            ve.scalar_tensor_tensor(A["h_f32"][:, h * 128:(h + 1) * 128],
                                    A["h_f32"][:, h * 128:(h + 1) * 128],
                                    A["el_bc"][:, h:h + 1],
                                    B[6 + hp][:, 0:128], ALU.mult, ALU.add).then_inc(s_dve, 1)
            P.bank_add_r(6 + hp, s_dve)
            ve.tensor_copy(A["h_bf16"][:, h * 128:(h + 1) * 128],
                           A["h_f32"][:, h * 128:(h + 1) * 128]).then_inc(s_dve, 1)
            P.s("p2_h_valid", s_dve.n)
            if h % 8 == 7:
                sy.wait(s_dve, d["yd"])
                sy.dma_start(out=A["sp_Yd"][c * 128:(c + 1) * 128, grp * 2048:(grp + 1) * 2048],
                             in_=yd_st[gpar][:]).then_inc(s_sp, 16)
                sy.wait(s_pool, d["ce"])
                sy.wait(s_pe, d["y"])
                sy.dma_start(out=A["sp_Cexp"][c * 128:(c + 1) * 128, grp * 2048:(grp + 1) * 2048],
                             in_=ce_st[gpar][:]).then_inc(s_sp, 16)
            del pend[h]

        s1(0)
        sdiff(0)
        s1(1)
        sdiff(1)
        for p in range(H // 2):
            h = 2 * p
            s2pair(h)
            if h + 2 < H:
                s1(h + 2)
                sdiff(h + 2)
                s1(h + 3)
                sdiff(h + 3)
            s3(h)
            s4(h)
            s3(h + 1)
            s4(h + 1)

        P.s("p2_fm_pe", s_pe.n)
        P.s("p2_fm_pool", s_pool.n)
        P.s("p2_acts_prev", s_act.n)
        P.s("p2_pool_prev", s_pool.n)
    p2_spills = s_sp.n
    p2_h = P.g("p2_h_valid")

    if PHASE_LIMIT < 3:
        sy.wait(s_sp, p2_spills)
        _dump_taps(P, A, S)
        return

    ag3 = A["ag_in"].rearrange("(h n) d -> n h d", n=128)
    h3 = A["h_f32"][:].rearrange("p (h d) -> p h d", h=H)
    gp.wait(s_dve, p2_h)
    for q in range(4):
        sl = slice(q * 8, (q + 1) * 8)
        gp.dma_start(out=ag3[:, sl, :], in_=h3[:, sl, :]).then_inc(s_w, 16)
    ag_in_done = s_w.n
    gp.wait(s_w, ag_in_done)
    gp.collective_compute("AllGather", ALU.bypass,
                          replica_groups=[[0, 1], [2, 3], [4, 5], [6, 7]],
                          ins=[A["ag_in"].opt()], outs=[A["ag_out"].opt()]).then_inc(s_cc, 1)
    ago3 = A["ag_out"][0:4096, :].rearrange("(h n) d -> n h d", n=128)
    sy.wait(s_cc, 1)
    for q in range(4):
        sl = slice(q * 8, (q + 1) * 8)
        sy.dma_start(out=h3[:, sl, :], in_=ago3[:, sl, :]).then_inc(s_ld, 16)
    h8_done = s_ld.n

    yd_l = A["val_fm"]
    ce_l = A["B_fm"]
    gt_l = A["C_fm"]
    gt3 = gt_l[:].rearrange("p (i t) -> p i t", i=32)
    spg3 = A["sp_gate"].rearrange("(i p) t -> p i t", p=128)
    yg_st = [A["yd_st0"], A["ce_st0"]]
    yg_free = [0, 0]
    spyg3 = A["sp_Yg"].rearrange("(i p) t -> p i t", p=128)
    for c in range(NCH):
        sy.wait(s_sp, p2_spills)
        sy.wait(s_pe, P.g("p25_ld_pe"))
        sy.wait(s_dve, P.g("p25_ld_dve"))
        sy.wait(s_pool, P.g("p25_ld_pool"))
        sy.dma_start(out=yd_l[:], in_=A["sp_Yd"][c * 128:(c + 1) * 128, :]).then_inc(s_chl, 16)
        sy.dma_start(out=ce_l[:], in_=A["sp_Cexp"][c * 128:(c + 1) * 128, :]).then_inc(s_chl, 16)
        for q in range(4):
            sl = slice(q * 8, (q + 1) * 8)
            sy.dma_start(out=gt3[:, sl, :], in_=spg3[:, sl, c * CS:(c + 1) * CS]).then_inc(s_chl, 16)
        ld_c = s_chl.n
        ve.wait(s_dve, p2_cum)
        ve.wait(s_ld, h8_done)
        ve.wait(s_pe, P.g("p2_diag_pe"))
        ve.tensor_scalar_mul(A["dtscr"][0:32, 0:32], A["ident"][0:32, 0:32],
                             A["decays"][:, c:c + 1]).then_inc(s_dve, 1)
        diag_rdy = s_dve.n
        te.wait(s_dve, diag_rdy)
        P.bank_w(te, 7)
        te.matmul(B[7][:, 0:32], A["ones32"][:], A["dtscr"][0:32, 0:32],
                  start=True, stop=True).then_inc(s_pe, 1)
        P.s("p2_diag_pe", s_pe.n)
        P.bank_set_w(7, s_pe)
        ve.wait(s_pe, s_pe.n)
        P.bank_r(ve, 7)
        ve.wait(s_act, P.g("p25_dbc_act"))
        ve.tensor_copy(A["lastsbc"][:], B[7][:, 0:32]).then_inc(s_dve, 1)
        P.bank_add_r(7, s_dve)
        sc.wait(s_dve, s_dve.n)
        sc.wait(s_dve, P.g("p25_cl_last"))
        sc.activation(A["D_bc"][:], A["lastsbc"][:], AF.Exp).then_inc(s_act, 1)
        P.s("p25_dbc_act", s_act.n)
        dbc_rdy = s_act.n
        def cl(h):
            hp = h % 4
            ve.wait(s_act, dbc_rdy)
            ve.wait(s_ld, h8_done)
            ve.wait(s_pe, P.g("p25_cl_pe_%d" % hp))
            ve.tensor_scalar(A["clbf"][:, hp * 128:(hp + 1) * 128],
                             A["h_f32"][:, h * 128:(h + 1) * 128],
                             A["flag"][:, 0:1], A["D_bc"][:, h:h + 1],
                             ALU.mult, ALU.mult).then_inc(s_dve, 1)
            P.s("p25_cl_last", s_dve.n)
            P.s("p25_cl_rdy_%d" % hp, s_dve.n)

        def corr(h):
            hp = h % 4
            te.wait(s_dve, P.g("p25_cl_rdy_%d" % hp))
            te.wait(s_chl, ld_c)
            P.bank_w(te, hp)
            te.matmul(B[hp][:, 0:256], A["clbf"][:, hp * 128:(hp + 1) * 128],
                      ce_l[:, h * 256:(h + 1) * 256], start=True, stop=True).then_inc(s_pe, 1)
            P.s("p25_ld_pe", s_pe.n)
            P.s("p25_cl_pe_%d" % hp, s_pe.n)
            P.s("p25_corr_%d" % hp, s_pe.n)
            P.bank_set_w(hp, s_pe)

        def finish(h):
            hp = h % 4
            grp = h // 8
            gpar = grp % 2
            ve.wait(s_pe, P.g("p25_corr_%d" % hp))
            ve.wait(s_pool, P.g("p25_ct_pool_%d" % hp))
            P.bank_r(ve, hp)
            ve.tensor_add(A["ctmp"][:, hp * 256:(hp + 1) * 256],
                          yd_l[:, h * 256:(h + 1) * 256], B[hp][:, 0:256]).then_inc(s_dve, 1)
            P.bank_add_r(hp, s_dve)
            P.s("p25_ld_dve", s_dve.n)
            t_rdy = s_dve.n
            gp.wait(s_dve, t_rdy)
            gp.wait(s_sp, s_sp.n)
            gp.tensor_mul(yg_st[gpar][:, (h % 8) * 256:(h % 8 + 1) * 256],
                          A["ctmp"][:, hp * 256:(hp + 1) * 256],
                          gt3[:, h, :]).then_inc(s_pool, 1)
            P.s("p25_ct_pool_%d" % hp, s_pool.n)
            P.s("p25_ld_pool", s_pool.n)
            yg_rdy = s_pool.n
            if h % 8 == 7:
                sy.wait(s_pool, yg_rdy)
                src = yg_st[gpar][:].rearrange("p (i t) -> p i t", i=8)
                sy.dma_start(out=spyg3[:, grp * 8:(grp + 1) * 8, c * CS:(c + 1) * CS],
                             in_=src).then_inc(s_sp, 16)

        for h0 in range(4):
            cl(h0)
        for h in range(H):
            corr(h)
            finish(h)
            if h + 4 < H:
                cl(h + 4)
    p25_spills = s_sp.n

    if PHASE_LIMIT < 4:
        sy.wait(s_sp, p25_spills)
        _dump_taps(P, A, S)
        return

    sy.wait(s_sp, p25_spills)
    sy.wait(s_pool, s_pool.n)
    sy.wait(s_dve, s_dve.n)
    sy.wait(s_pe, s_pe.n)
    sy.wait(S["swc"], P.g("sw1b_rdy"))
    s_wh = [S["wh0"], S["wh1"]]
    for k in range(32):
        sy.dma_start(out=A["wot"][:, k * 2048:(k + 1) * 2048],
                     in_=A["d_woutb"][k * 128:(k + 1) * 128, :]).then_inc(s_wh[0], 16)
    wot_rdy = s_wh[0].n
    ygb = [A["yg_t"], A["yg_t2"]]
    for tt in range(16):
        par = tt % 2
        sy.wait(s_sp, p25_spills)
        sy.wait(s_pe, P.g("p3_mm_%d" % par))
        ygv = ygb[par][:].rearrange("p (i t) -> p i t", i=32)
        for q in range(4):
            sl = slice(q * 8, (q + 1) * 8)
            sy.dma_start(out=ygv[:, sl, :], in_=spyg3[:, sl, tt * 128:(tt + 1) * 128]).then_inc(s_chl, 16)
        sy.wait(s_dve, P.g("p3_x1_rdy"))
        sy.dma_start(out=A["xtm_b"][:], in_=A["in_x_tm"][tt * 128:(tt + 1) * 128, :]).then_inc(s_chl, 16)
        yg_rdy2 = s_chl.n
        te.wait(s_chl, yg_rdy2)
        te.wait(s_wh[0], wot_rdy)
        mmd = [0] * 4
        for dmg in range(4):
            P.bank_w(te, dmg)
            for k in range(32):
                mi = te.matmul(B[dmg][:], ygv[:, k, :],
                               A["wot"][:, k * 2048 + dmg * 512: k * 2048 + (dmg + 1) * 512],
                               start=(k == 0), stop=(k == 31))
            mi.then_inc(s_pe, 1)
            mmd[dmg] = s_pe.n
            P.bank_set_w(dmg, s_pe)
        P.s("p3_mm_%d" % par, s_pe.n)
        for dmg in range(4):
            ve.wait(s_pe, mmd[dmg])
            P.bank_r(ve, dmg)
            ve.scalar_tensor_tensor(A["xtm_b"][:, dmg * 512:(dmg + 1) * 512],
                                    B[dmg][:], 1.0,
                                    A["xtm_b"][:, dmg * 512:(dmg + 1) * 512],
                                    ALU.mult, ALU.add).then_inc(s_dve, 1)
            P.bank_add_r(dmg, s_dve)
        ve.reduce_sum(A["ln_stats"][:, 0:1], A["xtm_b"][:], axis=AX.X).then_inc(s_dve, 1)
        x2_rdy = s_dve.n
        sc.wait(s_dve, x2_rdy)
        sc.activation(A["sq_b"][:], A["xtm_b"][:], AF.Square,
                      accum_out=A["ln_stats"][:, 1:2]).then_inc(s_act, 1)
        sq_rdy = s_act.n
        ve.wait(s_act, sq_rdy)
        ve.tensor_scalar_mul(A["ln_stats"][:, 2:3], A["ln_stats"][:, 0:1], 1.0 / DM).then_inc(s_dve, 1)
        ve.tensor_scalar_mul(A["ln_stats"][:, 3:4], A["ln_stats"][:, 1:2], 1.0 / DM).then_inc(s_dve, 1)
        ve.tensor_mul(A["ln_stats"][:, 4:5], A["ln_stats"][:, 2:3], A["ln_stats"][:, 2:3]).then_inc(s_dve, 1)
        ve.tensor_sub(A["ln_stats"][:, 4:5], A["ln_stats"][:, 3:4], A["ln_stats"][:, 4:5]).then_inc(s_dve, 1)
        ve.tensor_scalar_add(A["ln_stats"][:, 4:5], A["ln_stats"][:, 4:5], EPS).then_inc(s_dve, 1)
        var_rdy = s_dve.n
        sc.wait(s_dve, var_rdy)
        sc.activation(A["ln_stats"][:, 5:6], A["ln_stats"][:, 4:5], AF.Ln).then_inc(s_act, 1)
        ln_rdy = s_act.n
        ve.wait(s_act, ln_rdy)
        ve.tensor_scalar_mul(A["ln_stats"][:, 5:6], A["ln_stats"][:, 5:6], -0.5).then_inc(s_dve, 1)
        nl_rdy = s_dve.n
        sc.wait(s_dve, nl_rdy)
        sc.activation(A["ln_stats"][:, 6:7], A["ln_stats"][:, 5:6], AF.Exp).then_inc(s_act, 1)
        rstd_rdy = s_act.n
        ve.wait(s_act, rstd_rdy)
        ve.wait(s_sp, s_sp.n)
        ve.wait(s_spg, s_spg.n)
        ve.tensor_scalar(A["x1stage"][:], A["xtm_b"][:], A["ln_stats"][:, 2:3],
                         A["ln_stats"][:, 6:7], ALU.subtract, ALU.mult).then_inc(s_dve, 1)
        x1_rdy = s_dve.n
        P.s("p3_x1_rdy", x1_rdy)
        sy.wait(s_dve, x1_rdy)
        sy.dma_start(out=A["sp_x1f"][tt * 128:(tt + 1) * 128, :], in_=A["x1stage"][:]).then_inc(s_sp, 16)
        gp.wait(s_dve, x1_rdy)
        gp.dma_start(out=A["sp_x1bf"][tt * 128:(tt + 1) * 128, :], in_=A["x1stage"][:]).then_inc(s_spg, 16)
        P.s("p3_x1_sp", s_sp.n)
        P.s("p3_x1_spg", s_spg.n)
    p3_done = s_sp.n
    p3_done_g = s_spg.n

    if PHASE_LIMIT < 5:
        sy.wait(s_sp, p3_done)
        _dump_taps(P, A, S)
        return

    for i in range(16):
        sy.wait(s_sp, p3_done)
        sy.wait(s_spg, p3_done_g)
        sy.dma_start(out=A["x1fm"][:, i * 2048:(i + 1) * 2048],
                     in_=A["sp_x1bf"][:, i * 128:(i + 1) * 128], transpose=True).then_inc(s_chl, 16)
    x1fm_rdy = s_chl.n
    gp.dma_start(out=A["gwbf"][:].rearrange("p (k e) -> p k e", k=16),
                 in_=A["in_gate_w"].rearrange("(k p) e -> p k e", p=128)).then_inc(s_w, 16)
    gw_rdy = s_w.n
    x1v = A["x1fm"][:].rearrange("p (k t) -> p k t", k=16)
    gwv = A["gwbf"][:].rearrange("p (k e) -> p k e", k=16)
    g = A["gtmp"]
    g2 = A["gtmp2"]
    for tt in range(16):
        te.wait(s_chl, x1fm_rdy)
        te.wait(s_w, gw_rdy)
        P.bank_w(te, 2)
        for k in range(16):
            mi = te.matmul(B[2][:, 0:8], x1v[:, k, tt * 128:(tt + 1) * 128], gwv[:, k, :],
                           start=(k == 0), stop=(k == 15))
        mi.then_inc(s_pe, 1)
        lg_done = s_pe.n
        P.bank_set_w(2, s_pe)
        ve.wait(s_pe, lg_done)
        ve.wait(s_act, P.g("p4_g_act"))
        P.bank_r(ve, 2)
        ve.tensor_copy(g[:, 0:8], B[2][:, 0:8]).then_inc(s_dve, 1)
        P.bank_add_r(2, s_dve)
        ve.reduce_max(g[:, 8:9], g[:, 0:8], axis=AX.X).then_inc(s_dve, 1)
        ve.tensor_scalar(g2[:, 0:8], g[:, 0:8], g[:, 8:9], None, ALU.is_ge).then_inc(s_dve, 1)
        ve.scalar_tensor_tensor(g2[:, 8:16], g2[:, 0:8], -1e30, g[:, 0:8],
                                ALU.mult, ALU.add).then_inc(s_dve, 1)
        ve.reduce_max(g[:, 9:10], g2[:, 8:16], axis=AX.X).then_inc(s_dve, 1)
        ve.tensor_scalar_mul(g[:, 10:11], g[:, 8:9], -1.0).then_inc(s_dve, 1)
        ve.tensor_sub(g[:, 11:12], g[:, 9:10], g[:, 8:9]).then_inc(s_dve, 1)
        pre_e = s_dve.n
        sc.wait(s_dve, pre_e)
        sc.activation(g2[:, 0:8], g[:, 0:8], AF.Exp, bias=g[:, 10:11]).then_inc(s_act, 1)
        sc.activation(g[:, 12:13], g[:, 11:12], AF.Exp).then_inc(s_act, 1)
        P.s("p4_g_act", s_act.n)
        e_done2 = s_act.n
        ve.wait(s_act, e_done2)
        ve.tensor_scalar_add(g[:, 12:13], g[:, 12:13], 1.0).then_inc(s_dve, 1)
        ve.reciprocal(g[:, 13:14], g[:, 12:13]).then_inc(s_dve, 1)
        ve.tensor_scalar(g2[:, 8:16], g[:, 0:8], g[:, 9:10], None, ALU.is_ge).then_inc(s_dve, 1)
        ve.tensor_scalar(g2[:, 0:8], g2[:, 0:8], g[:, 13:14], None, ALU.mult).then_inc(s_dve, 1)
        ve.tensor_mul(g[:, 0:8], g2[:, 0:8], g2[:, 8:16]).then_inc(s_dve, 1)
        gates_rdy = s_dve.n
        te.wait(s_dve, gates_rdy)
        P.bank_w(te, 3)
        te.transpose(B[3][0:8, 0:128], g[:, 0:8], A["ident"][:]).then_inc(s_pe, 1)
        gtr_done = s_pe.n
        P.bank_set_w(3, s_pe)
        ve.wait(s_pe, gtr_done)
        P.bank_r(ve, 3)
        ve.tensor_copy(A["g_rows"][:, tt * 128:(tt + 1) * 128], B[3][0:8, 0:128]).then_inc(s_dve, 1)
        P.bank_add_r(3, s_dve)
    grows_rdy = s_dve.n

    sy.dma_start(out=A["d_gidx"], in_=A["in_gidx_init"]).then_inc(s_gx, 16)
    for nm in ["l8u", "srow16", "iota_t", "iota_w"]:
        sy.dma_start(out=A[nm][:], in_=A["in_" + nm]).then_inc(s_gx, 16)
    rt_consts = s_gx.n
    gp.memset(A["ones1f"][:], 1.0).then_inc(s_pool, 1)
    ones1f_rdy = s_pool.n

    rr = A["rts"]
    ve.wait(s_dve, grows_rdy)
    ve.tensor_scalar(A["ind8"][:], A["g_rows"][:], 0.0, None, ALU.is_gt).then_inc(s_dve, 1)
    ve.memset(A["ones8r"][:], 1.0).then_inc(s_dve, 1)
    ve.tensor_tensor_scan(A["pos8"][:], A["ones8r"][:], A["ind8"][:], 0.0,
                          ALU.mult, ALU.add).then_inc(s_dve, 1)
    ve.tensor_copy(rr[:, 0:1], A["pos8"][:, 2047:2048]).then_inc(s_dve, 1)
    ve.wait(s_gx, rt_consts)
    ve.tensor_scalar(A["cmp16"][:], A["srow16"][:], rr[:, 0:1], None, ALU.is_lt).then_inc(s_dve, 1)
    ve.tensor_reduce(rr[:, 1:2], A["cmp16"][:], AX.X, ALU.add).then_inc(s_dve, 1)
    ve.tensor_scalar_mul(rr[:, 2:3], rr[:, 1:2], 512.0).then_inc(s_dve, 1)
    c128_rdy = s_dve.n
    te.wait(s_dve, c128_rdy)
    te.wait(s_gx, rt_consts)
    P.bank_w(te, 6)
    te.matmul(B[6][0:8, 0:1], A["l8u"][:], rr[:, 2:3], start=True, stop=True).then_inc(s_pe, 1)
    P.bank_set_w(6, s_pe)
    ve.wait(s_pe, s_pe.n)
    P.bank_r(ve, 6)
    ve.tensor_copy(rr[:, 3:4], B[6][0:8, 0:1]).then_inc(s_dve, 1)
    P.bank_add_r(6, s_dve)
    ve.tensor_add(rr[:, 4:5], rr[:, 3:4], rr[:, 2:3]).then_inc(s_dve, 1)
    ve.tensor_scalar(A["pos8"][:], A["pos8"][:], rr[:, 3:4], -1.0, ALU.add, ALU.add).then_inc(s_dve, 1)
    ve.tensor_scalar(A["ind8"][:], A["ind8"][:], -1e9, 1e9, ALU.mult, ALU.add).then_inc(s_dve, 1)
    ve.tensor_add(A["pos8"][:], A["pos8"][:], A["ind8"][:]).then_inc(s_dve, 1)
    ve.tensor_scalar(A["cmp16"][:], A["srow16"][:], rr[:, 4:5], None, ALU.is_ge).then_inc(s_dve, 1)
    slot_rdy = s_dve.n
    te.wait(s_dve, slot_rdy)
    P.bank_w(te, 6)
    te.matmul(B[6][0:1, 0:16], A["ones32"][0:8, 0:1], A["cmp16"][:],
              start=True, stop=True).then_inc(s_pe, 1)
    P.bank_set_w(6, s_pe)
    ve.wait(s_pe, s_pe.n)
    P.bank_r(ve, 6)
    ve.tensor_copy(A["egrow"][:], B[6][0:1, 0:16]).then_inc(s_dve, 1)
    P.bank_add_r(6, s_dve)
    ve.tensor_scalar_mul(A["wbrow"][:], A["egrow"][:], 128.0).then_inc(s_dve, 1)
    et_rdy = s_dve.n
    te.wait(s_dve, et_rdy)
    te.wait(s_pool, ones1f_rdy)
    P.bank_w(te, 7)
    te.matmul(B[7][:, 0:16], A["ones1f"][:], A["wbrow"][:], start=True, stop=True).then_inc(s_pe, 1)
    P.bank_set_w(7, s_pe)
    ve.wait(s_pe, s_pe.n)
    P.bank_r(ve, 7)
    ve.tensor_copy(A["wb_bc"][:], B[7][:, 0:16]).then_inc(s_dve, 1)
    P.bank_add_r(7, s_dve)
    wbbc_rdy = s_dve.n

    slot_tm = [A["slot_tm0"], A["slot_tm1"]]
    gate_tm = [A["gate_tm0"], A["gate_tm1"]]
    stage2 = [A["stage2a"], A["stage2b"], A["stage2c"], A["stage2d"]]
    s_scat = [s_ig, S["e8a"]]
    g1t = A["gtmp"]
    for tt in range(16):
        par = tt % 2
        te.wait(s_dve, slot_rdy)
        te.wait(s_dve, P.g("p4_rt_ve_%d" % par))
        P.bank_w(te, 6)
        te.transpose(B[6][0:128, 0:8], A["pos8"][:, tt * 128:(tt + 1) * 128],
                     A["ident"][0:8, 0:8]).then_inc(s_pe, 1)
        P.bank_set_w(6, s_pe)
        tr1 = s_pe.n
        P.bank_w(te, 7)
        te.transpose(B[7][0:128, 0:8], A["g_rows"][:, tt * 128:(tt + 1) * 128],
                     A["ident"][0:8, 0:8]).then_inc(s_pe, 1)
        P.bank_set_w(7, s_pe)
        tr2 = s_pe.n
        ve.wait(s_pe, tr2)
        P.bank_r(ve, 6)
        ve.tensor_copy(slot_tm[par][:], B[6][0:128, 0:8]).then_inc(s_dve, 1)
        P.bank_add_r(6, s_dve)
        P.bank_r(ve, 7)
        ve.tensor_copy(gate_tm[par][:], B[7][0:128, 0:8]).then_inc(s_dve, 1)
        P.bank_add_r(7, s_dve)
        g2t = A["gtmp2"]
        ve.tensor_reduce(g1t[:, 0:1], slot_tm[par][:], AX.X, ALU.min).then_inc(s_dve, 1)
        ve.tensor_scalar(g1t[:, 2:10], slot_tm[par][:], g1t[:, 0:1], None, ALU.is_le).then_inc(s_dve, 1)
        ve.scalar_tensor_tensor(g2t[:, 0:8], g1t[:, 2:10], 1e9, slot_tm[par][:],
                                ALU.mult, ALU.add).then_inc(s_dve, 1)
        ve.tensor_reduce(g1t[:, 1:2], g2t[:, 0:8], AX.X, ALU.min).then_inc(s_dve, 1)
        ve.tensor_copy(A["s1all"][:, tt:tt + 1], g1t[:, 0:1]).then_inc(s_dve, 1)
        ve.tensor_copy(A["s2all"][:, tt:tt + 1], g1t[:, 1:2]).then_inc(s_dve, 1)
        ve.tensor_scalar(g2t[:, 8:16], slot_tm[par][:], g1t[:, 1:2], None, ALU.is_le).then_inc(s_dve, 1)
        ve.tensor_sub(g2t[:, 8:16], g2t[:, 8:16], g1t[:, 2:10]).then_inc(s_dve, 1)
        ve.tensor_mul(g1t[:, 2:10], g1t[:, 2:10], gate_tm[par][:]).then_inc(s_dve, 1)
        ve.tensor_mul(g2t[:, 8:16], g2t[:, 8:16], gate_tm[par][:]).then_inc(s_dve, 1)
        scq = tt % 2
        ic0 = 2 * scq
        ve.wait(s_scat[scq], P.g("p4_scat_%d" % scq))
        ve.wait(s_gx, rt_consts)
        ve.tensor_copy(A["islot"][:, ic0:ic0 + 1], g1t[:, 0:1]).then_inc(s_dve, 1)
        ve.tensor_copy(A["islot"][:, ic0 + 1:ic0 + 2], g1t[:, 1:2]).then_inc(s_dve, 1)
        ve.tensor_copy(stage2[2 * scq][:, 0:1], A["iota_t"][:, tt:tt + 1]).then_inc(s_dve, 1)
        ve.tensor_reduce(stage2[2 * scq][:, 1:2], g1t[:, 2:10], AX.X, ALU.add).then_inc(s_dve, 1)
        ve.tensor_copy(stage2[2 * scq + 1][:, 0:1], A["iota_t"][:, tt:tt + 1]).then_inc(s_dve, 1)
        ve.tensor_reduce(stage2[2 * scq + 1][:, 1:2], g2t[:, 8:16], AX.X, ALU.add).then_inc(s_dve, 1)
        st_rdy = s_dve.n
        gp.wait(s_dve, st_rdy)
        gp.wait(s_gx, rt_consts)
        if tt == 0:
            gp.reg_mov(A["reg_bchk"], SLOTS - 1)
        for j in range(2):
            gp.indirect_dma_start(
                out=A["d_gidx"],
                out_offset=bass.IndirectOffsetOnAxis(A["islot"][:, ic0 + j:ic0 + j + 1], 0),
                in_=stage2[2 * scq + j][:], in_offset=None,
                bounds_check=A["reg_bchk"], oob_is_err=False).then_inc(s_scat[scq], 16)
        P.s("p4_scat_%d" % scq, s_scat[scq].n)
    scat_done = [s_scat[0].n, s_scat[1].n]

    s_gst = [S["gst"], S["gse"]]
    w1t = [A["w1t0"], A["w1t1"]]
    ghst = [A["ghe_st0"], A["ghe_st1"]]
    w1_free = [0, 0]
    for rs in range(16):
        par = rs % 2
        sy.wait(s_pe, w1_free[par])
        if rs == 0:
            sy.wait(S["swc"], P.g("sw1b_rdy"))
        sy.dma_start(out=w1t[par][:], in_=A["d_sw1b"][rs * 128:(rs + 1) * 128, :]).then_inc(s_wh[par], 16)
        w1_rdy = s_wh[par].n
        sc.wait(s_gst[par], P.g("p4_ghst_%d" % par))
        for tg in range(4):
            bk = tg % 2
            te.wait(s_wh[par], w1_rdy)
            te.wait(s_chl, x1fm_rdy)
            P.bank_w(te, bk)
            for k in range(16):
                mi = te.matmul(B[bk][:], w1t[par][:, k * 128:(k + 1) * 128],
                               x1v[:, k, tg * 512:(tg + 1) * 512], start=(k == 0), stop=(k == 15))
            mi.then_inc(s_pe, 1)
            h_done = s_pe.n
            P.bank_set_w(bk, s_pe)
            if tg == 3:
                w1_free[par] = s_pe.n
            sc.wait(s_pe, h_done)
            P.bank_r(sc, bk)
            sc.activation(ghst[par][:, tg * 512:(tg + 1) * 512], B[bk][:], AF.Gelu,
                          bias=A["sb1"][:, rs:rs + 1]).then_inc(s_act, 1)
            P.bank_add_r(bk, s_act)
        sc.wait(s_act, s_act.n)
        sc.dma_start(out=A["sp_ghe"][rs * 128:(rs + 1) * 128, :], in_=ghst[par][:]).then_inc(s_gst[par], 16)
        P.s("p4_ghst_%d" % par, s_gst[par].n)
    shared_done = [s_gst[0].n, s_gst[1].n]
    P.s("p4_wc_pe_0", s_pe.n)
    P.s("p4_wc_pe_1", s_pe.n)

    gp.wait(s_ig, scat_done[0])
    gp.wait(S["e8a"], scat_done[1])
    gp.dma_start(out=A["gixall"][:].rearrange("p (s c) -> p s c", c=2),
                 in_=bass.AP(A["d_gidx"].tensor, 0,
                             [[2, 128], [256, NST], [1, 2]])).then_inc(s_ig, 16)
    gix_rdy = s_ig.n
    ve.wait(s_ig, gix_rdy)
    ve.tensor_copy(A["giiall"][:], A["gixall"][:]).then_inc(s_dve, 1)
    ve.tensor_copy(A["gvall"][:].rearrange("p (s o) -> p s o", o=1),
                   A["gixall"][:].rearrange("p (s c) -> p s c", c=2)[:, :, 1:2]).then_inc(s_dve, 1)
    cv_rdy = s_dve.n
    P.s("p4_cv", cv_rdy)
    gp.wait(s_dve, cv_rdy)
    gp.reg_mov(A["reg_bchk"], T - 1)
    slot = [s_ig, S["e8a"], S["e8b"], S["b8b"],
            S["g4"], S["g5"], S["g6"], S["g7"]]
    ssp = [S["gt%d" % i] for i in range(NSPL)]
    assert NSPL == 10
    xg_sb = [A["xg_sb%d" % i] for i in range(8)]
    g_cnt = {}
    st_cnt = {}
    LAG = 7
    for it in range(NST + LAG):
        si = it
        if si < NST:
            q = si % 8
            if si >= 8:
                gp.wait(ssp[(si - 8) // TPS], st_cnt[si - 8])
            gp.indirect_dma_start(
                out=xg_sb[q][:], out_offset=None,
                in_=A["sp_x1bf"],
                in_offset=bass.IndirectOffsetOnAxis(A["giiall"][:, 2 * si:2 * si + 1], 0),
                bounds_check=A["reg_bchk"], oob_is_err=False).then_inc(slot[q], 16)
            g_cnt[si] = slot[q].n
        sj = it - LAG
        if 0 <= sj < NST:
            q = sj % 8
            gp.wait(slot[q], g_cnt[sj])
            gp.dma_start(out=A["d_xg"][sj * 128:(sj + 1) * 128, :],
                         in_=xg_sb[q][:]).then_inc(ssp[sj // TPS], 16)
            st_cnt[sj] = ssp[sj // TPS].n
    split_cnt = [st_cnt[s5 * TPS + TPS - 1] for s5 in range(NSPL)]
    xtb = [A["xts0a"], A["xts0b"], A["xts0c"], A["xts0d"]]
    s_xt4 = [S["xts"], S["xto"], S["xss0"], S["xss1"]]
    s_ld4 = [s_gx, S["gxo"], S["gxs0"], S["gxs1"]]
    qeng = [sy, sc]
    ld_cnt = {}
    xtr_done = [None] * NSPL

    def _xt_store(j):
        qj = j % 2
        sl = qj * 2 + (j // 2) % 2
        qeng[qj].wait(s_ld4[sl], ld_cnt[j])
        s5j, ij = j // 16, j % 16
        qeng[qj].dma_start(
            out=A["d_xgT"][ij * 128:(ij + 1) * 128, s5j * SSL:(s5j + 1) * SSL],
            in_=xtb[sl][:, 0:SSL]).then_inc(s_xt4[sl], 16)
        P.s("p4_xtb_%d" % sl, s_xt4[sl].n)
        if ij == 15:
            xtr_done[s5j] = tuple(x.n for x in s_xt4)

    for idx in range(NSPL * 16):
        s5, i = idx // 16, idx % 16
        q = idx % 2
        sl = q * 2 + (idx // 2) % 2
        qeng[q].wait(ssp[s5], split_cnt[s5])
        qeng[q].wait(s_xt4[sl], P.g("p4_xtb_%d" % sl))
        qeng[q].dma_start(out=xtb[sl][:, 0:SSL],
                          in_=A["d_xg"][s5 * SSL:(s5 + 1) * SSL, i * 128:(i + 1) * 128],
                          transpose=True).then_inc(s_ld4[sl], 16)
        ld_cnt[idx] = s_ld4[sl].n
        if idx >= 2:
            _xt_store(idx - 2)
    _xt_store(NSPL * 16 - 2)
    _xt_store(NSPL * 16 - 1)

    w1c = [[A["w1cA0"], A["w1cA1"]], [A["w1cB0"], A["w1cB1"]]]
    w2c = [[A["w2cA0"], A["w2cA1"]], [A["w2cB0"], A["w2cB1"]]]
    h_fm = [A["h_fm0"], A["h_fm1"]]
    osb = [A["osb0"], A["osb1"]]
    wii = [A["wii0"], A["wii1"]]
    s_m8 = [S["m8a"], S["m8b"]]
    s_os = [S["os0"], S["os1"]]
    s_xq = [s_gh[0], S["xq1"], S["xq2"], S["xq3"]]
    xv3 = A["xgg"][:].rearrange("p (k t) -> p k t", k=16)
    dxt3 = A["d_xgT"].rearrange("(k p) t -> p k t", p=128)
    for si in range(NST + 1):
      if si < NST:
        spar = si % 2
        grp, st = si // 4, si % 4
        gpar = grp % 2
        if st == 0:
            for _q in range(4):
                sy.wait(s_xt4[_q], xtr_done[(grp * 512 + 511) // SSL][_q])
            sy.wait(s_pe, P.g("p4_xgg_pe"))
            for q in range(4):
                sy.dma_start(out=xv3[:, q * 4:(q + 1) * 4, :],
                             in_=dxt3[:, q * 4:(q + 1) * 4, grp * 512:(grp + 1) * 512]).then_inc(s_xq[q], 16)
                P.s("p4_xgq_%d" % q, s_xq[q].n)
            ve.wait(s_dve, wbbc_rdy)
            ve.wait(s_m8[gpar], P.g("p4_m8b_%d" % gpar))
            ve.tensor_scalar_add(A["wif"][:, 0:1], A["iota_w"][:, 0:1],
                                 A["wb_bc"][:, grp:grp + 1]).then_inc(s_dve, 1)
            ve.tensor_scalar_mul(A["wif"][:, 1:2], A["wb_bc"][:, grp:grp + 1],
                                 1.0 / 128.0).then_inc(s_dve, 1)
            ve.tensor_copy(wii[gpar][:, 0:2], A["wif"][:, 0:2]).then_inc(s_dve, 1)
            wi_rdy = s_dve.n
            gp.wait(s_dve, wi_rdy)
            if si == 0:
                gp.wait(s_wc, P.g("wb_cast"))
                gp.wait(s_gst[0], shared_done[0])
                gp.wait(s_gst[1], shared_done[1])
                gp.reg_mov(A["reg_bchk"], 1023)
            gp.wait(s_pe, P.g("p4_wc_pe_%d" % gpar))
            for half in range(2):
                gp.indirect_dma_start(
                    out=bass.AP(w1c[half][gpar].tensor, 0, [[8192, 128], [1, 8192]]),
                    out_offset=None,
                    in_=A["d_w1p"], in_offset=bass.IndirectOffsetOnAxis(wii[gpar][:, 0:1], 0),
                    element_offset=half * 8192,
                    bounds_check=A["reg_bchk"], oob_is_err=False).then_inc(s_m8[gpar], 16)
                gp.indirect_dma_start(
                    out=bass.AP(w2c[half][gpar].tensor, 0, [[8192, 128], [1, 8192]]),
                    out_offset=None,
                    in_=A["d_w2p"], in_offset=bass.IndirectOffsetOnAxis(wii[gpar][:, 0:1], 0),
                    element_offset=half * 8192,
                    bounds_check=A["reg_bchk"], oob_is_err=False).then_inc(s_m8[gpar], 16)
            gp.wait(s_pe, P.g("p4_xgg_pe"))
            gp.indirect_dma_start(
                out=A["b1b"][:], out_offset=None,
                in_=A["d_b1b"], in_offset=bass.IndirectOffsetOnAxis(wii[gpar][:, 1:2], 0),
                bounds_check=A["reg_bchk"], oob_is_err=False).then_inc(s_m8[gpar], 16)
            P.s("p4_m8_%d" % gpar, s_m8[gpar].n)
        if st == 1:
            gp.wait(s_m8[gpar], P.g("p4_m8_%d" % gpar))
            gp.wait(s_pe, P.g("p4_w2pe_%d" % spar))
            gp.indirect_dma_start(
                out=A["b2b"][:], out_offset=None,
                in_=A["d_b2b"], in_offset=bass.IndirectOffsetOnAxis(wii[gpar][:, 1:2], 0),
                bounds_check=A["reg_bchk"], oob_is_err=False).then_inc(s_m8[gpar], 16)
            P.s("p4_m8b_%d" % gpar, s_m8[gpar].n)
        te.wait(s_m8[gpar], P.g("p4_m8_%d" % gpar))
        for bk in range(4, 8):
            P.bank_w(te, bk)
        for k in range(16):
            if k % 4 == 0:
                te.wait(s_xq[k // 4], P.g("p4_xgq_%d" % (k // 4)))
            for ht in range(8):
                half, hq = ht // 4, ht % 4
                bk = 4 + ht // 2
                c0 = (ht % 2) * 256
                te.matmul(B[bk][:, c0:c0 + 128],
                          w1c[half][gpar][:, hq * 2048 + k * 128: hq * 2048 + (k + 1) * 128],
                          xv3[:, k, st * 128:(st + 1) * 128],
                          start=(k == 0 and ht % 2 == 0), stop=False)
        for ht in range(8):
            bk = 4 + ht // 2
            c0 = (ht % 2) * 256
            mi = te.matmul(B[bk][:, c0:c0 + 128],
                           A["b1b"][0:1, ht * 128:(ht + 1) * 128],
                           A["ones1bf"][:], start=False, stop=(ht % 2 == 1))
        mi.then_inc(s_pe, 1)
        h_all = s_pe.n
        for bk in range(4, 8):
            P.bank_set_w(bk, s_pe)
        if st == 3:
            P.s("p4_xgg_pe", s_pe.n)
        sc.wait(s_pe, P.g("p4_w2pe_%d" % spar))
        sc.wait(s_pe, h_all)
        for ht in range(8):
            bk = 4 + ht // 2
            c0 = (ht % 2) * 256
            P.bank_r(sc, bk)
            sc.activation(h_fm[spar][:, ht * 128:(ht + 1) * 128], B[bk][:, c0:c0 + 128],
                          AF.Gelu).then_inc(s_act, 1)
            P.bank_add_r(bk, s_act)
        P.s("p4_gl_%d" % spar, s_act.n)
      sj = si - 1
      if 0 <= sj < NST:
        sparj = sj % 2
        grpj, stj = sj // 4, sj % 4
        gparj = grpj % 2
        te.wait(s_act, P.g("p4_gl_%d" % sparj))
        te.wait(s_m8[gparj], P.g("p4_m8b_%d" % gparj))
        for dmg in range(4):
            P.bank_w(te, dmg)
        for ht in range(8):
            half, hq = ht // 4, ht % 4
            for dmg in range(4):
                te.matmul(B[dmg][:], h_fm[sparj][:, ht * 128:(ht + 1) * 128],
                          w2c[half][gparj][:, hq * 2048 + dmg * 512: hq * 2048 + (dmg + 1) * 512],
                          start=(ht == 0), stop=False)
        for dmg in range(4):
            mi = te.matmul(B[dmg][:], A["ones1bf"][:],
                           A["b2b"][0:1, dmg * 512:(dmg + 1) * 512], start=False, stop=True)
            mi.then_inc(s_pe, 1)
            P.s("p4_o_done_%d" % dmg, s_pe.n)
            P.bank_set_w(dmg, s_pe)
        P.s("p4_w2pe_%d" % sparj, s_pe.n)
        if stj == 3:
            P.s("p4_wc_pe_%d" % gparj, s_pe.n)
        ve.wait(s_dve, P.g("p4_cv"))
        ve.wait(s_os[sparj], P.g("p4_ost_%d" % sparj))
        for dmg in range(4):
            ve.wait(s_pe, P.g("p4_o_done_%d" % dmg))
            P.bank_r(ve, dmg)
            ve.tensor_scalar_mul(osb[sparj][:, dmg * 512:(dmg + 1) * 512], B[dmg][:],
                                 A["gvall"][:, sj:sj + 1]).then_inc(s_dve, 1)
            P.bank_add_r(dmg, s_dve)
        o_rdy = s_dve.n
        sy.wait(s_dve, o_rdy)
        sy.dma_start(out=A["d_rout2"][sj * 128:(sj + 1) * 128, :],
                     in_=osb[sparj][:]).then_inc(s_os[sparj], 16)
        P.s("p4_ost_%d" % sparj, s_os[sparj].n)
    p4_done = [s_os[0].n, s_os[1].n]
    p4_pe = s_pe.n

    if PHASE_LIMIT < 6:
        sy.wait(s_os[0], p4_done[0])
        sy.wait(s_os[1], p4_done[1])
        _dump_taps(P, A, S)
        return

    w2k = [A["w2k0"], A["w2k1"]]
    ghk = [A["ghk0"], A["ghk1"]]
    w2_free = [0, 0]
    gh2_free = [0, 0]
    spghe3 = A["sp_ghe"].rearrange("(r p) t -> p r t", p=128)
    accv = A["out_acc"][:].rearrange("p (tt d) -> p tt d", tt=8)
    rgb = [A["r1sb"], A["r2sb"]]
    for th in range(2):
        sy.wait(s_dve, P.g("p5_acc_free"))
        sy.wait(s_os[0], p4_done[0])
        sy.wait(s_os[1], p4_done[1])
        for tt in range(8):
            row = th * 1024 + tt * 128
            sy.dma_start(out=accv[:, tt, :], in_=A["sp_x1f"][row:row + 128, :]).then_inc(s_chl, 16)
        acc_init = s_chl.n
        for kb in range(NKB):
            par = kb % 2
            sy.wait(s_pe, w2_free[par] if (th > 0 or kb >= 2) else p4_pe)
            for k in range(KB):
                row = kb * 1024 + k * 128
                sy.dma_start(out=w2k[par][:, k * 2048:(k + 1) * 2048],
                             in_=A["d_w2shb"][row:row + 128, :]).then_inc(s_wh[par], 16)
            w2_rdy = s_wh[par].n
            sy.wait(s_gst[0], shared_done[0])
            sy.wait(s_gst[1], shared_done[1])
            sy.wait(s_pe, gh2_free[par])
            sy.dma_start(out=ghk[par][:].rearrange("p (k t) -> p k t", k=KB),
                         in_=spghe3[:, kb * KB:(kb + 1) * KB, th * 1024:(th + 1) * 1024]).then_inc(s_gh[par], 16)
            gh_rdy = s_gh[par].n
            ghv = ghk[par][:].rearrange("p (k t) -> p k t", k=KB)
            for tt in range(8):
                for dmg in range(4):
                    bk = (tt * 4 + dmg) % 2
                    te.wait(s_wh[par], w2_rdy)
                    te.wait(s_gh[par], gh_rdy)
                    P.bank_w(te, bk)
                    n_extra = 1 if kb == 0 else 0
                    for k in range(KB):
                        mi = te.matmul(B[bk][:], ghv[:, k, tt * 128:(tt + 1) * 128],
                                       w2k[par][:, k * 2048 + dmg * 512: k * 2048 + (dmg + 1) * 512],
                                       start=(k == 0), stop=(k == KB - 1 and n_extra == 0))
                    if kb == 0:
                        te.wait(s_w, s_w.n)
                        mi = te.matmul(B[bk][:], A["ones1bf"][:],
                                       A["sb2bf"][:, dmg * 512:(dmg + 1) * 512], start=False, stop=True)
                    mi.then_inc(s_pe, 1)
                    mm_fin = s_pe.n
                    P.bank_set_w(bk, s_pe)
                    ve.wait(s_pe, mm_fin)
                    ve.wait(s_chl, acc_init)
                    P.bank_r(ve, bk)
                    ve.scalar_tensor_tensor(accv[:, tt, dmg * 512:(dmg + 1) * 512],
                                            B[bk][:], 1.0,
                                            accv[:, tt, dmg * 512:(dmg + 1) * 512],
                                            ALU.mult, ALU.add).then_inc(s_dve, 1)
                    P.bank_add_r(bk, s_dve)
            w2_free[par] = s_pe.n
            gh2_free[par] = s_pe.n
        shared_acc = s_dve.n
        for tt in range(8):
            gtt = th * 8 + tt
            gp.wait(s_os[0], p4_done[0])
            gp.wait(s_os[1], p4_done[1])
            gp.wait(s_dve, P.g("p5_rg_ve"))
            if th == 0 and tt == 0:
                gp.reg_mov(A["reg_bchk"], SLOTS - 1)
            for j in range(2):
                idx = A["s1all"] if j == 0 else A["s2all"]
                gp.indirect_dma_start(
                    out=rgb[j][:], out_offset=None,
                    in_=A["d_rout2"], in_offset=bass.IndirectOffsetOnAxis(idx[:, gtt:gtt + 1], 0),
                    bounds_check=A["reg_bchk"], oob_is_err=False).then_inc(s_ig, 16)
            rg_rdy = s_ig.n
            ve.wait(s_ig, rg_rdy)
            ve.wait(s_dve, shared_acc)
            for j in range(2):
                ve.tensor_add(accv[:, tt, :], accv[:, tt, :], rgb[j][:]).then_inc(s_dve, 1)
            P.s("p5_rg_ve", s_dve.n)
        acc_done = s_dve.n
        ve.wait(s_dve, acc_done)
        for tt in range(8):
            ve.reduce_sum(A["ln_stats"][:, tt:tt + 1], accv[:, tt, :],
                          axis=AX.X).then_inc(s_dve, 1)
        sum_rdy = s_dve.n
        sc.wait(s_dve, sum_rdy)
        sc.wait(s_st, s_st.n)
        for tt in range(8):
            sc.activation(A["lnscr"][:], accv[:, tt, :], AF.Square,
                          accum_out=A["ln_stats"][:, 8 + tt:9 + tt]).then_inc(s_act, 1)
        sq2 = s_act.n
        ve.wait(s_act, sq2)
        ve.tensor_scalar_mul(A["ln_stats"][:, 0:8], A["ln_stats"][:, 0:8], 1.0 / DM).then_inc(s_dve, 1)
        ve.tensor_scalar_mul(A["ln_stats"][:, 8:16], A["ln_stats"][:, 8:16], 1.0 / DM).then_inc(s_dve, 1)
        ve.tensor_mul(A["lnscr"][:, 0:8], A["ln_stats"][:, 0:8], A["ln_stats"][:, 0:8]).then_inc(s_dve, 1)
        ve.tensor_sub(A["ln_stats"][:, 8:16], A["ln_stats"][:, 8:16], A["lnscr"][:, 0:8]).then_inc(s_dve, 1)
        ve.tensor_scalar_add(A["ln_stats"][:, 8:16], A["ln_stats"][:, 8:16], EPS).then_inc(s_dve, 1)
        v2 = s_dve.n
        sc.wait(s_dve, v2)
        sc.activation(A["ln_stats"][:, 8:16], A["ln_stats"][:, 8:16], AF.Ln).then_inc(s_act, 1)
        l2 = s_act.n
        ve.wait(s_act, l2)
        ve.tensor_scalar_mul(A["ln_stats"][:, 8:16], A["ln_stats"][:, 8:16], -0.5).then_inc(s_dve, 1)
        n2 = s_dve.n
        sc.wait(s_dve, n2)
        sc.activation(A["ln_stats"][:, 8:16], A["ln_stats"][:, 8:16], AF.Exp).then_inc(s_act, 1)
        rstd_rdy2 = s_act.n
        ve.wait(s_act, rstd_rdy2)
        st_prev = s_st.n
        for tt in range(8):
            row = th * 1024 + tt * 128
            ve.wait(s_st, st_prev)
            ve.tensor_scalar(A["lnscr"][:], accv[:, tt, :], A["ln_stats"][:, tt:tt + 1],
                             A["ln_stats"][:, 8 + tt:9 + tt],
                             ALU.subtract, ALU.mult).then_inc(s_dve, 1)
            fin = s_dve.n
            sy.wait(s_dve, fin)
            sy.dma_start(out=A["out"][row:row + 128, :], in_=A["lnscr"][:]).then_inc(s_st, 16)
            st_prev = s_st.n
        P.s("p5_acc_free", s_dve.n)
    _dump_taps(P, A, S)
    sy.wait(s_st, s_st.n)


def build_nc():
    nc = bass.Bass("TRN2", target_bir_lowering=False, debug=False)
    P = Prog(nc)
    A = {}

    def din(name, shape, dtype=F32):
        A["in_" + name] = nc.dram_tensor("in_" + name, list(shape), dtype,
                                         kind="ExternalInput").ap()

    din("x_fm", (DM, T + HALO))
    din("x_tm", (T, DM))
    din("w_in", (NFT * 128, 2048))
    din("conv_w", (128, 128))
    din("conv_b", (128, 32))
    din("a_dt", (32, 2))
    din("d_param", (128, 32))
    din("w_out", (DI, DM))
    din("gate_w", (DM, 8))
    din("w1", (8192, 2048))
    din("b1", (128, 64))
    din("sw1", (2048, 2048))
    din("sb1", (128, 16))
    din("w2cat", (10240, 2048))
    din("b2", (8, 2048))
    din("sb2", (1, 2048))
    din("ident", (128, 128))
    din("flag", (128, 1))
    din("lt2", (128, 512))
    din("sel32", (32, 4096))
    din("seg_mask", (32, 2048))
    din("iota_t", (128, 16))
    din("iota_w", (128, 8))
    din("srow16", (8, 16))
    din("l8u", (8, 8))
    din("gidx_init", (SLOTS, 2))
    din("b1r", (16, 1024))
    din("b2r", (16, 2048))
    A["out"] = nc.dram_tensor("out", [T, DM], F32, kind="ExternalOutput").ap()

    def internal(name, shape, dtype):
        A[name] = nc.dram_tensor(name, list(shape), dtype, kind="Internal").ap()
        if name in DEBUG_TAPS:
            A["dbg_" + name] = nc.dram_tensor("dbg_" + name, list(shape), F32,
                                              kind="ExternalOutput").ap()

    internal("sp_val", (DI, T), BF16)
    internal("sp_gate", (DI, T), BF16)
    internal("sp_B", (DI, T), BF16)
    internal("sp_C", (DI, T), BF16)
    internal("sp_Yd", (NCH * 128, 8192), BF16)
    internal("sp_Cexp", (NCH * 128, 8192), BF16)
    internal("sp_Yg", (DI, T), BF16)
    internal("sp_x1f", (T, DM), F32)
    internal("sp_x1bf", (T, DM), BF16)
    internal("sp_ghe", (2048, T), BF16)
    internal("d_w1p", (1024, 16384), BF16)
    internal("d_w2p", (1024, 16384), BF16)
    internal("d_b1b", (16, 1024), BF16)
    internal("d_b2b", (16, 2048), BF16)
    internal("d_sw1b", (2048, 2048), BF16)
    internal("d_w2shb", (2048, 2048), BF16)
    internal("d_woutb", (4096, 2048), BF16)
    internal("d_gidx", (SLOTS, 2), F32)
    internal("d_xg", (SLOTS, DM), BF16)
    internal("d_xgT", (DM, SLOTS), BF16)
    internal("d_rout2", (SLOTS, DM), BF16)
    internal("ag_in", (4096, 128), F32)
    internal("ag_out", (8192, 128), F32)

    sdefs = [
        ("ident", [128, 128], F32), ("flag", [128, 1], F32), ("conv_w", [128, 128], F32),
        ("conv_b", [128, 32], F32), ("d_param", [128, 32], F32), ("a_dt", [32, 2], F32),
        ("negA", [32, 1], F32), ("b1", [128, 64], F32), ("sb1", [128, 16], F32),
        ("b2bf", [8, 2048], BF16), ("sb2bf", [1, 2048], BF16), ("ones1bf", [1, 128], BF16),
        ("ones32", [32, 128], F32), ("gwbf", [128, 128], BF16), ("g_rows", [8, 2048], F32),
        ("ln_stats", [128, 16], F32), ("s1all", [128, 16], I32T), ("s2all", [128, 16], I32T),
        ("sel32", [32, 4096], F32), ("seg_mask", [32, 2048], F32),
        ("dt_sb", [32, 2048], F32), ("h_f32", [128, 4096], F32), ("h_bf16", [128, 4096], BF16),
        ("lasts", [32, 8], F32), ("ones8", [32, 8], F32), ("decays", [32, 8], F32),
        ("cum_tm", [128, 64], F32), ("neg_cum", [128, 64], F32), ("w_mh", [128, 64], F32),
        ("lastsbc", [128, 32], F32), ("el_bc", [128, 32], F32), ("D_bc", [128, 32], F32),
        ("cum_fm", [32, 2048], F32), ("dtscr", [32, 2048], F32),
        ("xfm", [128, 32816], BF16), ("wt0", [128, 2048], BF16), ("wt1", [128, 2048], BF16),
        ("valraw0", [128, 2056], F32), ("convacc", [128, 2048], F32),
        ("stg0", [128, 2048], BF16), ("stg1", [128, 2048], BF16), ("stg2", [128, 2048], BF16),
        ("val_fm", [128, 8192], BF16), ("B_fm", [128, 8192], BF16), ("C_fm", [128, 8192], BF16),
        ("val_tm0", [128, 4096], BF16), ("val_tm1", [128, 4096], BF16),
        ("B_tm0", [128, 4096], BF16), ("B_tm1", [128, 4096], BF16),
        ("Bw", [128, 512], BF16), ("e0", [128, 1024], F32),
        ("Mt0", [128, 1024], BF16), ("expcum", [128, 512], BF16),
        ("lt2", [128, 512], F32),
        ("yd_st0", [128, 2048], BF16), ("ce_st0", [128, 2048], BF16),
        ("clbf", [128, 512], BF16), ("ctmp", [128, 1024], F32),
        ("wot", [128, 65536], BF16), ("yg_t", [128, 4096], BF16), ("yg_t2", [128, 4096], BF16), ("x1stage", [128, 2048], F32),
        ("xtm_b", [128, 2048], F32), ("sq_b", [128, 2048], F32),
        ("x1fm", [128, 32768], BF16), ("w1t0", [128, 2048], BF16), ("w1t1", [128, 2048], BF16),
        ("ghe_st0", [128, 2048], BF16), ("ghe_st1", [128, 2048], BF16),
        ("gel0", [128, 512], F32), ("gel1", [128, 512], F32),
        ("gtmp", [128, 16], F32), ("gtmp2", [128, 16], F32),
        ("ind8", [8, 2048], F32), ("pos8", [8, 2048], F32),
        ("slot_tm0", [128, 8], F32), ("slot_tm1", [128, 8], F32),
        ("gate_tm0", [128, 8], F32), ("gate_tm1", [128, 8], F32),
        ("islot", [128, 8], I32T),
        ("stage2a", [128, 2], F32), ("stage2b", [128, 2], F32),
        ("stage2c", [128, 2], F32), ("stage2d", [128, 2], F32),
        ("gvall", [128, 60], F32),
        ("gixall", [128, 120], F32), ("giiall", [128, 120], I32T),
        ("wif", [128, 8], F32), ("wii0", [128, 8], I32T), ("wii1", [128, 8], I32T),
        ("wb_bc", [128, 16], F32), ("cmp16", [8, 16], F32), ("rts", [8, 8], F32),
        ("egrow", [1, 16], F32), ("wbrow", [1, 16], F32),
        ("ones1f", [1, 128], F32),
        ("w1cA0", [128, 8192], BF16), ("w1cA1", [128, 8192], BF16),
        ("w1cB0", [128, 8192], BF16), ("w1cB1", [128, 8192], BF16),
        ("w2cA0", [128, 8192], BF16), ("w2cA1", [128, 8192], BF16),
        ("w2cB0", [128, 8192], BF16), ("w2cB1", [128, 8192], BF16),
        ("xgg", [128, 8192], BF16),
        ("ones8r", [8, 2048], F32),
        ("xts0a", [128, 768], BF16), ("xts0b", [128, 768], BF16),
        ("xts0c", [128, 768], BF16), ("xts0d", [128, 768], BF16),
        ("r1sb", [128, 2048], BF16), ("r2sb", [128, 2048], BF16),
        ("h_fm0", [128, 1024], BF16), ("h_fm1", [128, 1024], BF16),
        ("osb0", [128, 2048], BF16), ("osb1", [128, 2048], BF16),
        ("xg_sb0", [128, 2048], BF16), ("xg_sb1", [128, 2048], BF16),
        ("xg_sb2", [128, 2048], BF16), ("xg_sb3", [128, 2048], BF16),
        ("xg_sb4", [128, 2048], BF16), ("xg_sb5", [128, 2048], BF16),
        ("xg_sb6", [128, 2048], BF16), ("xg_sb7", [128, 2048], BF16),
        ("b2b", [128, 2048], BF16), ("b1b", [128, 1024], BF16),
        ("l8u", [8, 8], F32), ("srow16", [8, 16], F32),
        ("iota_t", [128, 16], F32), ("iota_w", [128, 8], F32),
        ("w2k0", [128, 16384], BF16), ("w2k1", [128, 16384], BF16),
        ("ghk0", [128, 8192], BF16), ("ghk1", [128, 8192], BF16),
        ("out_acc", [128, 16384], F32), ("lnscr", [128, 2048], F32),
    ]
    for nm, shp, dt in sdefs:
        A[nm] = nc.alloc_sbuf_tensor_at("sb_" + nm, shp, dt, offset=_MAP[nm]).ap()

    A["reg_exp"] = nc.gpsimd.alloc_register("r_exp")
    A["reg_off"] = nc.gpsimd.alloc_register("r_off")
    A["reg_bchk"] = nc.gpsimd.alloc_register("r_bchk")
    A["reg_spares"] = [nc.gpsimd.alloc_register("r_spare%d" % i) for i in range(16)]
    with ExitStack() as stack:
        e = stack.enter_context
        S = {nm: Sem(e(nc.semaphore("s_" + nm)), nm) for nm in
             ["w", "ld", "sp", "st", "pe", "act", "dve", "pool", "cc",
              "wta", "wtb", "ch", "gha", "ghb", "spg",
              "b8a", "b8b", "e8a", "e8b", "m8a", "m8b", "ig", "gx",
              "gt0", "gt1", "gt2", "gt3", "gt4", "gt5", "gt6", "gt7", "gt8", "gt9", "gst", "gse", "xts", "xto", "wh0", "wh1", "swc", "gxo", "gxs0", "gxs1", "xss0", "xss1", "xq1", "xq2", "xq3", "os0", "os1", "g4", "g5", "g6", "g7"]}
        Bh = [e(nc.psum_tensor("pb%d" % i, [128, 512], F32)) for i in range(8)]
        B = [b[:] for b in Bh]
        P.sems = S
        block = e(nc.Block())
        for en in ["sync", "tensor", "scalar", "vector", "gpsimd"]:
            P.reset()
            P.active = en

            def run(eng, en=en):
                P.real[en] = eng
                emit(P, A, S, B)

            getattr(block, en)(run)
    return nc


def pack_inputs(inputs):
    f32 = np.float32
    x = np.asarray(inputs["L0_attn_x"], f32)
    W_in = np.asarray(inputs["L0_attn_W_in"], f32)
    conv_w = np.asarray(inputs["L0_attn_conv_w"], f32)
    conv_b = np.asarray(inputs["L0_attn_conv_b"], f32)
    A_log = np.asarray(inputs["L0_attn_A_log"], f32)
    D_param = np.asarray(inputs["L0_attn_D_param"], f32)
    dt_bias = np.asarray(inputs["L0_attn_dt_bias"], f32)
    W_out = np.asarray(inputs["L0_attn_W_out"], f32)
    gate_w = np.asarray(inputs["L0_ffn_gate_w"], f32)
    w1 = np.asarray(inputs["L0_ffn_w1"], f32)
    b1 = np.asarray(inputs["L0_ffn_b1"], f32)
    w2 = np.asarray(inputs["L0_ffn_w2"], f32)
    b2 = np.asarray(inputs["L0_ffn_b2"], f32)
    sw1 = np.asarray(inputs["L0_ffn_shared_w1"], f32)
    sb1 = np.asarray(inputs["L0_ffn_shared_b1"], f32)
    sw2 = np.asarray(inputs["L0_ffn_shared_w2"], f32)
    sb2 = np.asarray(inputs["L0_ffn_shared_b2"], f32)

    W_pad = np.zeros((NFT * 128, 2048), f32)
    W_pad[:16416] = W_in
    w_in_p = np.ascontiguousarray(
        W_pad.reshape(NFT, 128, 16, 128).transpose(0, 3, 2, 1).reshape(NFT * 128, 2048))
    conv_w_p = np.ascontiguousarray(conv_w.reshape(32, 128, 4).transpose(1, 0, 2).reshape(128, 128))
    conv_b_p = np.ascontiguousarray(conv_b.reshape(32, 128).T)
    d_p = np.ascontiguousarray(D_param.reshape(32, 128).T)
    a_dt = np.stack([A_log, dt_bias], axis=1).astype(f32)
    w_out_p = np.ascontiguousarray(W_out.T)
    w1_p = np.ascontiguousarray(
        w1.reshape(8, 16, 128, 8, 128).transpose(0, 3, 2, 1, 4).reshape(8192, 2048))
    b1_p = np.ascontiguousarray(b1.reshape(8, 8, 128).transpose(2, 0, 1).reshape(128, 64))
    sw1_p = np.ascontiguousarray(
        sw1.reshape(16, 128, 16, 128).transpose(2, 1, 0, 3).reshape(2048, 2048))
    sb1_p = np.ascontiguousarray(sb1.reshape(16, 128).T)
    w2cat = np.ascontiguousarray(np.concatenate([w2.reshape(8192, 2048), sw2], axis=0))
    sb2_p = np.ascontiguousarray(sb2.reshape(1, 2048))
    ident = np.eye(128, dtype=f32)
    lt2 = np.zeros((128, 512), f32)
    ll = np.arange(256)[None, :]
    pp = np.arange(128)[:, None]
    lt2[:, 0:256] = np.where(ll >= pp, 0.0, -60000.0)
    lt2[:, 256:512] = np.where(ll >= 128 + pp, 0.0, -60000.0)
    sel32 = (np.eye(32, dtype=f32)[:, :, None] * np.ones(128, f32)).reshape(32, 4096)
    iota_t = (np.arange(16)[None, :] * 128 + np.arange(128)[:, None]).astype(f32)
    iota_w = (np.arange(8)[None, :] * 128 + np.arange(128)[:, None]).astype(f32)
    srow16 = np.broadcast_to((np.arange(16) * 512).astype(f32), (8, 16)).copy()
    l8u = np.triu(np.ones((8, 8), f32), 1)
    gidx_init = np.zeros((SLOTS, 2), f32)
    seg = np.ones((32, 2048), f32)
    seg[:, ::CS] = 0.0

    common = {
        "in_w_in": w_in_p, "in_conv_w": conv_w_p, "in_conv_b": conv_b_p,
        "in_a_dt": a_dt, "in_d_param": d_p, "in_w_out": w_out_p,
        "in_gate_w": gate_w, "in_w1": w1_p, "in_b1": b1_p, "in_sw1": sw1_p,
        "in_sb1": sb1_p, "in_w2cat": w2cat, "in_b2": b2, "in_sb2": sb2_p,
        "in_ident": ident, "in_lt2": lt2, "in_sel32": sel32, "in_seg_mask": seg,
        "in_iota_t": iota_t, "in_iota_w": iota_w, "in_srow16": srow16, "in_l8u": l8u,
        "in_gidx_init": gidx_init,
        "in_b1r": np.concatenate([b1.reshape(8, 1024), np.zeros((8, 1024), f32)], axis=0),
        "in_b2r": np.concatenate([b2, np.zeros((8, 2048), f32)], axis=0),
    }
    maps = []
    for core in range(8):
        b_i = core // 2
        half = core % 2
        t0 = half * T
        xs = x[b_i, t0:t0 + T]
        if half == 0:
            xh = np.concatenate([np.zeros((HALO, DM), f32), xs], axis=0)
        else:
            xh = x[b_i, t0 - HALO:t0 + T]
        m = dict(common)
        m["in_x_fm"] = np.ascontiguousarray(xh.T)
        m["in_x_tm"] = np.ascontiguousarray(xs)
        m["in_flag"] = np.full((128, 1), float(half), f32)
        maps.append(m)
    return maps


def kernel(**inputs):
    maps = pack_inputs(inputs)
    nc = build_nc()
    res = run_bass_kernel_spmd(nc, maps, core_ids=list(range(8)))
    out = np.zeros((4, 4096, 2048), np.float32)
    for core in range(8):
        b_i = core // 2
        half = core % 2
        out[b_i, half * T:(half + 1) * T] = np.asarray(res.results[core]["out"], np.float32)
    if DEBUG_TAPS:
        kernel.debug = res.results
    return out
